# version 15
# speedup vs baseline: 1.1188x; 1.0314x over previous
import contextlib
import functools
import numpy as np

import concourse.bass as bass
import concourse.mybir as mybir
from concourse.bass_utils import run_bass_kernel_spmd

ENGINES = ("tensor", "vector", "scalar", "gpsimd", "sync")
PE_DRAIN_NOP = 128
DRAM_SETTLE_NOP = 2048
REORDER = True


def ap_region(ap):
    t = ap.tensor
    name = t.name
    dims = list(ap.ap)
    space = str(ap.space)
    is_psum = "PSUM" in space.upper()
    is_dram = "RAM" in space.upper() and not is_psum and "SB" not in space.upper()
    if is_psum:
        return (name, 0, 128, 0, 1 << 40, True)
    if is_dram:
        off = ap.offset if isinstance(ap.offset, int) else 0
        span = 1
        for st, ct in dims:
            span += abs(int(st)) * (int(ct) - 1)
        return (name, 0, 1, int(off), int(off) + span, False)
    try:
        p0 = ap.base_partition()
    except Exception:
        dims = list(ap.ap)
        rp = int(dims[0][0]) if dims else 1
        p0 = int(ap.offset) // rp if rp else 0
    psz = ap.partition_size()
    row_pitch = int(dims[0][0]) if dims else 1
    off = ap.offset if isinstance(ap.offset, int) else 0
    col0 = int(off) - p0 * row_pitch if row_pitch else int(off)
    span = 1
    for st, ct in dims[1:]:
        span += abs(int(st)) * (int(ct) - 1)
    if col0 < 0:
        col0, span = 0, 1 << 40
    return (name, p0, p0 + psz, col0, col0 + span, False)


def _overlap(a, b):
    return (a[1] < b[2] and b[1] < a[2] and a[3] < b[4] and b[3] < a[4])


def _covers(a, b):
    return (a[1] <= b[1] and a[2] >= b[2] and a[3] <= b[3] and a[4] >= b[4])


class Op:
    __slots__ = ("idx", "eng", "emit", "reads", "writes", "is_dma", "dma_sem_key",
                 "tag", "waits", "inc", "drain", "settle")

    def __init__(self, idx, eng, emit, reads, writes, is_dma, dma_sem_key, tag):
        self.idx = idx
        self.eng = eng
        self.emit = emit
        self.reads = reads
        self.writes = writes
        self.is_dma = is_dma
        self.dma_sem_key = dma_sem_key
        self.tag = tag
        self.waits = []
        self.inc = None
        self.drain = False
        self.settle = False


class Prog:
    def __init__(self, same_engine_sync=True, pe_sync=False):
        self.ops = []
        self.same_engine_sync = same_engine_sync
        self.pe_sync = pe_sync

    def _mk_regions(self, reads, writes):
        r, w = [], []
        for a in reads:
            reg = a if isinstance(a, tuple) else ap_region(a)
            (w if reg[5] else r).append(reg)
        for a in writes:
            reg = a if isinstance(a, tuple) else ap_region(a)
            w.append(reg)
        return r, w

    def op(self, eng, emit, reads=(), writes=(), tag=""):
        assert eng in ENGINES, eng
        r, w = self._mk_regions(reads, writes)
        self.ops.append(Op(len(self.ops), eng, emit, r, w, False, None, tag))

    def dma(self, queue, out_ap, in_ap, tag="", transpose=False, sem_key=None):
        assert queue in ("sync", "scalar", "gpsimd")
        oreg = ap_region(out_ap)
        if sem_key is None:
            sem_key = "D_%s@%d_%d" % (oreg[0], oreg[1], oreg[3])
        if transpose:
            def emit(eng, o=out_ap, i=in_ap):
                return eng.dma_start_transpose(o, i)
        else:
            def emit(eng, o=out_ap, i=in_ap):
                return eng.dma_start(out=o, in_=i)
        r, w = self._mk_regions([in_ap], [out_ap])
        self.ops.append(Op(len(self.ops), queue, emit, r, w, True, sem_key, tag))

    def barrier(self, tag="barrier"):
        self.ops.append(Op(len(self.ops), "barrier", None, [], [], False, None, tag))

    def _op_cost(self, op):
        if op.is_dma:
            span = 1
            for w in op.writes:
                span = max(span, (w[2] - w[1]) * (w[4] - w[3]))
            bytes_ = span * 4
            return 120.0, 1400.0 + bytes_ / 250.0
        if op.eng == "tensor":
            n = 64
            for r in op.reads:
                if not r[5]:
                    n = max(n, min(r[4] - r[3], 1024))
            return 40.0 + n * 0.42, 60.0
        n = 64
        for w in op.writes:
            if not w[5]:
                n = max(n, min(w[4] - w[3], 8192))
        slope = {"vector": 1.05, "scalar": 0.85, "gpsimd": 0.9, "sync": 0.2}[op.eng]
        base = {"vector": 190.0, "scalar": 330.0, "gpsimd": 260.0, "sync": 60.0}[op.eng]
        return base + n * slope, 0.0

    def reorder(self):
        import heapq
        skip = getattr(self, "no_reorder_segs", set())
        mode = getattr(self, "reorder_mode", "all")
        segs = []
        cur = []
        for op in self.ops:
            if op.eng == "barrier":
                segs.append(cur)
                segs.append([op])
                cur = []
            else:
                cur.append(op)
        segs.append(cur)

        new_ops = []
        pred_makespan = 0.0
        segidx = -1
        for seg in segs:
            if len(seg) == 1 and seg[0].eng == "barrier":
                new_ops.extend(seg)
                continue
            segidx += 1
            if segidx in skip:
                new_ops.extend(seg)
                continue
            hist = {}
            preds = {op.idx: set() for op in seg}
            if mode != "all":
                prev = None
                for op in seg:
                    pin = True if mode == "none" else ((not op.is_dma) if mode == "dma" else op.is_dma)
                    if pin:
                        if prev is not None:
                            preds[op.idx].add(prev)
                        prev = op.idx
            for op in seg:
                for reg in op.reads:
                    h = hist.get(reg[0])
                    if h:
                        for (o2, k2, r2) in reversed(h):
                            if k2 == "w" and _overlap(reg, r2):
                                preds[op.idx].add(o2.idx)
                                if _covers(r2, reg):
                                    break
                for reg in op.writes:
                    h = hist.get(reg[0])
                    if h:
                        for (o2, k2, r2) in reversed(h):
                            if _overlap(reg, r2):
                                preds[op.idx].add(o2.idx)
                                if k2 == "w" and _covers(r2, reg):
                                    break
                for reg in op.reads:
                    hist.setdefault(reg[0], []).append((op, "r", reg))
                for reg in op.writes:
                    hist.setdefault(reg[0], []).append((op, "w", reg))
            succs = {op.idx: [] for op in seg}
            npred = {}
            for op in seg:
                npred[op.idx] = len(preds[op.idx])
                for pp in preds[op.idx]:
                    succs[pp].append(op.idx)
            byid = {op.idx: op for op in seg}
            lane_free = {}
            ready_t = {op.idx: 0.0 for op in seg}
            ready = {e: [] for e in ENGINES}
            for op in seg:
                if npred[op.idx] == 0:
                    heapq.heappush(ready[op.eng], op.idx)
            done_t = {}
            order = []
            npend = len(seg)
            clock = 0.0
            import itertools
            while npend:
                best = None
                for e in ENGINES:
                    while ready[e] and ready[e][0] in done_t:
                        heapq.heappop(ready[e])
                    if not ready[e]:
                        continue
                    i = ready[e][0]
                    st = max(lane_free.get(e, 0.0), ready_t[i])
                    if best is None or (st, i) < best[0]:
                        best = ((st, i), e)
                if best is None:
                    raise RuntimeError("scheduler stuck")
                (st, i), e = best
                heapq.heappop(ready[e])
                op = byid[i]
                cost, extra = self._op_cost(op)
                lane_free[e] = st + cost
                dt = st + cost + extra
                done_t[i] = dt
                order.append(op)
                npend -= 1
                for sj in succs[i]:
                    npred[sj] -= 1
                    if ready_t[sj] < dt:
                        ready_t[sj] = dt
                    if npred[sj] == 0:
                        heapq.heappush(ready[byid[sj].eng], sj)
            pred_makespan += max(done_t.values()) if done_t else 0.0
            new_ops.extend(order)
        for k, op in enumerate(new_ops):
            op.idx = k
        self.ops = new_ops
        self.predicted_ns = pred_makespan
        return pred_makespan

    def schedule(self):
        if REORDER:
            self.reorder()
        deps = [None] * len(self.ops)
        hist = {}

        bar_pending = None
        bar_seen = None

        last_per_stream = {}

        for op in self.ops:
            if op.eng == "barrier":
                bar_pending = set(last_per_stream.values())
                bar_seen = set()
                continue
            prods = set()
            if bar_pending is not None and op.eng not in bar_seen:
                prods |= bar_pending
                bar_seen.add(op.eng)
            for reg in op.reads:
                h = hist.get(reg[0])
                if h:
                    for (o2, k2, r2) in reversed(h):
                        if k2 == "w" and _overlap(reg, r2):
                            prods.add(o2.idx)
                            if _covers(r2, reg):
                                break
            for reg in op.writes:
                h = hist.get(reg[0])
                if h:
                    for (o2, k2, r2) in reversed(h):
                        if _overlap(reg, r2):
                            prods.add(o2.idx)
                            if k2 == "w" and _covers(r2, reg):
                                break
            for reg in op.reads:
                hist.setdefault(reg[0], []).append((op, "r", reg))
            for reg in op.writes:
                hist.setdefault(reg[0], []).append((op, "w", reg))
            last_per_stream[("dma", op.dma_sem_key) if op.is_dma else op.eng] = op.idx
            keep = set()
            for p in prods:
                po = self.ops[p]
                if po.is_dma:
                    keep.add(p)
                elif po.eng == op.eng and not op.is_dma:
                    if po.eng == "tensor" and not self.pe_sync:
                        continue
                    if self.same_engine_sync:
                        keep.add(p)
                else:
                    keep.add(p)
            deps[op.idx] = keep

        needed = set()
        for op in self.ops:
            if op.eng != "barrier":
                needed |= deps[op.idx]
        last_of_eng = {}
        for op in self.ops:
            if op.eng != "barrier" and not op.is_dma:
                last_of_eng[op.eng] = op.idx
        needed |= set(last_of_eng.values())

        phase_of = {}
        ph = 0
        for op in self.ops:
            if op.eng == "barrier":
                ph += 1
            else:
                phase_of[op.idx] = ph
        NPOOL = 80
        pool_counter = {}
        key2handle = {}
        inc_count = {}

        def handle_for(op):
            k = (phase_of[op.idx], op.dma_sem_key)
            h = key2handle.get(k)
            if h is None:
                slot = pool_counter.get(k[0], 0)
                assert slot < NPOOL, "DMA sem pool exhausted in phase %d" % k[0]
                pool_counter[k[0]] = slot + 1
                h = "Dp%d" % slot
                key2handle[k] = h
            return h

        for op in self.ops:
            if op.eng == "barrier":
                continue
            if op.is_dma:
                key = handle_for(op)
                val = inc_count.get(key, 0) + 16
                inc_count[key] = val
                op.inc = (key, val)
            elif op.idx in needed:
                key = "E_" + op.eng
                val = inc_count.get(key, 0) + 1
                inc_count[key] = val
                op.inc = (key, val)

        observed = {e: {} for e in ENGINES}
        drained = {e: 0 for e in ENGINES}
        for op in self.ops:
            if op.eng == "barrier":
                continue
            obs = observed[op.eng]
            want = {}
            psum_tensor_dep = 0
            reads_psum = any(r[5] for r in op.reads) or any(w[5] for w in op.writes)
            for p in deps[op.idx]:
                po = self.ops[p]
                inc = po.inc
                if inc is None:
                    continue
                key, val = inc
                if want.get(key, 0) < val:
                    want[key] = val
                if po.eng == "tensor" and not po.is_dma and reads_psum:
                    if val > psum_tensor_dep and key == "E_tensor":
                        psum_tensor_dep = val
            for key, val in sorted(want.items()):
                if obs.get(key, 0) < val:
                    op.waits.append((key, val))
                    obs[key] = val
            if (op.eng != "tensor" and psum_tensor_dep > 0):
                op.drain = True
                drained[op.eng] = psum_tensor_dep


        per_eng = {e: [o for o in self.ops if o.eng == e] for e in ENGINES}
        self._final_counts = dict(inc_count)
        return per_eng, sorted(inc_count.keys())

    def finalize(self, nc, block, dump=None):
        per_eng, sem_keys = self.schedule()
        if dump:
            with open(dump, "w") as fh:
                for e in ENGINES:
                    for op in per_eng[e]:
                        fh.write(f"{e:7s} {op.idx:5d} {op.tag:24s} waits={op.waits} inc={op.inc}\n")
        sems = {}
        ctxs = []
        for key in sem_keys:
            cm = nc.semaphore(key[:30])
            sems[key] = cm.__enter__()
            ctxs.append(cm)

        def run_engine(eng_obj, eng_name):
            for op in per_eng[eng_name]:
                for (key, val) in op.waits:
                    eng_obj.wait_ge(sems[key], val)
                if op.drain:
                    eng_obj.nop(cycle_cnt=PE_DRAIN_NOP)
                if op.settle:
                    eng_obj.nop(cycle_cnt=DRAM_SETTLE_NOP)
                inst = op.emit(eng_obj)
                if op.inc is not None:
                    key, _ = op.inc
                    inst.then_inc(sems[key], 16 if op.is_dma else 1)

        for name in ENGINES:
            if per_eng[name]:
                getattr(block, name)(functools.partial(run_engine, eng_name=name))
        return ctxs




F32 = mybir.dt.float32
BF16 = mybir.dt.bfloat16
AF = mybir.ActivationFunctionType
ALU = mybir.AluOpType
AX = mybir.AxisListType

T = 4096
NT = 32
D = 1024
SEQ = 2048
EPS = 1e-5
DEBUG_WIN = False
DBG_D = None
DEBUG_P0_MEMSET = ()
DEBUG_OUTS = []


class Ctx:
    def __init__(self, nc, prog):
        self.nc = nc
        self.prog = prog
        self.stack = contextlib.ExitStack()
        self.epsc = None
        self.ident = None

    def sb(self, name, shape, dtype=F32):
        return self.stack.enter_context(self.nc.sbuf_tensor(name, shape, dtype))

    def ps(self, name, shape, dtype=F32):
        return self.stack.enter_context(self.nc.psum_tensor(name, shape, dtype))

    def close(self):
        self.stack.close()


def matmul_group(p, ps_ap, pairs, tag):
    n = len(pairs)
    for i, (l, r) in enumerate(pairs):
        p.op("tensor",
             lambda e, o=ps_ap, l=l, r=r, s=(i == 0), st=(i == n - 1):
                 e.matmul(o, l, r, start=s, stop=st),
             reads=[l, r], writes=[ps_ap], tag=tag)


def emit_ln_stats(ctx, src_f32, width, bn, st, tag=""):
    p = ctx.prog
    nb = (width + 511) // 512
    for c in range(nb):
        lo, hi = c * 512, min(width, (c + 1) * 512)
        p.op("vector", lambda e, c=c, lo=lo, hi=hi: e.bn_stats(
            bn[:, 6 * c:6 * c + 6], src_f32[:, lo:hi]),
            reads=[src_f32[:, lo:hi]], writes=[bn[:, 6 * c:6 * c + 6]], tag=tag + "_bn")
    p.op("vector", lambda e: e.bn_aggr(st[:, 0:2], bn[:, 0:6 * nb]),
         reads=[bn[:, 0:6 * nb]], writes=[st[:, 0:2]], tag=tag + "_aggr")
    p.op("scalar", lambda e: e.activation(st[:, 3:4], st[:, 1:2], AF.Sqrt,
                                          bias=ctx.epsc[:, 0:1]),
         reads=[st[:, 1:2], ctx.epsc[:, 0:1]], writes=[st[:, 3:4]], tag=tag + "_sqrt")
    p.op("vector", lambda e: e.reciprocal(st[:, 1:2], st[:, 3:4]),
         reads=[st[:, 3:4]], writes=[st[:, 1:2]], tag=tag + "_rcp")
    p.op("vector", lambda e: e.tensor_scalar(st[:, 2:3], st[:, 0:1], st[:, 1:2],
                                             -1.0, ALU.mult, ALU.mult),
         reads=[st[:, 0:2]], writes=[st[:, 2:3]], tag=tag + "_nb")


def emit_ln_apply(ctx, src, st, out, tag="", engine="scalar"):
    p = ctx.prog
    p.op(engine, lambda e: e.activation(out, src, AF.Identity,
                                        bias=st[:, 2:3], scale=st[:, 1:2]),
         reads=[src, st[:, 0:4]], writes=[out], tag=tag + "_apply")


def transpose_tile(ctx, out_ps_ap, in_ap, n_in, tag=""):
    p = ctx.prog
    ident = ctx.ident[0:n_in, 0:n_in]
    p.op("tensor", lambda e: e.transpose(out_ps_ap, in_ap, ident),
         reads=[in_ap, ctx.ident[:]], writes=[out_ps_ap], tag=tag + "_tr")


def ld8(p, dst, src, q, tag):
    p.dma(q, dst[:].rearrange("p (a n) -> p a n", a=8),
          src.rearrange("(a p) n -> p a n", p=128), tag=tag)



def build_phase0(ctx, xin_d, xout_d, din, cin):
    nc, p = ctx.nc, ctx.prog
    gb = ctx.sb("p0_gb", [128, D])
    bbt = ctx.sb("p0_bb", [128, D])
    p.dma("sync", gb[:], cin["c_l0_gamma_b"], tag="p0_ldg")
    p.dma("sync", bbt[:], cin["c_l0_beta_b"], tag="p0_ldb")

    wq = ctx.sb("p0_wq", [128, 8 * 1024], BF16)
    wk = ctx.sb("p0_wk", [128, 8 * 1024], BF16)
    wkp = ctx.sb("p0_wkp", [128, 8 * 1024], BF16)
    wv = ctx.sb("p0_wv", [128, 8 * 1024], BF16)
    wo = ctx.sb("p0_wo", [128, 8 * 1024], BF16)
    for dst, name in ((wq, "L0_attn_W_Q"), (wk, "L0_attn_W_K"),
                      (wkp, "L0_attn_W_Kp"), (wv, "L0_attn_W_V"),
                      (wo, "L0_attn_W_O")):
        ld8(p, dst, din[name], "gpsimd", "p0_ldw")

    selh = ctx.sb("p0_selh", [128, 2], F32)
    selhT = ctx.sb("p0_selhT", [2, 128], F32)
    p.dma("sync", selh[:], cin["c_selhalf"], tag="p0_lds")
    p.dma("sync", selhT[:], cin["c_selhalfT"], tag="p0_ldst")
    selj = ctx.sb("p0_selj", [32, 8 * 128], BF16)
    p.dma("sync", selj[:], cin["c_selj"], tag="p0_ldsj")
    selk = ctx.sb("p0_selk", [32, 128], BF16)
    p.dma("sync", selk[:], cin["c_selk"], tag="p0_ldsk")
    mask01 = ctx.sb("p0_mask01", [128, 8 * 32], BF16)
    p.dma("sync", mask01[:].rearrange("p (a n) -> p a n", a=8),
          cin["c_winmask01"].rearrange("(a k) n -> k a n", k=128), tag="p0_ldm")

    RD = 4
    ND = 4
    xin = [ctx.sb("p0_xin%d" % i, [128, D]) for i in range(RD)]
    xnf = [ctx.sb("p0_xn%d" % i, [128, D]) for i in range(ND)]
    abuf = [ctx.sb("p0_a%d" % i, [128, D]) for i in range(ND)]
    obuf = [ctx.sb("p0_o%d" % i, [128, D]) for i in range(ND)]
    bnt = [ctx.sb("p0_bn%d" % i, [128, 12]) for i in range(ND)]
    stt = [ctx.sb("p0_st%d" % i, [128, 8]) for i in range(ND)]

    w = {}
    w["xwb"] = ctx.sb("p0_xwb", [32, D], BF16)
    w["xwt"] = ctx.sb("p0_xwt", [128, 8 * 32], BF16)
    w["qf"] = ctx.sb("p0_qf", [128, 8 * 32], BF16)
    w["kf"] = ctx.sb("p0_kf", [128, 8 * 32], BF16)
    w["kpf"] = ctx.sb("p0_kpf", [128, 8 * 32], BF16)
    w["qF"] = ctx.sb("p0_qF", [128, 8 * 32])
    w["sqs"] = ctx.sb("p0_sqs", [128, 8 * 32])
    w["ssq2"] = ctx.sb("p0_ssq2", [2, 256])
    w["vt"] = ctx.sb("p0_vt", [32, D], BF16)
    w["mh"] = ctx.sb("p0_mh", [128, 1024], BF16)
    w["ph"] = ctx.sb("p0_ph", [128, 8 * 32], BF16)
    w["u2"] = ctx.sb("p0_u2", [128, 64], BF16)
    w["vv"] = ctx.sb("p0_vv", [128, 65], BF16)
    w["zt"] = ctx.sb("p0_zt", [32, D])
    w["zb"] = ctx.sb("p0_zb", [32, D], BF16)
    w["zr"] = ctx.sb("p0_zr", [32, 2])
    w["ztt"] = ctx.sb("p0_ztt", [128, 8 * 32], BF16)
    w["ps_a"] = ctx.ps("p0_ps_a", [128, 512])
    w["ps_b"] = ctx.ps("p0_ps_b", [128, 512])
    w["ps_tr"] = ctx.ps("p0_ps_tr", [128, 512], BF16)
    w["ps_n"] = ctx.ps("p0_ps_n", [128, 256])
    w["ps_z"] = ctx.ps("p0_ps_z", [32, 512])
    for key in DEBUG_P0_MEMSET:
        buf = w[key]
        val = 7.0 if key in ("nrm1", "inv") else 0.0
        p.op("vector", lambda e, b=buf, v=val: e.memset(b[:], v),
             reads=[], writes=[buf[:]], tag="dbg_ms_" + key)
    p.op("vector", lambda e: e.memset(w["vv"][:, 64:65], 1.0),
         reads=[], writes=[w["vv"][:, 64:65]], tag="p0_vvones")

    for it in range(NT):
        xi = xin[it % RD]
        xn = xnf[it % ND]
        ab = abuf[it % ND]
        ob = obuf[it % ND]
        bn = bnt[it % ND]
        st = stt[it % ND]
        p.dma("sync", xi[:], xin_d[it * 128:(it + 1) * 128, :], tag="p0_ldx%d" % it)
        emit_ln_stats(ctx, xi[:], D, bn, st, tag="p0_ln1_%d" % it)
        emit_ln_apply(ctx, xi[:], st, xn[:], tag="p0_ln1_%d" % it)
        if it in (15, 31):
            build_p0_window(ctx, xn, w, wq, wk, wkp, wv, wo, selh, selhT,
                            selj, selk, mask01, it)
        emit_ln_stats(ctx, xn[:], D, bn, st, tag="p0_ln2_%d" % it)
        emit_ln_apply(ctx, xn[:], st, ab[:], tag="p0_ln2_%d" % it)
        p.op("gpsimd", lambda e, a=ab: e.tensor_mul(a[:], a[:], gb[:]),
             reads=[ab[:], gb[:]], writes=[ab[:]], tag="p0_gmul%d" % it)
        p.op("gpsimd", lambda e, a=ab: e.tensor_add(a[:], a[:], bbt[:]),
             reads=[ab[:], bbt[:]], writes=[ab[:]], tag="p0_badd%d" % it)
        emit_ln_stats(ctx, ab[:], D, bn, st, tag="p0_ln3_%d" % it)
        emit_ln_apply(ctx, ab[:], st, ab[:], tag="p0_ln3_%d" % it)
        p.op("vector", lambda e, o=ob, xi=xi, a=ab: e.tensor_add(o[:], xi[:], a[:]),
             reads=[xi[:], ab[:]], writes=[ob[:]], tag="p0_add%d" % it)
        p.dma("sync", xout_d[it * 128:(it + 1) * 128, :], ob[:], tag="p0_stx%d" % it)


def build_p0_window(ctx, xn, w, wq, wk, wkp, wv, wo, selh, selhT, selj, selk,
                    mask01, it):
    p = ctx.prog
    tg = "p0w%d_" % it
    psA, psB, psT, psN, psZ = w["ps_a"], w["ps_b"], w["ps_tr"], w["ps_n"], w["ps_z"]
    p.op("scalar", lambda e: e.copy(w["xwb"][:], xn[96:128, :]),
         reads=[xn[:]], writes=[w["xwb"][:]], tag=tg + "cp")
    for k in range(8):
        transpose_tile(ctx, psT[:, k * 32:(k + 1) * 32],
                       w["xwb"][:, k * 128:(k + 1) * 128], 32, tag=tg + "trx%d" % k)
    p.op("vector", lambda e: e.tensor_copy(w["xwt"][:], psT[:, 0:256]),
         reads=[psT[:]], writes=[w["xwt"][:]], tag=tg + "trev")

    for wi_, (wmat, dst, qscale) in enumerate(
            ((wq, "qf", True), (wk, "kf", False), (wkp, "kpf", False))):
        ps = psA
        for ft in range(8):
            pairs = []
            for dt_ in range(8):
                pairs.append((wmat[:, dt_ * 1024 + ft * 128: dt_ * 1024 + ft * 128 + 128],
                              w["xwt"][:, dt_ * 32:(dt_ + 1) * 32]))
            matmul_group(p, ps[:, ft * 32:(ft + 1) * 32], pairs, tg + "proj")
        p.op("vector", lambda e: e.tensor_copy(w["qF"][:], ps[:, 0:256]),
             reads=[ps[:]], writes=[w["qF"][:]], tag=tg + "pev")
        p.op("scalar", lambda e: e.activation(w["sqs"][:], w["qF"][:], AF.Square),
             reads=[w["qF"][:]], writes=[w["sqs"][:]], tag=tg + "sq")
        for ft in range(8):
            matmul_group(p, psN[0:2, ft * 32:(ft + 1) * 32],
                         [(selh[:], w["sqs"][:, ft * 32:(ft + 1) * 32])], tg + "ssq")
        mulv = 8.0 if qscale else 1.0
        p.op("vector", lambda e: e.tensor_copy(w["ssq2"][:], psN[0:2, 0:256]),
             reads=[psN[:]], writes=[w["ssq2"][:]], tag=tg + "nev")
        for ft in range(8):
            matmul_group(p, psB[:, (ft % 4) * 32:(ft % 4) * 32 + 32],
                         [(selhT[:], w["ssq2"][:, ft * 32:ft * 32 + 32])],
                         tg + "nbc")
            sc_t = w["sqs"][:, ft * 32:(ft + 1) * 32]
            p.op("scalar", lambda e, d=sc_t, s=psB[:, (ft % 4) * 32:(ft % 4) * 32 + 32]:
                 e.activation(d, s, AF.Sqrt),
                 reads=[psB[:]], writes=[sc_t], tag=tg + "nsq")
            p.op("vector", lambda e, d=sc_t: e.tensor_scalar(d, d, 1e-7, mulv,
                                                             ALU.add, ALU.mult),
                 reads=[sc_t], writes=[sc_t], tag=tg + "neps")
            p.op("vector", lambda e, d=sc_t: e.reciprocal(d, d),
                 reads=[sc_t], writes=[sc_t], tag=tg + "ninv")
            dstt = w[dst][:, ft * 32:(ft + 1) * 32]
            p.op("vector", lambda e, d=dstt, s=sc_t,
                 q=w["qF"][:, ft * 32:(ft + 1) * 32]: e.tensor_mul(d, q, s),
                 reads=[sc_t, w["qF"][:, ft * 32:(ft + 1) * 32]], writes=[dstt],
                 tag=tg + "nsc")

    for sc in range(2):
        pairs = []
        for dt_ in range(8):
            pairs.append((w["xwt"][:, dt_ * 32:(dt_ + 1) * 32],
                          wv[:, dt_ * 1024 + sc * 512: dt_ * 1024 + sc * 512 + 512]))
        matmul_group(p, psZ[:, 0:512], pairs, tg + "vproj")
        p.op("vector", lambda e, sc=sc: e.tensor_copy(w["vt"][:, sc * 512:(sc + 1) * 512],
                                                      psZ[:, 0:512]),
             reads=[psZ[:]], writes=[w["vt"][:, sc * 512:(sc + 1) * 512]],
             tag=tg + "vev")

    for h in range(16):
        r0 = (h % 2) * 64
        c0 = (h // 2) * 32
        kf_h = w["kf"][r0:r0 + 64, c0:c0 + 32]
        kpf_h = w["kpf"][r0:r0 + 64, c0:c0 + 32]
        qf_h = w["qf"][r0:r0 + 64, c0:c0 + 32]
        mh = w["mh"][r0:r0 + 64, :]
        eng = "vector" if h % 2 == 0 else "gpsimd"
        p.op(eng, lambda e, mh=mh, a=kf_h, b=kpf_h: e.tensor_tensor(
            mh.rearrange("d (j k) -> d j k", k=32),
            a.unsqueeze(2).broadcast_to([64, 32, 32]),
            b.unsqueeze(1).broadcast_to([64, 32, 32]), ALU.mult),
            reads=[kf_h, kpf_h], writes=[mh], tag=tg + "mh%d" % h)
        matmul_group(p, psB[:, 0:64], [(selk[:], w["vt"][:, h * 64:(h + 1) * 64])],
                     tg + "u2_%d" % h)
        p.op("vector", lambda e: e.tensor_copy(w["u2"][:], psB[:, 0:64]),
             reads=[psB[:]], writes=[w["u2"][:]], tag=tg + "u2ev%d" % h)
        for jt in range(8):
            matmul_group(p, psA[:, (jt % 4) * 32:(jt % 4) * 32 + 32],
                         [(mh[:, jt * 128:(jt + 1) * 128], qf_h)], tg + "att%d_%d" % (h, jt))
            php = w["ph"][:, jt * 32:(jt + 1) * 32]
            p.op("scalar", lambda e, d=php, s=psA[:, (jt % 4) * 32:(jt % 4) * 32 + 32]:
                 e.activation(d, s, AF.Exp),
                 reads=[psA[:]], writes=[php], tag=tg + "exp%d_%d" % (h, jt))
            p.op("vector", lambda e, d=php, m=mask01[:, jt * 32:(jt + 1) * 32]:
                 e.tensor_mul(d, d, m),
                 reads=[php, mask01[:]], writes=[php], tag=tg + "msk%d_%d" % (h, jt))
            matmul_group(p, psB[:, 64:128],
                         [(selj[:, jt * 128:(jt + 1) * 128], w["vt"][:, h * 64:(h + 1) * 64])],
                         tg + "u1_%d_%d" % (h, jt))
            p.op("vector", lambda e, s=psB[:, 64:128]: e.tensor_mul(
                w["vv"][:, 0:64], s, w["u2"][:]),
                reads=[psB[:], w["u2"][:]], writes=[w["vv"][:, 0:64]],
                tag=tg + "vv%d_%d" % (h, jt))
            p.op("tensor", lambda e, l=php, s=(jt == 0), st=(jt == 7): e.matmul(
                psZ[:, 0:65], l, w["vv"][:], start=s, stop=st),
                reads=[php, w["vv"][:]], writes=[psZ[:]], tag=tg + "zmm%d_%d" % (h, jt))
        p.op("vector", lambda e: e.tensor_scalar(w["zr"][:, 0:1], psZ[:, 64:65],
                                                 1e-30, None, ALU.add),
             reads=[psZ[:]], writes=[w["zr"][:, 0:1]], tag=tg + "zn%d" % h)
        p.op("vector", lambda e: e.reciprocal(w["zr"][:, 1:2], w["zr"][:, 0:1]),
             reads=[w["zr"][:, 0:1]], writes=[w["zr"][:, 1:2]], tag=tg + "zr%d" % h)
        p.op("vector", lambda e, h=h: e.tensor_scalar(w["zt"][:, h * 64:(h + 1) * 64],
                                                      psZ[:, 0:64], w["zr"][:, 1:2],
                                                      None, ALU.mult),
             reads=[psZ[:], w["zr"][:, 1:2]], writes=[w["zt"][:, h * 64:(h + 1) * 64]],
             tag=tg + "zsc%d" % h)

    p.op("scalar", lambda e: e.copy(w["zb"][:], w["zt"][:]),
         reads=[w["zt"][:]], writes=[w["zb"][:]], tag=tg + "zcast")
    for k in range(8):
        transpose_tile(ctx, psT[:, k * 32:(k + 1) * 32],
                       w["zb"][:, k * 128:(k + 1) * 128], 32, tag=tg + "trz%d" % k)
    p.op("vector", lambda e: e.tensor_copy(w["ztt"][:], psT[:, 0:256]),
         reads=[psT[:]], writes=[w["ztt"][:]], tag=tg + "trzev")
    for sc in range(2):
        pairs = []
        for dt_ in range(8):
            pairs.append((w["ztt"][:, dt_ * 32:(dt_ + 1) * 32],
                          wo[:, dt_ * 1024 + sc * 512: dt_ * 1024 + sc * 512 + 512]))
        matmul_group(p, psZ[:, 0:512], pairs, tg + "omm%d" % sc)
        p.op("vector", lambda e, sc=sc: e.tensor_add(
            xn[96:128, sc * 512:(sc + 1) * 512], xn[96:128, sc * 512:(sc + 1) * 512],
            psZ[:, 0:512]),
            reads=[psZ[:], xn[:]], writes=[xn[:]], tag=tg + "oadd%d" % sc)
    if DEBUG_WIN and it == 15:
        for (row, buf, nr, nc_) in (
                (0, w["xwt"], 128, 256), (128, w["qf"], 128, 256),
                (256, w["kf"], 128, 256), (384, w["kpf"], 128, 256),
                (512, w["vt"], 32, 1024), (544, w["zt"], 32, 1024),
                (576, w["mh"], 128, 1024), (704, w["ph"], 128, 256),
                (832, w["vv"], 128, 65), (960, w["u2"], 128, 64),
                (1088, w["qF"], 128, 256), (1216, w["sqs"], 128, 256)):
            p.dma("gpsimd", DBG_D[row:row + nr, 0:nc_], buf[0:nr, 0:nc_],
                  tag=tg + "dbgdump")



def build_phase1(ctx, xin_d, xout_d, din, cin):
    nc, p = ctx.nc, ctx.prog
    ec = ctx.sb("p1_ec", [128, 8 * 1024], BF16)
    es = ctx.sb("p1_es", [128, 8 * 1024], BF16)
    esn = ctx.sb("p1_esn", [128, 8 * 1024], BF16)
    wr = ctx.sb("p1_wr", [128, 8 * 1024], BF16)
    wi = ctx.sb("p1_wi", [128, 8 * 1024], BF16)
    p1t = ctx.sb("p1_p1t", [128, 8 * 1024], BF16)
    p2t = ctx.sb("p1_p2t", [128, 8 * 1024], BF16)
    msb = ctx.sb("p1_m", [128, 8 * 1024], BF16)

    ld8(p, ec, cin["c_ec"], "sync", "p1_ld")
    ld8(p, es, cin["c_es"], "sync", "p1_ld")
    ld8(p, esn, cin["c_esn"], "sync", "p1_ld")
    ld8(p, wr, din["L0_ffn_w_freq_real"], "gpsimd", "p1_ld")
    ld8(p, wi, din["L0_ffn_w_freq_imag"], "gpsimd", "p1_ld")

    psA = ctx.ps("p1_psA", [128, 512], F32)
    psB = ctx.ps("p1_psB", [128, 512], F32)
    pss = [psA, psB]

    def stage(dst, lr_pairs, scale, tag):
        for ot in range(8):
            for tc in range(2):
                ps = pss[(ot * 2 + tc) % 2]
                pairs = []
                for dt_ in range(8):
                    for (lh, rh) in lr_pairs:
                        pairs.append((
                            lh[:, dt_ * 1024 + ot * 128: dt_ * 1024 + ot * 128 + 128],
                            rh[:, dt_ * 1024 + tc * 512: dt_ * 1024 + tc * 512 + 512]))
                matmul_group(p, ps[:], pairs, tag)
                dap = dst[:, ot * 1024 + tc * 512: ot * 1024 + tc * 512 + 512]
                if scale is None:
                    p.op("vector", lambda e, d=dap, s=ps: e.tensor_copy(d, s[:]),
                         reads=[ps[:]], writes=[dap], tag=tag + "_ev")
                else:
                    p.op("vector", lambda e, d=dap, s=ps: e.tensor_scalar(
                        d, s[:], scale, None, ALU.mult),
                        reads=[ps[:]], writes=[dap], tag=tag + "_ev")

    stage(p1t, [(wr, ec), (wi, es)], None, "p1_P1")
    stage(p2t, [(wi, ec), (wr, esn)], None, "p1_P2")
    stage(msb, [(p1t, ec), (p2t, esn)], 1.0 / 1024.0, "p1_M")

    RD = 4
    ND = 3
    xin = [ctx.sb("p1_xin%d" % i, [128, D]) for i in range(RD)]
    xnb = [ctx.sb("p1_xnb%d" % i, [128, D], BF16) for i in range(ND)]
    xnt = [ctx.sb("p1_xnt%d" % i, [128, D], BF16) for i in range(ND)]
    ybuf = [ctx.sb("p1_y%d" % i, [128, D]) for i in range(ND)]
    obuf = [ctx.sb("p1_o%d" % i, [128, D]) for i in range(ND)]
    bnt = [ctx.sb("p1_bn%d" % i, [128, 12]) for i in range(ND)]
    stt = [ctx.sb("p1_st%d" % i, [128, 8]) for i in range(ND)]
    pstr = [ctx.ps("p1_pstr%d" % i, [128, 1024], BF16) for i in range(2)]
    psm = [ctx.ps("p1_psm%d" % i, [128, 512]) for i in range(4)]
    biasb = ctx.sb("p1_biasb", [128, D])
    p.dma("sync", biasb[:], cin["c_l0ffn_bias_b"], tag="p1_ldbias")

    for it in range(NT):
        xi = xin[it % RD]
        xb = xnb[it % ND]
        xt = xnt[it % ND]
        yb = ybuf[it % ND]
        ob = obuf[it % ND]
        bn = bnt[it % ND]
        st = stt[it % ND]
        ptr_t = pstr[it % 2]
        p.dma("sync", xi[:], xin_d[it * 128:(it + 1) * 128, :], tag="p1_ldx%d" % it)
        emit_ln_stats(ctx, xi[:], D, bn, st, tag="p1_ln1_%d" % it)
        emit_ln_apply(ctx, xi[:], st, xb[:], tag="p1_ln1_%d" % it)
        for k in range(8):
            transpose_tile(ctx, ptr_t[:, k * 128:(k + 1) * 128],
                           xb[:, k * 128:(k + 1) * 128], 128, tag="p1_tr%d_%d" % (it, k))
        p.op("vector", lambda e, d=xt, s=ptr_t: e.tensor_copy(d[:], s[:]),
             reads=[ptr_t[:]], writes=[xt[:]], tag="p1_trev%d" % it)
        for sc in range(2):
            ps = psm[(it * 2 + sc) % 4]
            pairs = []
            for k in range(8):
                pairs.append((xt[:, k * 128:(k + 1) * 128],
                              msb[:, k * 1024 + sc * 512: k * 1024 + sc * 512 + 512]))
            matmul_group(p, ps[:], pairs, "p1_mm%d" % it)
            yap = yb[:, sc * 512:(sc + 1) * 512]
            bap = biasb[:, sc * 512:(sc + 1) * 512]
            p.op("vector", lambda e, d=yap, s=ps, bb=bap: e.tensor_add(d, s[:], bb),
                 reads=[ps[:], biasb[:]], writes=[yap], tag="p1_yev%d" % it)
        emit_ln_stats(ctx, yb[:], D, bn, st, tag="p1_ln3_%d" % it)
        emit_ln_apply(ctx, yb[:], st, yb[:], tag="p1_ln3_%d" % it)
        p.op("gpsimd", lambda e, o=ob, xi=xi, y=yb: e.tensor_add(o[:], xi[:], y[:]),
             reads=[xi[:], yb[:]], writes=[ob[:]], tag="p1_add%d" % it)
        p.dma("sync", xout_d[it * 128:(it + 1) * 128, :], ob[:], tag="p1_stx%d" % it)



def host_constants():
    c = {}
    d = np.arange(1024)
    k = (np.outer(d, d) % 1024).astype(np.float64) * (2.0 * np.pi / 1024.0)
    c["c_ec"] = np.cos(k).astype(np.float32)
    c["c_es"] = np.sin(k).astype(np.float32)
    c["c_esn"] = (-np.sin(k)).astype(np.float32)
    import ml_dtypes
    for n in ("c_ec", "c_es", "c_esn"):
        c[n] = c[n].astype(ml_dtypes.bfloat16)
    c["c_ident"] = np.eye(128, dtype=np.float32).astype(ml_dtypes.bfloat16)
    pp = np.arange(128)
    c["c_selhalf"] = (np.stack([(pp // 64) == 0, (pp // 64) == 1], 1)).astype(np.float32)
    c["c_selhalfT"] = np.ascontiguousarray(c["c_selhalf"].T)
    selj = np.zeros((32, 8, 128), np.float32)
    for jt in range(8):
        for pcol in range(128):
            selj[jt * 4 + pcol // 32, jt, pcol] = 1.0
    c["c_selj"] = selj.reshape(32, 8 * 128).astype(ml_dtypes.bfloat16)
    selk = np.zeros((32, 128), np.float32)
    for pcol in range(128):
        selk[pcol % 32, pcol] = 1.0
    c["c_selk"] = selk.astype(ml_dtypes.bfloat16)
    jj = np.arange(1024) // 32
    kk = np.arange(1024) % 32
    ss = np.arange(32)
    keep = ((jj[:, None] <= ss[None, :]) & (kk[:, None] <= ss[None, :]))
    c["c_winmask01"] = keep.astype(ml_dtypes.bfloat16)
    sidx = np.arange(128)
    triu = (sidx[:, None] <= sidx[None, :]).astype(np.float32)
    c["c_triu"] = triu
    c["c_triu_b"] = triu.astype(ml_dtypes.bfloat16)
    c["c_identf"] = np.eye(128, dtype=np.float32)
    c["c_onesrow"] = np.ones((1, 128), np.float32)
    c["c_onescol"] = np.ones((128, 1), np.float32)
    rs = np.zeros((32, 32, 128), np.float32)
    for h in range(32):
        rs[h, h, :] = 1.0
    c["c_rowsel"] = np.ascontiguousarray(rs.transpose(1, 0, 2).reshape(32, 32 * 128))
    return c


def host_bcasts(inputs):
    b = {}
    def bc(name, v):
        b[name] = np.ascontiguousarray(np.broadcast_to(
            np.asarray(v, np.float32)[None, :], (128, len(v))))
    bc("c_l0_gamma_b", inputs["L0_attn_ln_gamma"])
    bc("c_l0_beta_b", inputs["L0_attn_ln_beta"])
    bc("c_l0ffn_bias_b", inputs["L0_ffn_bias"])
    def tiled(name, v, ntile):
        v = np.asarray(v, np.float32)
        b[name] = np.ascontiguousarray(v.reshape(ntile, 128).T)
    tiled("c_normw_t", inputs["L1_attn_norm_w"], 16)
    tiled("c_rmsw_t", inputs["L1_attn_rms_w"], 8)
    tiled("c_b1_t", inputs["L1_attn_mlp_b1"], 32)
    bc("c_b2_b", inputs["L1_attn_mlp_b2"])
    cw = np.asarray(inputs["L1_attn_conv_w"], np.float32)
    b["c_convw_t"] = np.ascontiguousarray(
        cw.reshape(17, 128, 4).transpose(1, 0, 2).reshape(128, 17 * 4))
    cb = np.asarray(inputs["L1_attn_conv_b"], np.float32)
    b["c_convb_t"] = np.ascontiguousarray(cb.reshape(17, 128).T)
    nope_all = np.concatenate([np.arange(h * 64, h * 64 + 32) for h in range(16)])
    perm = np.concatenate(
        [np.concatenate([np.arange(h * 64, h * 64 + 32),
                         h * 64 + 32 + np.arange(0, 32, 2),
                         h * 64 + 32 + np.arange(1, 32, 2)]) for h in range(16)])
    b["c_wqperm"] = np.ascontiguousarray(np.asarray(inputs["L1_ffn_Wq"], np.float32)[:, perm])
    perm160 = np.concatenate([np.arange(128), 128 + np.arange(0, 32, 2),
                              128 + np.arange(1, 32, 2)])
    b["c_wkvaperm"] = np.ascontiguousarray(np.asarray(inputs["L1_ffn_Wkva"], np.float32)[:, perm160])
    b["c_woeff"] = np.ascontiguousarray(np.asarray(inputs["L1_ffn_Wo"], np.float32)[nope_all, :])
    wkvb = np.asarray(inputs["L1_ffn_Wkvb"], np.float32)
    weff = np.zeros((128, 1024), np.float32)
    for h in range(16):
        ft, sl = h // 2, h % 2
        weff[:, ft * 128 + sl * 64: ft * 128 + sl * 64 + 32] = wkvb[:, h * 32:(h + 1) * 32]
    b["c_wkvbeff"] = weff
    bc("c_wln1_b", inputs["L1_ffn_w_ln1"])
    bc("c_wln2_b", inputs["L1_ffn_w_ln2"])
    bc("c_expbias_b", inputs["L1_ffn_expert_bias"])
    import ml_dtypes
    tpos = np.arange(2048, dtype=np.float64)
    freqs = 1.0 / (1e5 ** (np.arange(0, 32, 2, dtype=np.float64) / 32.0))
    ang = np.outer(tpos, freqs)
    b["c_ropec"] = np.cos(ang).astype(np.float32).astype(ml_dtypes.bfloat16)
    b["c_ropes"] = np.sin(ang).astype(np.float32).astype(ml_dtypes.bfloat16)
    return b


def build_program(phases, io_external=True):
    nc = bass.Bass("TRN2", target_bir_lowering=False)
    prog = Prog()
    ctx = Ctx(nc, prog)

    din = {}
    WEIGHT_SHAPES = {
        "L0_attn_W_Q": (D, D), "L0_attn_W_K": (D, D), "L0_attn_W_V": (D, D),
        "L0_attn_W_Kp": (D, D), "L0_attn_W_O": (D, D),
        "L0_ffn_w_freq_real": (D, D), "L0_ffn_w_freq_imag": (D, D),
        "L1_attn_W_in": (D, 4256), "L1_attn_W_out": (2048, D),
        "L1_attn_mlp_w1": (D, 4096), "L1_attn_mlp_w2": (4096, D),
        "L1_attn_dt_bias": (1, 32), "L1_attn_A_log": (1, 32),
        "L1_attn_D_param": (1, 32),
        "L1_ffn_Wkvb": (128, 512), "L1_ffn_Wgate": (D, 8),
    }
    WEIGHT_SHAPES3 = {
        "L1_ffn_Wg": (8, D, 512), "L1_ffn_Wu": (8, D, 512),
        "L1_ffn_Wd": (8, 512, D),
        "L1_ffn_sWg": (1, D, 512), "L1_ffn_sWu": (1, D, 512),
        "L1_ffn_sWd": (1, 512, D),
    }
    for n, s in WEIGHT_SHAPES3.items():
        din[n] = nc.dram_tensor(n, list(s), F32, kind="ExternalInput").ap()
    for n, s in WEIGHT_SHAPES.items():
        din[n] = nc.dram_tensor(n, list(s), F32, kind="ExternalInput").ap()
    cin = {}
    CONST_SHAPES = {
        "c_ec": (D, D, BF16), "c_es": (D, D, BF16), "c_esn": (D, D, BF16),
        "c_ident": (128, 128, BF16),
        "c_selhalf": (128, 2, F32), "c_selhalfT": (2, 128, F32),
        "c_selj": (32, 8 * 128, BF16), "c_selk": (32, 128, BF16),
        "c_winmask01": (1024, 32, BF16),
        "c_l0_gamma_b": (128, D, F32), "c_l0_beta_b": (128, D, F32),
        "c_l0ffn_bias_b": (128, D, F32),
        "c_triu": (128, 128, F32), "c_triu_b": (128, 128, BF16),
        "c_identf": (128, 128, F32), "c_onesrow": (1, 128, F32),
        "c_onescol": (128, 1, F32), "c_rowsel": (32, 32 * 128, F32),
        "c_normw_t": (128, 16, F32), "c_rmsw_t": (128, 8, F32),
        "c_b1_t": (128, 32, F32), "c_b2_b": (128, D, F32),
        "c_convw_t": (128, 17 * 4, F32), "c_convb_t": (128, 17, F32),
        "c_wqperm": (D, D, F32), "c_wkvaperm": (D, 160, F32), "c_wkvbeff": (128, 1024, F32),
        "c_woeff": (512, D, F32), "c_wln1_b": (128, D, F32),
        "c_wln2_b": (128, D, F32), "c_expbias_b": (128, 8, F32),
        "c_ropec": (2048, 16, BF16), "c_ropes": (2048, 16, BF16),
    }
    for n, (a, b_, dt_) in CONST_SHAPES.items():
        cin[n] = nc.dram_tensor(n, [a, b_], dt_, kind="ExternalInput").ap()

    xd = {}
    if DEBUG_WIN:
        global DBG_D
        DBG_D = nc.dram_tensor("dbg", [1664, 1024], F32, kind="ExternalOutput").ap()
    xd[phases[0]] = nc.dram_tensor("x_in", [T, D], F32, kind="ExternalInput").ap()
    for i, ph in enumerate(phases):
        last = (i == len(phases) - 1)
        if last:
            xd[ph + 1] = nc.dram_tensor("x_out", [T, D], F32, kind="ExternalOutput").ap()
        else:
            xd[ph + 1] = nc.dram_tensor(
                "x_mid%d" % ph, [T, D], F32,
                kind="Internal").ap()

    ctx.epsc = ctx.sb("g_eps", [128, 2])
    prog.op("vector", lambda e: e.memset(ctx.epsc[:, 0:1], EPS),
            reads=[], writes=[ctx.epsc[:, 0:1]], tag="g_eps")
    ctx.ident = ctx.sb("g_ident", [128, 128], BF16)
    prog.dma("sync", ctx.ident[:], cin["c_ident"], tag="g_ldident")

    BUILDERS = {0: build_phase0, 1: build_phase1, 2: build_phase2,
                3: build_phase3}
    for i, ph in enumerate(phases):
        sub = contextlib.ExitStack()
        old = ctx.stack
        ctx.stack = sub
        BUILDERS[ph](ctx, xd[ph], xd[ph + 1], din, cin)
        ctx.stack = old
        sub.close()
        prog.barrier("phase%d_end" % ph)

    prog.op("sync", lambda e: e.nop(), reads=[xd[phases[-1] + 1]], writes=[], tag="final")

    with nc.Block() as block:
        prog.finalize(nc, block)
    ctx.close()
    return nc


def make_in_maps(inputs, phases, x_stage):
    consts = host_constants()
    bcasts = host_bcasts(inputs)
    in_maps = []
    for core in range(8):
        m = {}
        xs = np.ascontiguousarray(
            x_stage[2 * core:2 * core + 2].reshape(T, D).astype(np.float32))
        m["x_in"] = xs
        for n in ("L0_attn_W_Q", "L0_attn_W_K", "L0_attn_W_V", "L0_attn_W_Kp",
                  "L0_attn_W_O", "L0_ffn_w_freq_real", "L0_ffn_w_freq_imag",
                  "L1_attn_W_in", "L1_attn_W_out", "L1_attn_mlp_w1",
                  "L1_attn_mlp_w2"):
            m[n] = np.asarray(inputs[n], np.float32)
        for n in ("L1_attn_dt_bias", "L1_attn_A_log", "L1_attn_D_param"):
            m[n] = np.asarray(inputs[n], np.float32).reshape(1, 32)
        for n in ("L1_ffn_Wkvb", "L1_ffn_Wgate", "L1_ffn_Wg", "L1_ffn_Wu",
                  "L1_ffn_Wd", "L1_ffn_sWg", "L1_ffn_sWu", "L1_ffn_sWd"):
            m[n] = np.asarray(inputs[n], np.float32)
        for n, v in consts.items():
            m[n] = v
        for n, v in bcasts.items():
            m[n] = v
        in_maps.append(m)
    return in_maps


WEIGHTS_USED = {
    0: ["L0_attn_W_Q", "L0_attn_W_K", "L0_attn_W_V", "L0_attn_W_Kp", "L0_attn_W_O"],
    1: ["L0_ffn_w_freq_real", "L0_ffn_w_freq_imag"],
}



DIN = 2048
CONV = 2176
DPROJ = 4256
NH = 32


def build_phase2(ctx, xin_d, xout_d, din, cin):
    nc, p = ctx.nc, ctx.prog
    h2_d = nc.dram_tensor("h2_mid", [T, D], F32, kind="Internal").ap()
    build_phase2a(ctx, xin_d, h2_d, din, cin)
    p.barrier("p2a_end")
    build_phase2b(ctx, xin_d, h2_d, xout_d, din, cin)


def build_phase2a(ctx, xin_d, h2_d, din, cin):
    nc, p = ctx.nc, ctx.prog
    sub = contextlib.ExitStack()
    old = ctx.stack
    ctx.stack = sub

    win = ctx.sb("p2_win", [128, 8 * DPROJ], BF16)
    ld8(p, win, din["L1_attn_W_in"], "gpsimd", "p2_ldwin")
    wout = ctx.sb("p2_wout", [128, 16 * 1024], BF16)
    p.dma("gpsimd", wout[:].rearrange("p (a n) -> p a n", a=16),
          din["L1_attn_W_out"].rearrange("(a p) n -> p a n", p=128), tag="p2_ldwout")
    normw = ctx.sb("p2_normw", [128, 16])
    p.dma("sync", normw[:], cin["c_normw_t"], tag="p2_ldnw")
    for kt in range(16):
        ap_ = wout[:, kt * 1024:(kt + 1) * 1024]
        p.op("vector", lambda e, a=ap_, kt=kt: e.tensor_scalar(
            a, a, normw[:, kt:kt + 1], None, ALU.mult),
            reads=[ap_, normw[:]], writes=[ap_], tag="p2_nwfold")

    convw = ctx.sb("p2_convw", [128, 17 * 4])
    convb = ctx.sb("p2_convb", [128, 17])
    p.dma("sync", convw[:], cin["c_convw_t"], tag="p2_ldcw")
    p.dma("sync", convb[:], cin["c_convb_t"], tag="p2_ldcb")
    triu = ctx.sb("p2_triu", [128, 128])
    triub = ctx.sb("p2_triub", [128, 128], BF16)
    identf = ctx.sb("p2_identf", [128, 128])
    onesrow = ctx.sb("p2_onesrow", [1, 128])
    p.dma("sync", triu[:], cin["c_triu"], tag="p2_ldtr")
    p.dma("sync", triub[:], cin["c_triu_b"], tag="p2_ldtrb")
    p.dma("sync", identf[:], cin["c_identf"], tag="p2_ldid")
    p.dma("sync", onesrow[:], cin["c_onesrow"], tag="p2_ldon")
    onescol = ctx.sb("p2_onescol", [128, 1])
    p.dma("sync", onescol[:], cin["c_onescol"], tag="p2_ldoc")
    rowsel = ctx.sb("p2_rowsel", [32, 32 * 128])
    p.dma("sync", rowsel[:], cin["c_rowsel"], tag="p2_ldrs")
    cumend = ctx.sb("p2_cumend", [1, 64])
    row32 = ctx.sb("p2_row32", [1, 3 * 32])
    p.dma("sync", row32[:, 0:32], din["L1_attn_dt_bias"], tag="p2_lddtb")
    p.dma("sync", row32[:, 32:64], din["L1_attn_A_log"], tag="p2_ldalog")
    p.dma("sync", row32[:, 64:96], din["L1_attn_D_param"], tag="p2_lddp")
    p.op("scalar", lambda e: e.activation(row32[:, 32:64], row32[:, 32:64], AF.Exp),
         reads=[row32[:, 32:64]], writes=[row32[:, 32:64]], tag="p2_nega1")
    p.op("vector", lambda e: e.tensor_scalar(row32[:, 32:64], row32[:, 32:64],
                                             -1.0, None, ALU.mult),
         reads=[row32[:, 32:64]], writes=[row32[:, 32:64]], tag="p2_nega2")

    pA = [ctx.ps("p2_pA%d" % i, [128, 512]) for i in range(2)]
    pB = [ctx.ps("p2_pB%d" % i, [128, 1024], BF16) for i in range(2)]
    pC = [ctx.ps("p2_pC%d" % i, [128, 512]) for i in range(2)]
    pY1 = ctx.ps("p2_pY1", [128, 512])
    pY2 = ctx.ps("p2_pY2", [128, 512])

    bc32 = ctx.sb("p2_bc32", [128, 3 * 32])
    matmul_group(p, pA[0][:, 0:96], [(onesrow[:], row32[:])], "p2_bc32")
    p.op("vector", lambda e: e.tensor_copy(bc32[:], pA[0][:, 0:96]),
         reads=[pA[0][:]], writes=[bc32[:]], tag="p2_bc32ev")
    onec = ctx.sb("p2_onec", [128, 2])
    p.op("vector", lambda e: e.memset(onec[:, 0:1], 1.0), reads=[],
         writes=[onec[:, 0:1]], tag="p2_onec")
    p.op("vector", lambda e: e.memset(onec[:, 1:2], -1.0), reads=[],
         writes=[onec[:, 1:2]], tag="p2_onec2")

    xi_r = [ctx.sb("p2_xi%d" % i, [128, D]) for i in range(2)]
    xnb_r = [ctx.sb("p2_xnb%d" % i, [128, D], BF16) for i in range(1)]
    xnf32 = ctx.sb("p2_xnf32", [128, D])
    xnt_r = [ctx.sb("p2_xnt%d" % i, [128, D], BF16) for i in range(1)]
    zsil_r = [ctx.sb("p2_zsil%d" % i, [128, DIN], BF16) for i in range(1)]
    xbct_r = [ctx.sb("p2_xbct%d" % i, [128, CONV], BF16) for i in range(1)]
    dtb_r = [ctx.sb("p2_dt%d" % i, [128, 320]) for i in range(2)]
    convsrc = ctx.sb("p2_convsrc", [128, 17 * 131], BF16)
    convacc = ctx.sb("p2_convacc", [128, CONV])
    convtmp = ctx.sb("p2_convtmp", [128, CONV], BF16)
    sil_r = [ctx.sb("p2_sil%d" % i, [128, 17 * 128], BF16) for i in range(1)]
    xst_r = [ctx.sb("p2_xst%d" % i, [128, DIN], BF16) for i in range(1)]
    dtx_r = [ctx.sb("p2_dtx%d" % i, [128, DIN], BF16) for i in range(1)]
    dx2_r = [ctx.sb("p2_dx2%d" % i, [128, DIN], BF16) for i in range(1)]
    cum_r = [ctx.sb("p2_cum%d" % i, [128, 32]) for i in range(2)]
    cumT_r = [ctx.sb("p2_cumT%d" % i, [32, 128]) for i in range(2)]
    gm_r = [ctx.sb("p2_gm%d" % i, [128, 128], BF16) for i in range(2)]
    lgt_r = [ctx.sb("p2_lgt%d" % i, [128, 128], BF16) for i in range(4)]
    bt_r = [ctx.sb("p2_bt%d" % i, [128, 64], BF16) for i in range(2)]
    cf0 = ctx.sb("p2_cf0", [64, 128], BF16)
    dif32 = [ctx.sb("p2_dif32_%d" % i, [128, 128]) for i in range(2)]
    y2e = ctx.sb("p2_y2e", [128, 512], BF16)
    yt_r = [ctx.sb("p2_yt%d" % i, [128, DIN]) for i in range(1)]
    S = ctx.sb("p2_S", [64, NH * 64])
    Sb = ctx.sb("p2_Sb", [64, NH * 64], BF16)
    h2buf = [ctx.sb("p2_h2%d" % i, [128, D]) for i in range(1)]
    bnt = [ctx.sb("p2_bn%d" % i, [128, 12]) for i in range(2)]
    stt = [ctx.sb("p2_st%d" % i, [128, 8]) for i in range(2)]

    cs3 = convsrc[:].rearrange("p (c n) -> p c n", n=131)
    ca3 = convacc[:].rearrange("p (c n) -> p c n", n=128)
    ct3 = convtmp[:].rearrange("p (c n) -> p c n", n=128)
    w3 = convw[:].rearrange("p (c k) -> p c k", k=4)
    cb3 = convb[:].rearrange("p (c o) -> p c o", o=1)

    for b in range(2):
        p.op("vector", lambda e: e.memset(S[:], 0.0), reads=[], writes=[S[:]],
             tag="p2_S0_b%d" % b)
        p.op("vector", lambda e: e.memset(cs3[:, :, 0:3], 0.0),
             reads=[], writes=[convsrc[:]], tag="p2_tl0_b%d" % b)
        for c in range(16):
            it = b * 16 + c
            r = it % 2
            tg = "p2c%d_" % it
            xi = xi_r[it % 2]
            xnb, xnt = xnb_r[0], xnt_r[0]
            zsil, xbct, dtb = zsil_r[0], xbct_r[0], dtb_r[r]
            sil, xst, dtx, dx2 = sil_r[0], xst_r[0], dtx_r[0], dx2_r[0]
            cum, cumT, gm, bt, yt = cum_r[r], cumT_r[r], gm_r[r], bt_r[r], yt_r[0]
            h2t = h2buf[0]
            bn, st = bnt[r], stt[r]

            p.dma("sync", xi[:], xin_d[it * 128:(it + 1) * 128, :], tag=tg + "ldx")
            emit_ln_stats(ctx, xi[:], D, bn, st, tag=tg + "ln")
            emit_ln_apply(ctx, xi[:], st, xnf32[:], tag=tg + "ln")
            p.op("vector", lambda e, xb=xnb: e.tensor_copy(xb[:], xnf32[:]),
                 reads=[xnf32[:]], writes=[xnb[:]], tag=tg + "lncast")
            for k in range(8):
                transpose_tile(ctx, pB[0][:, k * 128:(k + 1) * 128],
                               xnb[:, k * 128:(k + 1) * 128], 128, tag=tg + "trx")
            p.op("vector", lambda e, d=xnt: e.tensor_copy(d[:], pB[0][:]),
                 reads=[pB[0][:]], writes=[xnt[:]], tag=tg + "trxev")
            for g in range(9):
                lo = g * 512
                hi = min(DPROJ, lo + 512)
                ps = pA[g % 2]
                pairs = [(xnt[:, k * 128:(k + 1) * 128],
                          win[:, k * DPROJ + lo: k * DPROJ + hi]) for k in range(8)]
                matmul_group(p, ps[:, 0:hi - lo], pairs, tg + "zx")
                if hi <= DIN:
                    p.op("scalar", lambda e, d=zsil[:, lo:hi], s=ps: e.activation(
                        d, s[:, 0:512], AF.Silu),
                        reads=[ps[:]], writes=[zsil[:, lo:hi]], tag=tg + "zsil")
                elif lo >= DIN and hi <= DIN + CONV:
                    d = xbct[:, lo - DIN:hi - DIN]
                    p.op("vector", lambda e, d=d, s=ps, n=hi - lo: e.tensor_copy(
                        d, s[:, 0:n]),
                        reads=[ps[:]], writes=[d], tag=tg + "xbev")
                else:
                    d = xbct[:, 4096 - DIN:4224 - DIN]
                    p.op("vector", lambda e, d=d, s=ps: e.tensor_copy(d, s[:, 0:128]),
                         reads=[ps[:]], writes=[d], tag=tg + "xbev")
                    p.op("vector", lambda e, s=ps, dtb=dtb: e.tensor_copy(
                        dtb[:, 0:32], s[:, 128:160]),
                        reads=[ps[:]], writes=[dtb[:, 0:32]], tag=tg + "dtev")
            for gi, (ct0, nt_) in enumerate(((0, 8), (8, 8), (16, 1))):
                pb = pB[gi % 2]
                for j in range(nt_):
                    ct = ct0 + j
                    transpose_tile(ctx, pb[:, j * 128:(j + 1) * 128],
                                   xbct[:, ct * 128:(ct + 1) * 128], 128,
                                   tag=tg + "trb")
                dst = cs3[:, ct0:ct0 + nt_, 3:131]
                p.op("vector", lambda e, d=dst, s=pb, n=nt_: e.tensor_copy(
                    d, s[:].rearrange("p (c n) -> p c n", n=128)[:, 0:n, :]),
                    reads=[pb[:]], writes=[convsrc[:]], tag=tg + "trbev")
            p.op("gpsimd", lambda e: e.tensor_tensor(
                ca3, cs3[:, :, 0:128],
                w3[:, :, 0:1].broadcast_to([128, 17, 128]), ALU.mult),
                reads=[convsrc[:], convw[:]], writes=[convacc[:]], tag=tg + "cv0")
            p.op("gpsimd", lambda e: e.tensor_tensor(
                ca3, ca3, cb3.broadcast_to([128, 17, 128]), ALU.add),
                reads=[convacc[:], convb[:]], writes=[convacc[:]], tag=tg + "cvb")
            for k in range(1, 4):
                p.op("gpsimd", lambda e, k=k: e.tensor_tensor(
                    ct3, cs3[:, :, k:k + 128],
                    w3[:, :, k:k + 1].broadcast_to([128, 17, 128]), ALU.mult),
                    reads=[convsrc[:], convw[:]], writes=[convtmp[:]], tag=tg + "cvm%d" % k)
                p.op("gpsimd", lambda e: e.tensor_tensor(ca3, ca3, ct3, ALU.add),
                     reads=[convacc[:], convtmp[:]], writes=[convacc[:]],
                     tag=tg + "cva%d" % k)
            p.op("vector", lambda e: e.tensor_copy(cs3[:, :, 0:3], cs3[:, :, 128:131]),
                 reads=[convsrc[:]], writes=[convsrc[:]], tag=tg + "tail")
            p.op("scalar", lambda e: e.activation(sil[:], convacc[:], AF.Silu),
                 reads=[convacc[:]], writes=[sil[:]], tag=tg + "csil")
            p.op("vector", lambda e, dtb=dtb: e.tensor_add(dtb[:, 32:64], dtb[:, 0:32],
                                                  bc32[:, 0:32]),
                 reads=[dtb[:, 0:32], bc32[:]], writes=[dtb[:, 32:64]], tag=tg + "dt0")
            p.op("scalar", lambda e, dtb=dtb: e.activation(dtb[:, 64:96], dtb[:, 32:64], AF.Abs),
                 reads=[dtb[:, 32:64]], writes=[dtb[:, 64:96]], tag=tg + "dtabs")
            p.op("scalar", lambda e, dtb=dtb: e.activation(dtb[:, 64:96], dtb[:, 64:96], AF.Exp,
                                                  scale=onec[:, 1:2]),
                 reads=[dtb[:, 64:96], onec[:]], writes=[dtb[:, 64:96]], tag=tg + "dtexp")
            p.op("scalar", lambda e, dtb=dtb: e.activation(dtb[:, 64:96], dtb[:, 64:96], AF.Ln,
                                                  bias=onec[:, 0:1]),
                 reads=[dtb[:, 64:96], onec[:]], writes=[dtb[:, 64:96]], tag=tg + "dtln")
            p.op("vector", lambda e, dtb=dtb: e.tensor_scalar(dtb[:, 96:128], dtb[:, 32:64],
                                                     0.0, None, ALU.max),
                 reads=[dtb[:, 32:64]], writes=[dtb[:, 96:128]], tag=tg + "dtrelu")
            p.op("vector", lambda e, dtb=dtb: e.tensor_add(dtb[:, 96:128], dtb[:, 96:128],
                                                  dtb[:, 64:96]),
                 reads=[dtb[:, 96:128], dtb[:, 64:96]], writes=[dtb[:, 96:128]],
                 tag=tg + "dtsum")
            p.op("vector", lambda e, dtb=dtb: e.tensor_mul(dtb[:, 128:160], dtb[:, 96:128],
                                                  bc32[:, 32:64]),
                 reads=[dtb[:, 96:128], bc32[:]], writes=[dtb[:, 128:160]],
                 tag=tg + "dtA")
            matmul_group(p, pA[0][:, 0:32], [(triu[:], dtb[:, 128:160])], tg + "cum")
            p.op("vector", lambda e, cum=cum: e.tensor_copy(cum[:], pA[0][:, 0:32]),
                 reads=[pA[0][:]], writes=[cum[:]], tag=tg + "cumev")
            p.op("vector", lambda e, dtb=dtb, cum=cum: e.tensor_scalar(dtb[:, 160:192], cum[:], -1.0,
                                                     None, ALU.mult),
                 reads=[cum[:]], writes=[dtb[:, 160:192]], tag=tg + "ncum")
            p.op("scalar", lambda e, dtb=dtb, cum=cum: e.activation(dtb[:, 224:256], cum[:], AF.Exp),
                 reads=[cum[:]], writes=[dtb[:, 224:256]], tag=tg + "ecum")
            p.op("tensor", lambda e, cum=cum: e.transpose(pC[0][0:32, 0:128], cum[:], identf[:]),
                 reads=[cum[:], identf[:]], writes=[pC[0][:]], tag=tg + "cumT")
            p.op("vector", lambda e, cumT=cumT: e.tensor_copy(cumT[:], pC[0][0:32, 0:128]),
                 reads=[pC[0][:]], writes=[cumT[:]], tag=tg + "cumTev")
            matmul_group(p, pC[1][0:1, 0:32], [(onescol[:], dtb[:, 128:160])],
                         tg + "cerow")
            p.op("vector", lambda e: e.tensor_copy(cumend[:, 0:32], pC[1][0:1, 0:32]),
                 reads=[pC[1][:]], writes=[cumend[:, 0:32]], tag=tg + "cerowev")
            p.op("scalar", lambda e: e.activation(cumend[:, 32:64], cumend[:, 0:32],
                                                  AF.Exp),
                 reads=[cumend[:, 0:32]], writes=[cumend[:, 32:64]], tag=tg + "aerow")
            matmul_group(p, pC[1][:, 0:32], [(onesrow[:], cumend[:, 0:32])], tg + "ceb")
            p.op("vector", lambda e, dtb=dtb, cum=cum: e.tensor_sub(dtb[:, 256:288], pC[1][:, 0:32], cum[:]),
                 reads=[pC[1][:], cum[:]], writes=[dtb[:, 256:288]], tag=tg + "dsub")
            p.op("scalar", lambda e, dtb=dtb: e.activation(dtb[:, 256:288], dtb[:, 256:288],
                                                  AF.Exp),
                 reads=[dtb[:, 256:288]], writes=[dtb[:, 256:288]], tag=tg + "dexp")
            matmul_group(p, pC[1][:, 64:96], [(onesrow[:], cumend[:, 32:64])],
                         tg + "aeb")
            p.op("vector", lambda e, dtb=dtb: e.tensor_copy(dtb[:, 192:224], pC[1][:, 64:96]),
                 reads=[pC[1][:]], writes=[dtb[:, 192:224]], tag=tg + "aebev")
            for half in range(2):
                pb = pB[half]
                for j in range(8):
                    ct = half * 8 + j
                    transpose_tile(ctx, pb[:, j * 128:(j + 1) * 128],
                                   sil[:, ct * 128:(ct + 1) * 128], 128,
                                   tag=tg + "trs")
                p.op("vector", lambda e, d=xst[:, half * 1024:(half + 1) * 1024],
                     s=pb: e.tensor_copy(d, s[:]),
                     reads=[pb[:]], writes=[xst[:, half * 1024:(half + 1) * 1024]],
                     tag=tg + "trsev")
            xst3 = xst[:].rearrange("p (h d) -> p h d", d=64)
            dtx3 = dtx[:].rearrange("p (h d) -> p h d", d=64)
            dx23 = dx2[:].rearrange("p (h d) -> p h d", d=64)
            p.op("gpsimd", lambda e, dtb=dtb, dtx3=dtx3, xst3=xst3: e.tensor_tensor(
                dtx3, xst3, dtb[:, 96:128].unsqueeze(2).broadcast_to([128, 32, 64]),
                ALU.mult),
                reads=[xst[:], dtb[:, 96:128]], writes=[dtx[:]], tag=tg + "dtx")
            p.op("vector", lambda e, dtb=dtb, dx23=dx23, dtx3=dtx3: e.tensor_tensor(
                dx23, dtx3, dtb[:, 256:288].unsqueeze(2).broadcast_to([128, 32, 64]),
                ALU.mult),
                reads=[dtx[:], dtb[:, 256:288]], writes=[dx2[:]], tag=tg + "dx2")
            Bf = sil[0:64, 16 * 128:17 * 128]
            Cfsrc = sil[64:128, 16 * 128:17 * 128]
            Cf = cf0[:]
            p.op("vector", lambda e: e.tensor_copy(cf0[:], Cfsrc),
                 reads=[sil[:]], writes=[cf0[:]], tag=tg + "cf0")
            matmul_group(p, pA[1][:, 0:128], [(Bf, Cf)], tg + "G")
            p.op("vector", lambda e, g=gm: e.tensor_mul(g[:], pA[1][:, 0:128], triub[:]),
                 reads=[pA[1][:], triub[:]], writes=[gm[:]], tag=tg + "Gm")
            transpose_tile(ctx, pB[0][:, 0:64], Bf, 64, tag=tg + "trB")
            p.op("vector", lambda e, d=bt: e.tensor_copy(d[:], pB[0][:, 0:64]),
                 reads=[pB[0][:]], writes=[bt[:]], tag=tg + "trBev")
            for hg in range(4):
                pyt = [pY1, pY2][hg % 2]
                for hh in range(8):
                    h = hg * 8 + hh
                    lg = lgt_r[h % 4]
                    matmul_group(p, pC[h % 2][:, 0:128],
                                 [(rowsel[:, h * 128:(h + 1) * 128], cumT[:])],
                                 tg + "ctb%d" % h)
                    df = dif32[h % 2]
                    p.op("vector", lambda e, h=h, pc=pC[h % 2], dtb=dtb, df=df: e.tensor_scalar(
                        df[:], pc[:, 0:128], dtb[:, 160 + h:160 + h + 1], 0.0,
                        ALU.add, ALU.min),
                        reads=[pC[h % 2][:], dtb[:, 160:192]], writes=[df[:]],
                        tag=tg + "ediff%d" % h)
                    p.op("scalar", lambda e, o=lg, df=df: e.activation(o[:], df[:], AF.Exp),
                         reads=[df[:]], writes=[lg[:]], tag=tg + "edec%d" % h)
                    eng = "vector" if h % 2 == 0 else "gpsimd"
                    p.op(eng, lambda e, o=lg, g=gm: e.tensor_mul(o[:], o[:], g[:]),
                         reads=[lg[:], gm[:]], writes=[lg[:]], tag=tg + "lgt%d" % h)
                    p.op("tensor", lambda e, o=pyt[:, hh * 64:(hh + 1) * 64], l=lg,
                         rr=dtx[:, h * 64:(h + 1) * 64]: e.matmul(o, l[:], rr,
                                                                 start=True, stop=True),
                         reads=[lg[:], dtx[:]], writes=[pyt[:]], tag=tg + "y1_%d" % h)
                py2 = [pY2, pY1][hg % 2]
                if c > 0:
                    for hh in range(8):
                        h = hg * 8 + hh
                        p.op("tensor", lambda e, o=py2[:, hh * 64:(hh + 1) * 64],
                             l=Cf, rr=Sb[:, h * 64:(h + 1) * 64]: e.matmul(
                                 o, l, rr, start=True, stop=True),
                             reads=[Cf, Sb[:]], writes=[py2[:]], tag=tg + "y2_%d" % h)
                    p.op("scalar", lambda e, s=py2: e.copy(y2e[:], s[:]),
                         reads=[py2[:]], writes=[y2e[:]], tag=tg + "y2ev%d" % hg)
                    y2e3 = y2e[:].rearrange("p (h d) -> p h d", d=64)
                    p.op("vector", lambda e, hg=hg, dtb=dtb, y2e3=y2e3: e.tensor_tensor(
                        y2e3, y2e3,
                        dtb[:, 224 + hg * 8:224 + hg * 8 + 8].unsqueeze(2)
                        .broadcast_to([128, 8, 64]), ALU.mult),
                        reads=[y2e[:], dtb[:, 224:256]], writes=[y2e[:]],
                        tag=tg + "y2sc%d" % hg)
                    p.op("vector", lambda e, s=pyt, d=yt[:, hg * 512:(hg + 1) * 512]:
                         e.tensor_add(d, s[:], y2e[:]),
                         reads=[pyt[:], y2e[:]], writes=[yt[:, hg * 512:(hg + 1) * 512]],
                         tag=tg + "ymrg%d" % hg)
                else:
                    p.op("vector", lambda e, s=pyt, d=yt[:, hg * 512:(hg + 1) * 512]:
                         e.tensor_copy(d, s[:]),
                         reads=[pyt[:]], writes=[yt[:, hg * 512:(hg + 1) * 512]],
                         tag=tg + "ycp%d" % hg)
                ptS = pA[(hg + 1) % 2]
                for hh in range(8):
                    h = hg * 8 + hh
                    p.op("tensor", lambda e, o=ptS[0:64, hh * 64:(hh + 1) * 64],
                         l=bt, rr=dx2[:, h * 64:(h + 1) * 64]: e.matmul(
                             o, l[:], rr, start=True, stop=True),
                         reads=[bt[:], dx2[:]], writes=[ptS[:]], tag=tg + "tm%d" % h)
                S3 = S[:].rearrange("p (h d) -> p h d", d=64)
                p.op("vector", lambda e, hg=hg, dtb=dtb, S3=S3: e.tensor_tensor(
                    S3[:, hg * 8:(hg + 1) * 8, :], S3[:, hg * 8:(hg + 1) * 8, :],
                    dtb[0:64, 192 + hg * 8:192 + hg * 8 + 8].unsqueeze(2)
                    .broadcast_to([64, 8, 64]), ALU.mult),
                    reads=[S[:], dtb[:, 192:224]], writes=[S[:]], tag=tg + "sdk%d" % hg)
                p.op("vector", lambda e, s=ptS, hg=hg: e.tensor_add(
                    S[:, hg * 512:(hg + 1) * 512], S[:, hg * 512:(hg + 1) * 512],
                    s[0:64, 0:512]),
                    reads=[ptS[:], S[:]], writes=[S[:]], tag=tg + "sup%d" % hg)
            p.op("scalar", lambda e: e.copy(Sb[:], S[:]),
                 reads=[S[:]], writes=[Sb[:]], tag=tg + "sbcp")
            yt3 = yt[:].rearrange("p (h d) -> p h d", d=64)
            p.op("gpsimd", lambda e: e.tensor_tensor(
                convtmp[:, 0:DIN].rearrange("p (h d) -> p h d", d=64),
                xst3, bc32[:, 64:96].unsqueeze(2).broadcast_to([128, 32, 64]),
                ALU.mult),
                reads=[xst[:], bc32[:]], writes=[convtmp[:]], tag=tg + "dxs")
            p.op("gpsimd", lambda e: e.tensor_add(yt[:], yt[:], convtmp[:, 0:DIN]),
                 reads=[yt[:], convtmp[:]], writes=[yt[:]], tag=tg + "dadd")
            p.op("gpsimd", lambda e: e.tensor_mul(yt[:], yt[:], zsil[:]),
                 reads=[yt[:], zsil[:]], writes=[yt[:]], tag=tg + "gate")
            p.op("scalar", lambda e, st=st, yt=yt: e.activation(convtmp[:, 0:DIN], yt[:], AF.Square,
                                                  accum_out=st[:, 4:5]),
                 reads=[yt[:]], writes=[convtmp[:, 0:DIN], st[:, 4:5]], tag=tg + "rsq")
            p.op("vector", lambda e, st=st: e.tensor_scalar(st[:, 4:5], st[:, 4:5],
                                                     1.0 / DIN, EPS, ALU.mult, ALU.add),
                 reads=[st[:, 4:5]], writes=[st[:, 4:5]], tag=tg + "rms1")
            p.op("scalar", lambda e, st=st: e.activation(st[:, 5:6], st[:, 4:5], AF.Sqrt),
                 reads=[st[:, 4:5]], writes=[st[:, 5:6]], tag=tg + "rms2")
            p.op("vector", lambda e, st=st: e.reciprocal(st[:, 4:5], st[:, 5:6]),
                 reads=[st[:, 5:6]], writes=[st[:, 4:5]], tag=tg + "rms3")
            p.op("vector", lambda e, st=st, yt=yt, dtx=dtx: e.tensor_scalar(dtx[:], yt[:], st[:, 4:5],
                                                     None, ALU.mult),
                 reads=[yt[:], st[:, 4:5]], writes=[dtx[:]], tag=tg + "ynorm")
            for half in range(2):
                pb = pB[half]
                for j in range(8):
                    kt = half * 8 + j
                    transpose_tile(ctx, pb[:, j * 128:(j + 1) * 128],
                                   dtx[:, kt * 128:(kt + 1) * 128], 128,
                                   tag=tg + "try")
                p.op("vector", lambda e, d=dx2[:, half * 1024:(half + 1) * 1024],
                     s=pb: e.tensor_copy(d, s[:]),
                     reads=[pb[:]], writes=[dx2[:, half * 1024:(half + 1) * 1024]],
                     tag=tg + "tryev")
            for sc in range(2):
                ps = pA[sc]
                pairs = [(dx2[:, kt * 128:(kt + 1) * 128],
                          wout[:, kt * 1024 + sc * 512: kt * 1024 + sc * 512 + 512])
                         for kt in range(16)]
                matmul_group(p, ps[:], pairs, tg + "mam")
                p.op("vector", lambda e, s=ps, d=h2t[:, sc * 512:(sc + 1) * 512],
                     x=xnf32[:, sc * 512:(sc + 1) * 512]: e.tensor_add(d, s[:], x),
                     reads=[ps[:], xnf32[:]], writes=[h2t[:, sc * 512:(sc + 1) * 512]],
                     tag=tg + "h2")
            p.dma("sync", h2_d[it * 128:(it + 1) * 128, :], h2t[:], tag=tg + "sth2")

    ctx.stack = old
    sub.close()


def build_phase2b(ctx, xin_d, h2_d, xout_d, din, cin):
    nc, p = ctx.nc, ctx.prog
    sub = contextlib.ExitStack()
    old = ctx.stack
    ctx.stack = sub

    w1 = ctx.sb("p2b_w1", [128, 8 * 4096], BF16)
    ld8(p, w1, din["L1_attn_mlp_w1"], "gpsimd", "p2b_ldw1")
    w2 = ctx.sb("p2b_w2", [128, 32 * 1024], BF16)
    p.dma("gpsimd", w2[:].rearrange("p (a n) -> p a n", a=32),
          din["L1_attn_mlp_w2"].rearrange("(a p) n -> p a n", p=128), tag="p2b_ldw2")
    rmsw = ctx.sb("p2b_rmsw", [128, 8])
    p.dma("sync", rmsw[:], cin["c_rmsw_t"], tag="p2b_ldrw")
    for kt in range(8):
        ap_ = w1[:, kt * 4096:(kt + 1) * 4096]
        p.op("vector", lambda e, a=ap_, kt=kt: e.tensor_scalar(
            a, a, rmsw[:, kt:kt + 1], None, ALU.mult),
            reads=[ap_, rmsw[:]], writes=[ap_], tag="p2b_rwfold")
    b1t = ctx.sb("p2b_b1", [128, 32])
    p.dma("sync", b1t[:], cin["c_b1_t"], tag="p2b_ldb1")
    b2b = ctx.sb("p2b_b2", [128, D])
    p.dma("sync", b2b[:], cin["c_b2_b"], tag="p2b_ldb2")

    pM = [ctx.ps("p2b_pM%d" % i, [128, 512]) for i in range(4)]
    pB = [ctx.ps("p2b_pB%d" % i, [128, 1024], BF16) for i in range(2)]

    h2 = [ctx.sb("p2b_h2_%d" % i, [128, D]) for i in range(4)]
    x2 = [ctx.sb("p2b_x2_%d" % i, [128, D]) for i in range(2)]
    hnb = [ctx.sb("p2b_hnb%d" % i, [128, D], BF16) for i in range(2)]
    hnT = [ctx.sb("p2b_hnT%d" % i, [128, 8 * 512], BF16) for i in range(1)]
    hid = [ctx.sb("p2b_hid%d" % i, [128, 32 * 512], BF16) for i in range(1)]
    ob = [ctx.sb("p2b_ob%d" % i, [128, D]) for i in range(1)]
    bnt = [ctx.sb("p2b_bn%d" % i, [128, 12]) for i in range(2)]
    stt = [ctx.sb("p2b_st%d" % i, [128, 8]) for i in range(4)]

    for u in range(8):
        ur = u % 2
        tg = "p2u%d_" % u
        ht = hnT[0]
        hd = hid[0]
        for tt in range(4):
            it = u * 4 + tt
            h2t = h2[tt]
            st = stt[tt]
            bn = bnt[ur]
            p.dma("sync", h2t[:], h2_d[it * 128:(it + 1) * 128, :], tag=tg + "ldh%d" % tt)
            p.op("scalar", lambda e, h=h2t, st=st, hb=hnb[ur]: e.activation(
                hb[:], h[:], AF.Square, accum_out=st[:, 4:5]),
                reads=[h2t[:]], writes=[hnb[ur][:], st[:, 4:5]], tag=tg + "rsq%d" % tt)
            p.op("vector", lambda e, st=st: e.tensor_scalar(
                st[:, 4:5], st[:, 4:5], 1.0 / D, EPS, ALU.mult, ALU.add),
                reads=[st[:, 4:5]], writes=[st[:, 4:5]], tag=tg + "r1%d" % tt)
            p.op("scalar", lambda e, st=st: e.activation(st[:, 5:6], st[:, 4:5], AF.Sqrt),
                 reads=[st[:, 4:5]], writes=[st[:, 5:6]], tag=tg + "r2%d" % tt)
            p.op("vector", lambda e, st=st: e.reciprocal(st[:, 4:5], st[:, 5:6]),
                 reads=[st[:, 5:6]], writes=[st[:, 4:5]], tag=tg + "r3%d" % tt)
            p.op("vector", lambda e, h=h2t, st=st, hb=hnb[ur]: e.tensor_scalar(
                hb[:], h[:], st[:, 4:5], None, ALU.mult),
                reads=[h2t[:], st[:, 4:5]], writes=[hnb[ur][:]], tag=tg + "hn%d" % tt)
            pb = pB[tt % 2]
            for k in range(8):
                transpose_tile(ctx, pb[:, k * 128:(k + 1) * 128],
                               hnb[ur][:, k * 128:(k + 1) * 128], 128, tag=tg + "trh")
            dst = ht[:].rearrange("p (k t) -> p k t", t=512)[:, :, tt * 128:(tt + 1) * 128]
            p.op("vector", lambda e, d=dst, s=pb: e.tensor_copy(
                d, s[:].rearrange("p (k t) -> p k t", t=128)),
                reads=[pb[:]], writes=[ht[:]], tag=tg + "trhev%d" % tt)
        for ft in range(32):
            ps = pM[ft % 2]
            pairs = [(w1[:, k * 4096 + ft * 128: k * 4096 + ft * 128 + 128],
                      ht[:, k * 512:(k + 1) * 512]) for k in range(8)]
            matmul_group(p, ps[:], pairs, tg + "w1")
            p.op("scalar", lambda e, d=hd[:, ft * 512:(ft + 1) * 512], s=ps, ft=ft:
                 e.activation(d, s[:], AF.Silu, bias=b1t[:, ft:ft + 1]),
                 reads=[ps[:], b1t[:]], writes=[hd[:, ft * 512:(ft + 1) * 512]],
                 tag=tg + "silu%d" % ft)
        for tt in range(4):
            it = u * 4 + tt
            h2t = h2[tt]
            ot = ob[0]
            bn = bnt[tt % 2]
            st = stt[tt]
            x2t = x2[tt % 2]
            p.dma("sync", x2t[:], xin_d[it * 128:(it + 1) * 128, :], tag=tg + "ldx%d" % tt)
            for sc in range(2):
                ps = pM[2 + sc]
                pairs = [(hd[:].rearrange("p (f t) -> p f t", t=512)
                          [:, k, tt * 128:(tt + 1) * 128],
                          w2[:, k * 1024 + sc * 512: k * 1024 + sc * 512 + 512])
                         for k in range(32)]
                matmul_group(p, ps[:], pairs, tg + "w2_%d_%d" % (tt, sc))
                p.op("vector", lambda e, s=ps, d=ot[:, sc * 512:(sc + 1) * 512],
                     bb=b2b[:, sc * 512:(sc + 1) * 512]: e.tensor_add(d, s[:], bb),
                     reads=[ps[:], b2b[:]], writes=[ot[:, sc * 512:(sc + 1) * 512]],
                     tag=tg + "oev%d_%d" % (tt, sc))
            p.op("gpsimd", lambda e, o=ot, h=h2t: e.tensor_add(o[:], o[:], h[:]),
                 reads=[ot[:], h2t[:]], writes=[ot[:]], tag=tg + "hadd%d" % tt)
            emit_ln_stats(ctx, ot[:], D, bn, st, tag=tg + "ln%d" % tt)
            emit_ln_apply(ctx, ot[:], st, ot[:], tag=tg + "ln%d" % tt)
            p.op("vector", lambda e, o=ot, x=x2t: e.tensor_add(o[:], o[:], x[:]),
                 reads=[ot[:], x2t[:]], writes=[ot[:]], tag=tg + "xadd%d" % tt)
            p.dma("sync", xout_d[it * 128:(it + 1) * 128, :], ot[:], tag=tg + "stx%d" % tt)

    ctx.stack = old
    sub.close()



H = 16
HD = 64
NOPE = 32
ROPE = 32
R = 128
E = 8
I = 512


def build_phase3(ctx, xin_d, xout_d, din, cin):
    nc, p = ctx.nc, ctx.prog
    xb_d = nc.dram_tensor("xb_mid", [T, D], F32, kind="Internal").ap()
    t_d = nc.dram_tensor("t_mid", [T, D], F32, kind="Internal").ap()
    build_phase3a(ctx, xin_d, xb_d, t_d, din, cin)
    p.barrier("p3a_end")
    build_phase3b(ctx, xin_d, t_d, xout_d, din, cin)


def build_phase3a(ctx, xin_d, xb_d, t_d, din, cin):
    nc, p = ctx.nc, ctx.prog
    sub = contextlib.ExitStack()
    old = ctx.stack
    ctx.stack = sub

    wq = ctx.sb("p3_wq", [128, 8 * 1024], BF16)
    ld8(p, wq, cin["c_wqperm"], "gpsimd", "p3_ldwq")
    wkva = ctx.sb("p3_wkva", [128, 8 * 160], BF16)
    p.dma("gpsimd", wkva[:].rearrange("p (a n) -> p a n", a=8),
          cin["c_wkvaperm"].rearrange("(a p) n -> p a n", p=128), tag="p3_ldkva")
    wkvb = ctx.sb("p3_wkvb", [128, 8 * 128], BF16)
    p.dma("gpsimd", wkvb[:].rearrange("p (a n) -> p a n", a=8),
          cin["c_wkvbeff"].rearrange("p (a n) -> p a n", a=8), tag="p3_ldkvb")
    woe = ctx.sb("p3_woe", [128, 4 * 1024], BF16)
    p.dma("gpsimd", woe[:].rearrange("p (a n) -> p a n", a=4),
          cin["c_woeff"].rearrange("(a p) n -> p a n", p=128), tag="p3_ldwoe")
    wln1b = ctx.sb("p3_wln1b", [128, D])
    p.dma("sync", wln1b[:], cin["c_wln1_b"], tag="p3_ldln1")
    ropec = ctx.sb("p3_ropec", [128, 16 * 16], BF16)
    ropes = ctx.sb("p3_ropes", [128, 16 * 16], BF16)
    p.dma("sync", ropec[:].rearrange("p (a n) -> p a n", a=16),
          cin["c_ropec"].rearrange("(a p) n -> p a n", p=128), tag="p3_ldrc")
    p.dma("sync", ropes[:].rearrange("p (a n) -> p a n", a=16),
          cin["c_ropes"].rearrange("(a p) n -> p a n", p=128), tag="p3_ldrs")
    cpool = ctx.sb("p3_cpool", [128, 4])
    p.op("vector", lambda e: e.memset(cpool[:, 0:1], 0.125), reads=[],
         writes=[cpool[:, 0:1]], tag="p3_c0")
    p.op("vector", lambda e: e.memset(cpool[:, 1:2], -20.0), reads=[],
         writes=[cpool[:, 1:2]], tag="p3_c1")

    qf = ctx.sb("p3_qf", [128, 8 * 2048], BF16)
    kvf = ctx.sb("p3_kvf", [128, 8 * 2048], BF16)
    ckvf = ctx.sb("p3_ckvf", [128, 2048], BF16)
    krr = ctx.sb("p3_krr", [32, 2048], BF16)
    vaug = ctx.sb("p3_vaug", [128, 16 * 528], BF16)
    ytok = ctx.sb("p3_ytok", [128, 16 * 512], BF16)

    xi_r = [ctx.sb("p3_xi%d" % i, [128, D]) for i in range(2)]
    xbf = ctx.sb("p3_xbf", [128, D])
    xab = ctx.sb("p3_xab", [128, D], BF16)
    xaT = ctx.sb("p3_xaT", [128, D], BF16)
    qtok_r = [ctx.sb("p3_qtok%d" % i, [128, D], BF16) for i in range(2)]
    cktok = ctx.sb("p3_cktok", [128, 192], BF16)
    Pb = [ctx.sb("p3_P%d" % i, [128, 512], BF16) for i in range(2)]
    yaug = [ctx.sb("p3_yaug%d" % i, [33, 512], BF16) for i in range(2)]
    zr = ctx.sb("p3_zr", [128, 8])
    bnt = ctx.sb("p3_bn", [128, 12])
    stt = ctx.sb("p3_st", [128, 8])
    obuf = ctx.sb("p3_ob", [128, D])
    ropescr = ctx.sb("p3_ropescr", [128, 1024], BF16)

    pq = [ctx.ps("p3_pq%d" % i, [128, 512]) for i in range(2)]
    ptr = [ctx.ps("p3_ptr%d" % i, [128, 1024], BF16) for i in range(2)]
    pS = [ctx.ps("p3_pS%d" % i, [128, 512]) for i in range(2)]
    pZ0 = ctx.ps("p3_pZ", [33, 512])
    pZ = None
    pO = ctx.ps("p3_pO", [128, 512])

    pZ = [pZ0, pq[1]]
    p.op("vector", lambda e: e.memset(
        vaug[:].rearrange("p (k h c) -> p k h c", h=16, c=33)[:, :, :, 32:33], 1.0),
        reads=[], writes=[vaug[:]], tag="p3_vones")

    for b in range(2):
        tb = "p3b%d_" % b
        toff = b * 2048
        for it in range(16):
            tg = tb + "t%d_" % it
            xi = xi_r[it % 2]
            row = (toff + it * 128)
            p.dma("sync", xi[:], xin_d[row:row + 128, :], tag=tg + "ldx")
            emit_ln_stats(ctx, xi[:], D, bnt, stt, tag=tg + "ln")
            emit_ln_apply(ctx, xi[:], stt, xbf[:], tag=tg + "ln")
            p.dma("sync", xb_d[row:row + 128, :], xbf[:], tag=tg + "stxb")
            p.op("scalar", lambda e: e.activation(xab[:], xbf[:], AF.Square,
                                                  accum_out=stt[:, 4:5]),
                 reads=[xbf[:]], writes=[xab[:], stt[:, 4:5]], tag=tg + "rsq")
            p.op("vector", lambda e: e.tensor_scalar(stt[:, 4:5], stt[:, 4:5],
                                                     1.0 / D, EPS, ALU.mult, ALU.add),
                 reads=[stt[:, 4:5]], writes=[stt[:, 4:5]], tag=tg + "r1")
            p.op("scalar", lambda e: e.activation(stt[:, 5:6], stt[:, 4:5], AF.Sqrt),
                 reads=[stt[:, 4:5]], writes=[stt[:, 5:6]], tag=tg + "r2")
            p.op("vector", lambda e: e.reciprocal(stt[:, 4:5], stt[:, 5:6]),
                 reads=[stt[:, 5:6]], writes=[stt[:, 4:5]], tag=tg + "r3")
            p.op("vector", lambda e: e.tensor_scalar(obuf[:], xbf[:], stt[:, 4:5],
                                                     None, ALU.mult),
                 reads=[xbf[:], stt[:, 4:5]], writes=[obuf[:]], tag=tg + "rap")
            p.op("gpsimd", lambda e: e.tensor_mul(xab[:], obuf[:], wln1b[:]),
                 reads=[obuf[:], wln1b[:]], writes=[xab[:]], tag=tg + "wl1")
            for k in range(8):
                transpose_tile(ctx, ptr[0][:, k * 128:(k + 1) * 128],
                               xab[:, k * 128:(k + 1) * 128], 128, tag=tg + "trx")
            p.op("vector", lambda e: e.tensor_copy(xaT[:], ptr[0][:]),
                 reads=[ptr[0][:]], writes=[xaT[:]], tag=tg + "trxev")
            qt = qtok_r[it % 2]
            for sc in range(2):
                pairs = [(xaT[:, k * 128:(k + 1) * 128],
                          wq[:, k * 1024 + sc * 512: k * 1024 + sc * 512 + 512])
                         for k in range(8)]
                matmul_group(p, pq[sc][:], pairs, tg + "qmm")
                p.op("vector", lambda e, sc=sc, qt=qt: e.tensor_copy(
                    qt[:, sc * 512:(sc + 1) * 512], pq[sc][:]),
                    reads=[pq[sc][:]], writes=[qt[:, sc * 512:(sc + 1) * 512]],
                    tag=tg + "qev")
            q3e = qt[:].rearrange("p (h c) -> p h c", c=64)[:, :, 32:48]
            q3o = qt[:].rearrange("p (h c) -> p h c", c=64)[:, :, 48:64]
            ce = ropec[:, it * 16:(it + 1) * 16].unsqueeze(1).broadcast_to([128, 16, 16])
            se = ropes[:, it * 16:(it + 1) * 16].unsqueeze(1).broadcast_to([128, 16, 16])
            re1 = ropescr[:, 0:256].rearrange("p (h c) -> p h c", c=16)
            re2 = ropescr[:, 256:512].rearrange("p (h c) -> p h c", c=16)
            re3 = ropescr[:, 512:768].rearrange("p (h c) -> p h c", c=16)
            re4 = ropescr[:, 768:1024].rearrange("p (h c) -> p h c", c=16)
            p.op("gpsimd", lambda e, q3e=q3e, ce=ce, re1=re1: e.tensor_tensor(re1, q3e, ce, ALU.mult),
                 reads=[qt[:]], writes=[ropescr[:, 0:256]], tag=tg + "rp1")
            p.op("gpsimd", lambda e, q3o=q3o, se=se, re2=re2: e.tensor_tensor(re2, q3o, se, ALU.mult),
                 reads=[qt[:]], writes=[ropescr[:, 256:512]], tag=tg + "rp2")
            p.op("gpsimd", lambda e, q3e=q3e, se=se, re3=re3: e.tensor_tensor(
                re3, q3e, se, ALU.mult),
                reads=[qt[:]], writes=[ropescr[:, 512:768]], tag=tg + "rp3")
            p.op("gpsimd", lambda e, q3o=q3o, ce=ce, re4=re4: e.tensor_tensor(
                re4, q3o, ce, ALU.mult),
                reads=[qt[:]], writes=[ropescr[:, 768:1024]], tag=tg + "rp4")
            p.op("vector", lambda e, q3e=q3e, re1=re1, re2=re2: e.tensor_tensor(q3e, re1, re2, ALU.subtract),
                 reads=[ropescr[:, 0:512]], writes=[qt[:]], tag=tg + "rp5")
            p.op("vector", lambda e, q3o=q3o, re3=re3, re4=re4: e.tensor_tensor(q3o, re3, re4, ALU.add),
                 reads=[ropescr[:, 512:1024]], writes=[qt[:]], tag=tg + "rp6")
            for k in range(8):
                transpose_tile(ctx, ptr[1][:, k * 128:(k + 1) * 128],
                               qt[:, k * 128:(k + 1) * 128], 128, tag=tg + "trq")
            p.op("vector", lambda e, it=it: e.tensor_copy(
                qf[:].rearrange("p (k t) -> p k t", t=2048)[:, :, it * 128:(it + 1) * 128],
                ptr[1][:].rearrange("p (k t) -> p k t", t=128)),
                reads=[ptr[1][:]], writes=[qf[:]], tag=tg + "trqev")
            pairs = [(xaT[:, k * 128:(k + 1) * 128],
                      wkva[:, k * 160:(k + 1) * 160]) for k in range(8)]
            matmul_group(p, pq[0][:, 0:160], pairs, tg + "ckmm")
            p.op("vector", lambda e: e.tensor_copy(cktok[:, 0:160], pq[0][:, 0:160]),
                 reads=[pq[0][:]], writes=[cktok[:, 0:160]], tag=tg + "ckev")
            k3e = cktok[:, 128:144]
            k3o = cktok[:, 144:160]
            ce2 = ropec[:, it * 16:(it + 1) * 16]
            se2 = ropes[:, it * 16:(it + 1) * 16]
            p.op("vector", lambda e: e.tensor_mul(ropescr[:, 0:16], k3e, ce2),
                 reads=[cktok[:]], writes=[ropescr[:, 0:16]], tag=tg + "kr1")
            p.op("vector", lambda e: e.tensor_mul(ropescr[:, 16:32], k3o, se2),
                 reads=[cktok[:]], writes=[ropescr[:, 16:32]], tag=tg + "kr2")
            p.op("vector", lambda e: e.tensor_mul(ropescr[:, 32:48], k3e, se2),
                 reads=[cktok[:]], writes=[ropescr[:, 32:48]], tag=tg + "kr3")
            p.op("vector", lambda e: e.tensor_mul(ropescr[:, 48:64], k3o, ce2),
                 reads=[cktok[:]], writes=[ropescr[:, 48:64]], tag=tg + "kr4")
            p.op("vector", lambda e: e.tensor_sub(cktok[:, 160:176], ropescr[:, 0:16],
                                                  ropescr[:, 16:32]),
                 reads=[ropescr[:, 0:32]], writes=[cktok[:, 160:176]], tag=tg + "kr5")
            p.op("vector", lambda e: e.tensor_add(cktok[:, 176:192], ropescr[:, 32:48],
                                                  ropescr[:, 48:64]),
                 reads=[ropescr[:, 32:64]], writes=[cktok[:, 176:192]], tag=tg + "kr6")
            transpose_tile(ctx, ptr[0][:, 0:128], cktok[:, 0:128], 128, tag=tg + "trck")
            p.op("vector", lambda e, it=it: e.tensor_copy(
                ckvf[:, it * 128:(it + 1) * 128], ptr[0][:, 0:128]),
                reads=[ptr[0][:]], writes=[ckvf[:, it * 128:(it + 1) * 128]],
                tag=tg + "trckev")
            transpose_tile(ctx, ptr[1][0:32, 0:128], cktok[:, 160:192], 128, tag=tg + "trkr")
            p.op("vector", lambda e, it=it: e.tensor_copy(
                krr[:, it * 128:(it + 1) * 128], ptr[1][0:32, 0:128]),
                reads=[ptr[1][:]], writes=[krr[:, it * 128:(it + 1) * 128]],
                tag=tg + "krcp")
        for ft in range(8):
            for tc in range(4):
                pairs = [(wkvb[:, ft * 128:(ft + 1) * 128],
                          ckvf[:, tc * 512:(tc + 1) * 512])]
                matmul_group(p, pq[tc % 2][:], pairs, tb + "kvmm")
                p.op("vector", lambda e, ft=ft, tc=tc: e.tensor_copy(
                    kvf[:, ft * 2048 + tc * 512: ft * 2048 + tc * 512 + 512],
                    pq[tc % 2][:]),
                    reads=[pq[tc % 2][:]],
                    writes=[kvf[:, ft * 2048 + tc * 512: ft * 2048 + tc * 512 + 512]],
                    tag=tb + "kvev")
        for ft in range(8):
            for sl in range(2):
                dst = kvf[sl * 64 + 32: sl * 64 + 64, ft * 2048:(ft + 1) * 2048]
                p.op("vector", lambda e, d=dst: e.tensor_copy(d, krr[:]),
                     reads=[krr[:]], writes=[dst], tag=tb + "krfill")
        for kt in range(16):
            for fg in range(2):
                pb = ptr[fg]
                for fj in range(4):
                    ft = fg * 4 + fj
                    transpose_tile(ctx, pb[:, fj * 128:(fj + 1) * 128],
                                   kvf[:, ft * 2048 + kt * 128: ft * 2048 + kt * 128 + 128],
                                   128, tag=tb + "trv")
                for fj in range(4):
                    ft = fg * 4 + fj
                    for j in range(2):
                        h = ft * 2 + j
                        dst = vaug[:, kt * 528 + h * 33: kt * 528 + h * 33 + 32]
                        src_ = pb[:, fj * 128 + j * 64: fj * 128 + j * 64 + 32]
                        p.op("vector", lambda e, d=dst, s=src_: e.tensor_copy(d, s),
                             reads=[pb[:]], writes=[dst], tag=tb + "vev")
        for h in range(16):
            hb = (h % 2) * 64
            fb = (h // 2) * 2048
            for qc in range(4):
                m = h * 4 + qc
                pZv = pZ[m % 2]
                ya = yaug[m % 2]
                ptv = ptr[m % 2]
                zc = (m % 2) * 4
                nkt = 4 * qc + 4
                for kt in range(nkt):
                    ps = pS[kt % 2]
                    pairs = [
                        (kvf[hb:hb + 64, fb + kt * 128: fb + kt * 128 + 128],
                         qf[hb:hb + 64, fb + qc * 512: fb + qc * 512 + 512]),
                    ]
                    matmul_group(p, ps[:], pairs, tb + "sc%d_%d_%d" % (h, qc, kt))
                    Pt = Pb[kt % 2]
                    p.op("scalar", lambda e, d=Pt, s=ps: e.activation(
                        d[:], s[:], AF.Exp, bias=cpool[:, 1:2], scale=cpool[:, 0:1]),
                        reads=[ps[:], cpool[:]], writes=[Pt[:]],
                        tag=tb + "exp%d_%d_%d" % (h, qc, kt))
                    if kt >= 4 * qc:
                        base = qc * 512 - kt * 128
                        p.op("gpsimd", lambda e, d=Pt, base=base: e.affine_select(
                            d[:], d[:], [[1, 512]], ALU.is_ge, 0.0, base=base,
                            channel_multiplier=-1),
                            reads=[Pt[:]], writes=[Pt[:]],
                            tag=tb + "msk%d_%d_%d" % (h, qc, kt))
                    p.op("tensor", lambda e, l=vaug[:, kt * 528 + h * 33: kt * 528 + h * 33 + 33],
                         rr=Pt, s=(kt == 0), st=(kt == nkt - 1), o=pZv: e.matmul(
                             o[0:33, :], l, rr[:], start=s, stop=st),
                         reads=[vaug[:], Pt[:]], writes=[pZv[:]],
                         tag=tb + "pv%d_%d_%d" % (h, qc, kt))
                p.op("vector", lambda e, ya=ya, o=pZv: e.tensor_copy(ya[:], o[0:33, :]),
                     reads=[pZv[:]], writes=[ya[:]], tag=tb + "yev%d_%d" % (h, qc))
                for j in range(4):
                    transpose_tile(ctx, ptv[:, j * 34:j * 34 + 33],
                                   ya[:, j * 128:(j + 1) * 128], 33,
                                   tag=tb + "try%d_%d_%d" % (h, qc, j))
                p.op("vector", lambda e, ptv=ptv, zc=zc: e.reciprocal(
                    zr[:, zc:zc + 4],
                    ptv[:, 0:136].rearrange("p (j c) -> p j c", c=34)[:, :, 32:33]),
                    reads=[ptv[:]], writes=[zr[:, zc:zc + 4]], tag=tb + "zr%d_%d" % (h, qc))
                for j in range(4):
                    qtile = qc * 4 + j
                    dst = ytok[:, qtile * 512 + h * 32: qtile * 512 + h * 32 + 32]
                    p.op("vector", lambda e, d=dst, j=j, ptv=ptv, zc=zc: e.tensor_scalar(
                        d, ptv[:, j * 34:j * 34 + 32], zr[:, zc + j:zc + j + 1],
                        None, ALU.mult),
                        reads=[ptv[:], zr[:, zc:zc + 4]], writes=[dst],
                        tag=tb + "ynm%d_%d_%d" % (h, qc, j))
        for it in range(16):
            tg = tb + "o%d_" % it
            row = toff + it * 128
            for j in range(4):
                transpose_tile(ctx, ptr[0][:, j * 128:(j + 1) * 128],
                               ytok[:, it * 512 + j * 128: it * 512 + (j + 1) * 128],
                               128, tag=tg + "tro")
            p.op("vector", lambda e: e.tensor_copy(xaT[:, 0:512], ptr[0][:, 0:512]),
                 reads=[ptr[0][:]], writes=[xaT[:, 0:512]], tag=tg + "troev")
            xi = xi_r[it % 2]
            p.dma("sync", xi[:], xb_d[row:row + 128, :], tag=tg + "ldxb")
            for sc in range(2):
                pairs = [(xaT[:, k * 128:(k + 1) * 128],
                          woe[:, k * 1024 + sc * 512: k * 1024 + sc * 512 + 512])
                         for k in range(4)]
                matmul_group(p, pO[:], pairs, tg + "omm%d" % sc)
                p.op("vector", lambda e, sc=sc, xi=xi: e.tensor_add(
                    obuf[:, sc * 512:(sc + 1) * 512], pO[:],
                    xi[:, sc * 512:(sc + 1) * 512]),
                    reads=[pO[:], xi[:]], writes=[obuf[:, sc * 512:(sc + 1) * 512]],
                    tag=tg + "oadd%d" % sc)
            p.dma("sync", t_d[row:row + 128, :], obuf[:], tag=tg + "stt")

    ctx.stack = old
    sub.close()


def build_phase3b(ctx, xin_d, t_d, xout_d, din, cin):
    nc, p = ctx.nc, ctx.prog
    sub = contextlib.ExitStack()
    old = ctx.stack
    ctx.stack = sub

    wgate = ctx.sb("p3b_wgate", [128, 8 * 8], BF16)
    p.dma("gpsimd", wgate[:].rearrange("p (a n) -> p a n", a=8),
          din["L1_ffn_Wgate"].rearrange("(a p) n -> p a n", p=128), tag="p3b_ldwg")
    ebb = ctx.sb("p3b_ebb", [128, 8])
    p.dma("sync", ebb[:], cin["c_expbias_b"], tag="p3b_ldeb")
    wln2b = ctx.sb("p3b_wln2b", [128, D])
    p.dma("sync", wln2b[:], cin["c_wln2_b"], tag="p3b_ldln2")

    NW = 3
    wg_b = [ctx.sb("p3b_wgb%d" % i, [128, 8 * 512], BF16) for i in range(NW)]
    wu_b = [ctx.sb("p3b_wub%d" % i, [128, 8 * 512], BF16) for i in range(NW)]
    wd_b = [ctx.sb("p3b_wdb%d" % i, [128, 4 * 1024], BF16) for i in range(NW)]

    xmT = ctx.sb("p3b_xmT", [128, 8 * 1024], BF16)
    acc = ctx.sb("p3b_acc", [128, 8 * 1024])
    comb = ctx.sb("p3b_comb", [128, 8 * 16])
    gbuf = ctx.sb("p3b_g", [128, 4 * 512], BF16)
    gated = ctx.sb("p3b_gated", [128, 4 * 512], BF16)
    xt_r = [ctx.sb("p3b_xt%d" % i, [128, D]) for i in range(2)]
    xmb = ctx.sb("p3b_xmb", [128, D], BF16)
    gsc = ctx.sb("p3b_gsc", [128, 16])
    bnt = ctx.sb("p3b_bn", [128, 12])
    stt = ctx.sb("p3b_st", [128, 8])
    obuf = ctx.sb("p3b_ob", [128, D])

    pM = [ctx.ps("p3b_pM%d" % i, [128, 512]) for i in range(4)]
    ptr = [ctx.ps("p3b_ptr%d" % i, [128, 1024], BF16) for i in range(2)]

    WNAMES = [("L1_ffn_sWg", "L1_ffn_sWu", "L1_ffn_sWd", None)] + \
             [("L1_ffn_Wg", "L1_ffn_Wu", "L1_ffn_Wd", e) for e in range(E)]

    for u in range(4):
        tu = "p3u%d_" % u
        for tt in range(8):
            it = u * 8 + tt
            tg = tu + "t%d_" % tt
            xt = xt_r[tt % 2]
            p.dma("sync", xt[:], t_d[it * 128:(it + 1) * 128, :], tag=tg + "ldt")
            p.op("scalar", lambda e, xt=xt: e.activation(xmb[:], xt[:], AF.Square,
                                                         accum_out=stt[:, 4:5]),
                 reads=[xt[:]], writes=[xmb[:], stt[:, 4:5]], tag=tg + "rsq")
            p.op("vector", lambda e: e.tensor_scalar(stt[:, 4:5], stt[:, 4:5],
                                                     1.0 / D, EPS, ALU.mult, ALU.add),
                 reads=[stt[:, 4:5]], writes=[stt[:, 4:5]], tag=tg + "r1")
            p.op("scalar", lambda e: e.activation(stt[:, 5:6], stt[:, 4:5], AF.Sqrt),
                 reads=[stt[:, 4:5]], writes=[stt[:, 5:6]], tag=tg + "r2")
            p.op("vector", lambda e: e.reciprocal(stt[:, 4:5], stt[:, 5:6]),
                 reads=[stt[:, 5:6]], writes=[stt[:, 4:5]], tag=tg + "r3")
            p.op("vector", lambda e, xt=xt: e.tensor_scalar(obuf[:], xt[:], stt[:, 4:5],
                                                            None, ALU.mult),
                 reads=[xt[:], stt[:, 4:5]], writes=[obuf[:]], tag=tg + "rap")
            p.op("gpsimd", lambda e: e.tensor_mul(xmb[:], obuf[:], wln2b[:]),
                 reads=[obuf[:], wln2b[:]], writes=[xmb[:]], tag=tg + "wl2")
            for k in range(8):
                transpose_tile(ctx, ptr[tt % 2][:, k * 128:(k + 1) * 128],
                               xmb[:, k * 128:(k + 1) * 128], 128, tag=tg + "trm")
            p.op("vector", lambda e, tt=tt: e.tensor_copy(
                xmT[:].rearrange("p (k t) -> p k t", t=1024)[:, :, tt * 128:(tt + 1) * 128],
                ptr[tt % 2][:].rearrange("p (k t) -> p k t", t=128)),
                reads=[ptr[tt % 2][:]], writes=[xmT[:]], tag=tg + "trmev")
            pairs = [(xmT[:].rearrange("p (k t) -> p k t", t=1024)[:, k, tt * 128:(tt + 1) * 128],
                      wgate[:, k * 8:(k + 1) * 8]) for k in range(8)]
            matmul_group(p, pM[3][:, 0:8], pairs, tg + "gt")
            p.op("vector", lambda e: e.tensor_add(gsc[:, 0:8], pM[3][:, 0:8], ebb[:]),
                 reads=[pM[3][:], ebb[:]], writes=[gsc[:, 0:8]], tag=tg + "gt1")
            p.op("vector", lambda e: e.reduce_max(gsc[:, 8:9], gsc[:, 0:8], axis=AX.X),
                 reads=[gsc[:, 0:8]], writes=[gsc[:, 8:9]], tag=tg + "gt2")
            p.op("vector", lambda e: e.tensor_scalar(gsc[:, 8:9], gsc[:, 8:9], -1.0,
                                                     None, ALU.mult),
                 reads=[gsc[:, 8:9]], writes=[gsc[:, 8:9]], tag=tg + "gt3")
            p.op("scalar", lambda e: e.activation(gsc[:, 0:8], gsc[:, 0:8], AF.Exp,
                                                  bias=gsc[:, 8:9]),
                 reads=[gsc[:, 0:9]], writes=[gsc[:, 0:8]], tag=tg + "gt4")
            p.op("vector", lambda e: e.reduce_sum(gsc[:, 9:10], gsc[:, 0:8], axis=AX.X),
                 reads=[gsc[:, 0:8]], writes=[gsc[:, 9:10]], tag=tg + "gt5")
            p.op("vector", lambda e: e.reciprocal(gsc[:, 10:11], gsc[:, 9:10]),
                 reads=[gsc[:, 9:10]], writes=[gsc[:, 10:11]], tag=tg + "gt6")
            p.op("vector", lambda e: e.tensor_scalar(gsc[:, 0:8], gsc[:, 0:8],
                                                     gsc[:, 10:11], None, ALU.mult),
                 reads=[gsc[:, 0:11]], writes=[gsc[:, 0:8]], tag=tg + "gt7")
            c8 = comb[:, tt * 16:tt * 16 + 8]
            e8 = comb[:, tt * 16 + 8:tt * 16 + 16]
            csh = comb[:, tt * 16 + 8:tt * 16 + 9]
            p.op("vector", lambda e: e.reduce_max(gsc[:, 8:9], gsc[:, 0:8], axis=AX.X),
                 reads=[gsc[:, 0:8]], writes=[gsc[:, 8:9]], tag=tg + "tp1")
            p.op("vector", lambda e, e8=e8: e.tensor_scalar(e8, gsc[:, 0:8],
                                                            gsc[:, 8:9], None,
                                                            ALU.is_equal),
                 reads=[gsc[:, 0:9]], writes=[e8], tag=tg + "tp2")
            p.op("vector", lambda e, c8=c8, e8=e8: e.scalar_tensor_tensor(
                c8, e8, -2.0, gsc[:, 0:8], ALU.mult, ALU.add),
                reads=[e8, gsc[:, 0:8]], writes=[c8], tag=tg + "tp3")
            p.op("vector", lambda e, c8=c8: e.reduce_max(gsc[:, 9:10], c8, axis=AX.X),
                 reads=[c8], writes=[gsc[:, 9:10]], tag=tg + "tp4")
            p.op("vector", lambda e: e.tensor_add(gsc[:, 10:11], gsc[:, 8:9],
                                                  gsc[:, 9:10]),
                 reads=[gsc[:, 8:10]], writes=[gsc[:, 10:11]], tag=tg + "tp5")
            p.op("vector", lambda e: e.reciprocal(gsc[:, 11:12], gsc[:, 10:11]),
                 reads=[gsc[:, 10:11]], writes=[gsc[:, 11:12]], tag=tg + "tp6")
            p.op("vector", lambda e, c8=c8: e.tensor_scalar(c8, gsc[:, 0:8],
                                                            gsc[:, 9:10], None,
                                                            ALU.is_equal),
                 reads=[gsc[:, 0:10]], writes=[c8], tag=tg + "tp7")
            p.op("vector", lambda e: e.tensor_scalar(gsc[:, 12:13], gsc[:, 8:9],
                                                     gsc[:, 11:12], None, ALU.mult),
                 reads=[gsc[:, 8:12]], writes=[gsc[:, 12:13]], tag=tg + "tp8")
            p.op("vector", lambda e: e.tensor_scalar(gsc[:, 13:14], gsc[:, 9:10],
                                                     gsc[:, 11:12], None, ALU.mult),
                 reads=[gsc[:, 9:12]], writes=[gsc[:, 13:14]], tag=tg + "tp9")
            p.op("vector", lambda e, c8=c8: e.tensor_scalar(c8, c8, gsc[:, 13:14],
                                                            None, ALU.mult),
                 reads=[c8, gsc[:, 13:14]], writes=[c8], tag=tg + "tp10")
            p.op("vector", lambda e, c8=c8, e8=e8: e.scalar_tensor_tensor(
                c8, e8, gsc[:, 12:13], c8, ALU.mult, ALU.add),
                reads=[e8, gsc[:, 12:13], c8], writes=[c8], tag=tg + "tp11")
            p.op("vector", lambda e, csh=csh: e.memset(csh, 1.0),
                 reads=[], writes=[csh], tag=tg + "tp12")
        for ei, (gn, un, dn, eidx) in enumerate(WNAMES):
            wgs, wus, wds = wg_b[ei % NW], wu_b[ei % NW], wd_b[ei % NW]
            tg = tu + "e%d_" % ei
            gsrc = din[gn][0] if eidx is None else din[gn][eidx]
            usrc = din[un][0] if eidx is None else din[un][eidx]
            dsrc = din[dn][0] if eidx is None else din[dn][eidx]
            if True:
                p.dma("gpsimd", wgs[:].rearrange("p (a n) -> p a n", a=8),
                      gsrc.rearrange("(a p) n -> p a n", p=128), tag=tg + "ldg")
                p.dma("gpsimd", wus[:].rearrange("p (a n) -> p a n", a=8),
                      usrc.rearrange("(a p) n -> p a n", p=128), tag=tg + "ldu")
                p.dma("gpsimd", wds[:].rearrange("p (a n) -> p a n", a=4),
                      dsrc.rearrange("(a p) n -> p a n", p=128), tag=tg + "ldd")
            ccol = 8 if eidx is None else eidx
            for blk in range(2):
                for ft in range(4):
                    pairs = [(wgs[:, k * 512 + ft * 128: k * 512 + ft * 128 + 128],
                              xmT[:].rearrange("p (k t) -> p k t", t=1024)
                              [:, k, blk * 512:(blk + 1) * 512]) for k in range(8)]
                    matmul_group(p, pM[ft % 2][:], pairs, tg + "gmm%d_%d" % (blk, ft))
                    p.op("scalar", lambda e, ft=ft: e.activation(
                        gbuf[:, ft * 512:(ft + 1) * 512], pM[ft % 2][:], AF.Silu),
                        reads=[pM[ft % 2][:]], writes=[gbuf[:, ft * 512:(ft + 1) * 512]],
                        tag=tg + "sil%d_%d" % (blk, ft))
                for ft in range(4):
                    pairs = [(wus[:, k * 512 + ft * 128: k * 512 + ft * 128 + 128],
                              xmT[:].rearrange("p (k t) -> p k t", t=1024)
                              [:, k, blk * 512:(blk + 1) * 512]) for k in range(8)]
                    matmul_group(p, pM[2 + ft % 2][:], pairs, tg + "umm%d_%d" % (blk, ft))
                    p.op("vector", lambda e, ft=ft: e.tensor_mul(
                        gated[:, ft * 512:(ft + 1) * 512],
                        gbuf[:, ft * 512:(ft + 1) * 512], pM[2 + ft % 2][:]),
                        reads=[pM[2 + ft % 2][:], gbuf[:, ft * 512:(ft + 1) * 512]],
                        writes=[gated[:, ft * 512:(ft + 1) * 512]],
                        tag=tg + "gu%d_%d" % (blk, ft))
                for ts in range(4):
                    tt = blk * 4 + ts
                    for sc in range(2):
                        pairs = [(gated[:].rearrange("p (i t) -> p i t", t=512)
                                  [:, i, ts * 128:(ts + 1) * 128],
                                  wds[:, i * 1024 + sc * 512: i * 1024 + sc * 512 + 512])
                                 for i in range(4)]
                        ps = pM[sc]
                        matmul_group(p, ps[:], pairs, tg + "dmm%d_%d" % (tt, sc))
                        dst = acc[:, tt * 1024 + sc * 512: tt * 1024 + sc * 512 + 512]
                        cap = comb[:, tt * 16 + ccol: tt * 16 + ccol + 1]
                        if ei == 0:
                            p.op("vector", lambda e, d=dst, s=ps, c=cap: e.tensor_scalar(
                                d, s[:], c, None, ALU.mult),
                                reads=[ps[:], comb[:]], writes=[dst],
                                tag=tg + "acc%d_%d" % (tt, sc))
                        else:
                            p.op("vector", lambda e, d=dst, s=ps, c=cap:
                                 e.scalar_tensor_tensor(d, s[:], c, d, ALU.mult, ALU.add),
                                 reads=[ps[:], comb[:], dst], writes=[dst],
                                 tag=tg + "acc%d_%d" % (tt, sc))
        for tt in range(8):
            it = u * 8 + tt
            tg = tu + "f%d_" % tt
            xt = xt_r[tt % 2]
            p.dma("sync", xt[:], t_d[it * 128:(it + 1) * 128, :], tag=tg + "ldt2")
            p.op("gpsimd", lambda e, xt=xt, tt=tt: e.tensor_add(
                obuf[:], xt[:], acc[:, tt * 1024:(tt + 1) * 1024]),
                reads=[xt[:], acc[:]], writes=[obuf[:]], tag=tg + "tadd")
            emit_ln_stats(ctx, obuf[:], D, bnt, stt, tag=tg + "ln")
            emit_ln_apply(ctx, obuf[:], stt, obuf[:], tag=tg + "ln")
            p.dma("sync", xt[:], xin_d[it * 128:(it + 1) * 128, :], tag=tg + "ldx3")
            p.op("vector", lambda e, xt=xt: e.tensor_add(obuf[:], obuf[:], xt[:]),
                 reads=[obuf[:], xt[:]], writes=[obuf[:]], tag=tg + "xadd")
            p.dma("sync", xout_d[it * 128:(it + 1) * 128, :], obuf[:], tag=tg + "stx")

    ctx.stack = old
    sub.close()



def kernel(**inputs):
    x0 = np.asarray(inputs["L0_attn_x"], np.float32)
    phases = [0, 1, 2, 3]
    nc = build_program(phases)
    in_maps = make_in_maps(inputs, phases, x0)
    res = run_bass_kernel_spmd(nc, in_maps, core_ids=list(range(8)))
    out = np.concatenate(
        [res.results[c]["x_out"].reshape(2, SEQ, D) for c in range(8)], axis=0)
    return out.astype(np.float32)


# version 16 (final)
# speedup vs baseline: 1.1357x; 1.0151x over previous
import contextlib
import functools
import numpy as np

import concourse.bass as bass
import concourse.mybir as mybir
from concourse.bass_utils import run_bass_kernel_spmd

ENGINES = ("tensor", "vector", "scalar", "gpsimd", "sync")
PE_DRAIN_NOP = 64
DRAM_SETTLE_NOP = 2048
REORDER = True


def ap_region(ap):
    t = ap.tensor
    name = t.name
    dims = list(ap.ap)
    space = str(ap.space)
    is_psum = "PSUM" in space.upper()
    is_dram = "RAM" in space.upper() and not is_psum and "SB" not in space.upper()
    if is_psum:
        return (name, 0, 128, 0, 1 << 40, True)
    if is_dram:
        off = ap.offset if isinstance(ap.offset, int) else 0
        span = 1
        for st, ct in dims:
            span += abs(int(st)) * (int(ct) - 1)
        return (name, 0, 1, int(off), int(off) + span, False)
    try:
        p0 = ap.base_partition()
    except Exception:
        dims = list(ap.ap)
        rp = int(dims[0][0]) if dims else 1
        p0 = int(ap.offset) // rp if rp else 0
    psz = ap.partition_size()
    row_pitch = int(dims[0][0]) if dims else 1
    off = ap.offset if isinstance(ap.offset, int) else 0
    col0 = int(off) - p0 * row_pitch if row_pitch else int(off)
    span = 1
    for st, ct in dims[1:]:
        span += abs(int(st)) * (int(ct) - 1)
    if col0 < 0:
        col0, span = 0, 1 << 40
    return (name, p0, p0 + psz, col0, col0 + span, False)


def _overlap(a, b):
    return (a[1] < b[2] and b[1] < a[2] and a[3] < b[4] and b[3] < a[4])


def _covers(a, b):
    return (a[1] <= b[1] and a[2] >= b[2] and a[3] <= b[3] and a[4] >= b[4])


class Op:
    __slots__ = ("idx", "eng", "emit", "reads", "writes", "is_dma", "dma_sem_key",
                 "tag", "waits", "inc", "drain", "settle")

    def __init__(self, idx, eng, emit, reads, writes, is_dma, dma_sem_key, tag):
        self.idx = idx
        self.eng = eng
        self.emit = emit
        self.reads = reads
        self.writes = writes
        self.is_dma = is_dma
        self.dma_sem_key = dma_sem_key
        self.tag = tag
        self.waits = []
        self.inc = None
        self.drain = False
        self.settle = False


class Prog:
    def __init__(self, same_engine_sync=True, pe_sync=False):
        self.ops = []
        self.same_engine_sync = same_engine_sync
        self.pe_sync = pe_sync

    def _mk_regions(self, reads, writes):
        r, w = [], []
        for a in reads:
            reg = a if isinstance(a, tuple) else ap_region(a)
            (w if reg[5] else r).append(reg)
        for a in writes:
            reg = a if isinstance(a, tuple) else ap_region(a)
            w.append(reg)
        return r, w

    def op(self, eng, emit, reads=(), writes=(), tag=""):
        assert eng in ENGINES, eng
        r, w = self._mk_regions(reads, writes)
        self.ops.append(Op(len(self.ops), eng, emit, r, w, False, None, tag))

    def dma(self, queue, out_ap, in_ap, tag="", transpose=False, sem_key=None):
        assert queue in ("sync", "scalar", "gpsimd")
        oreg = ap_region(out_ap)
        if sem_key is None:
            sem_key = "D_%s@%d_%d" % (oreg[0], oreg[1], oreg[3])
        if transpose:
            def emit(eng, o=out_ap, i=in_ap):
                return eng.dma_start_transpose(o, i)
        else:
            def emit(eng, o=out_ap, i=in_ap):
                return eng.dma_start(out=o, in_=i)
        r, w = self._mk_regions([in_ap], [out_ap])
        self.ops.append(Op(len(self.ops), queue, emit, r, w, True, sem_key, tag))

    def barrier(self, tag="barrier"):
        self.ops.append(Op(len(self.ops), "barrier", None, [], [], False, None, tag))

    def _op_cost(self, op):
        if op.is_dma:
            span = 1
            for w in op.writes:
                span = max(span, (w[2] - w[1]) * (w[4] - w[3]))
            bytes_ = span * 4
            return 120.0, 1400.0 + bytes_ / 250.0
        if op.eng == "tensor":
            n = 64
            for r in op.reads:
                if not r[5]:
                    n = max(n, min(r[4] - r[3], 1024))
            return 40.0 + n * 0.42, 60.0
        n = 64
        for w in op.writes:
            if not w[5]:
                n = max(n, min(w[4] - w[3], 8192))
        slope = {"vector": 1.05, "scalar": 0.85, "gpsimd": 0.9, "sync": 0.2}[op.eng]
        base = {"vector": 190.0, "scalar": 330.0, "gpsimd": 260.0, "sync": 60.0}[op.eng]
        return base + n * slope, 0.0

    def reorder(self):
        import heapq
        skip = getattr(self, "no_reorder_segs", set())
        mode = getattr(self, "reorder_mode", "all")
        segs = []
        cur = []
        for op in self.ops:
            if op.eng == "barrier":
                segs.append(cur)
                segs.append([op])
                cur = []
            else:
                cur.append(op)
        segs.append(cur)

        new_ops = []
        pred_makespan = 0.0
        segidx = -1
        for seg in segs:
            if len(seg) == 1 and seg[0].eng == "barrier":
                new_ops.extend(seg)
                continue
            segidx += 1
            if segidx in skip:
                new_ops.extend(seg)
                continue
            hist = {}
            preds = {op.idx: set() for op in seg}
            if mode != "all":
                prev = None
                for op in seg:
                    pin = True if mode == "none" else ((not op.is_dma) if mode == "dma" else op.is_dma)
                    if pin:
                        if prev is not None:
                            preds[op.idx].add(prev)
                        prev = op.idx
            for op in seg:
                for reg in op.reads:
                    h = hist.get(reg[0])
                    if h:
                        for (o2, k2, r2) in reversed(h):
                            if k2 == "w" and _overlap(reg, r2):
                                preds[op.idx].add(o2.idx)
                                if _covers(r2, reg):
                                    break
                for reg in op.writes:
                    h = hist.get(reg[0])
                    if h:
                        for (o2, k2, r2) in reversed(h):
                            if _overlap(reg, r2):
                                preds[op.idx].add(o2.idx)
                                if k2 == "w" and _covers(r2, reg):
                                    break
                for reg in op.reads:
                    hist.setdefault(reg[0], []).append((op, "r", reg))
                for reg in op.writes:
                    hist.setdefault(reg[0], []).append((op, "w", reg))
            succs = {op.idx: [] for op in seg}
            npred = {}
            for op in seg:
                npred[op.idx] = len(preds[op.idx])
                for pp in preds[op.idx]:
                    succs[pp].append(op.idx)
            byid = {op.idx: op for op in seg}
            lane_free = {}
            ready_t = {op.idx: 0.0 for op in seg}
            ready = {e: [] for e in ENGINES}
            for op in seg:
                if npred[op.idx] == 0:
                    heapq.heappush(ready[op.eng], op.idx)
            done_t = {}
            order = []
            npend = len(seg)
            clock = 0.0
            import itertools
            while npend:
                best = None
                for e in ENGINES:
                    while ready[e] and ready[e][0] in done_t:
                        heapq.heappop(ready[e])
                    if not ready[e]:
                        continue
                    i = ready[e][0]
                    st = max(lane_free.get(e, 0.0), ready_t[i])
                    if best is None or (st, i) < best[0]:
                        best = ((st, i), e)
                if best is None:
                    raise RuntimeError("scheduler stuck")
                (st, i), e = best
                heapq.heappop(ready[e])
                op = byid[i]
                cost, extra = self._op_cost(op)
                lane_free[e] = st + cost
                dt = st + cost + extra
                done_t[i] = dt
                order.append(op)
                npend -= 1
                for sj in succs[i]:
                    npred[sj] -= 1
                    if ready_t[sj] < dt:
                        ready_t[sj] = dt
                    if npred[sj] == 0:
                        heapq.heappush(ready[byid[sj].eng], sj)
            pred_makespan += max(done_t.values()) if done_t else 0.0
            new_ops.extend(order)
        for k, op in enumerate(new_ops):
            op.idx = k
        self.ops = new_ops
        self.predicted_ns = pred_makespan
        return pred_makespan

    def schedule(self):
        if REORDER:
            self.reorder()
        deps = [None] * len(self.ops)
        hist = {}

        bar_pending = None
        bar_seen = None

        last_per_stream = {}

        for op in self.ops:
            if op.eng == "barrier":
                bar_pending = set(last_per_stream.values())
                bar_seen = set()
                continue
            prods = set()
            if bar_pending is not None and op.eng not in bar_seen:
                prods |= bar_pending
                bar_seen.add(op.eng)
            for reg in op.reads:
                h = hist.get(reg[0])
                if h:
                    for (o2, k2, r2) in reversed(h):
                        if k2 == "w" and _overlap(reg, r2):
                            prods.add(o2.idx)
                            if _covers(r2, reg):
                                break
            for reg in op.writes:
                h = hist.get(reg[0])
                if h:
                    for (o2, k2, r2) in reversed(h):
                        if _overlap(reg, r2):
                            prods.add(o2.idx)
                            if k2 == "w" and _covers(r2, reg):
                                break
            for reg in op.reads:
                hist.setdefault(reg[0], []).append((op, "r", reg))
            for reg in op.writes:
                hist.setdefault(reg[0], []).append((op, "w", reg))
            last_per_stream[("dma", op.dma_sem_key) if op.is_dma else op.eng] = op.idx
            keep = set()
            for p in prods:
                po = self.ops[p]
                if po.is_dma:
                    keep.add(p)
                elif po.eng == op.eng and not op.is_dma:
                    if po.eng == "tensor" and not self.pe_sync:
                        continue
                    if self.same_engine_sync:
                        keep.add(p)
                else:
                    keep.add(p)
            deps[op.idx] = keep

        needed = set()
        for op in self.ops:
            if op.eng != "barrier":
                needed |= deps[op.idx]
        last_of_eng = {}
        for op in self.ops:
            if op.eng != "barrier" and not op.is_dma:
                last_of_eng[op.eng] = op.idx
        needed |= set(last_of_eng.values())

        phase_of = {}
        ph = 0
        for op in self.ops:
            if op.eng == "barrier":
                ph += 1
            else:
                phase_of[op.idx] = ph
        NPOOL = 80
        pool_counter = {}
        key2handle = {}
        inc_count = {}

        def handle_for(op):
            k = (phase_of[op.idx], op.dma_sem_key)
            h = key2handle.get(k)
            if h is None:
                slot = pool_counter.get(k[0], 0)
                assert slot < NPOOL, "DMA sem pool exhausted in phase %d" % k[0]
                pool_counter[k[0]] = slot + 1
                h = "Dp%d" % slot
                key2handle[k] = h
            return h

        for op in self.ops:
            if op.eng == "barrier":
                continue
            if op.is_dma:
                key = handle_for(op)
                val = inc_count.get(key, 0) + 16
                inc_count[key] = val
                op.inc = (key, val)
            elif op.idx in needed:
                key = "E_" + op.eng
                val = inc_count.get(key, 0) + 1
                inc_count[key] = val
                op.inc = (key, val)

        observed = {e: {} for e in ENGINES}
        drained = {e: 0 for e in ENGINES}
        for op in self.ops:
            if op.eng == "barrier":
                continue
            obs = observed[op.eng]
            want = {}
            psum_tensor_dep = 0
            reads_psum = any(r[5] for r in op.reads) or any(w[5] for w in op.writes)
            for p in deps[op.idx]:
                po = self.ops[p]
                inc = po.inc
                if inc is None:
                    continue
                key, val = inc
                if want.get(key, 0) < val:
                    want[key] = val
                if po.eng == "tensor" and not po.is_dma and reads_psum:
                    if val > psum_tensor_dep and key == "E_tensor":
                        psum_tensor_dep = val
            for key, val in sorted(want.items()):
                if obs.get(key, 0) < val:
                    op.waits.append((key, val))
                    obs[key] = val
            if (op.eng != "tensor" and psum_tensor_dep > 0):
                op.drain = True
                drained[op.eng] = psum_tensor_dep


        per_eng = {e: [o for o in self.ops if o.eng == e] for e in ENGINES}
        self._final_counts = dict(inc_count)
        return per_eng, sorted(inc_count.keys())

    def finalize(self, nc, block, dump=None):
        per_eng, sem_keys = self.schedule()
        if dump:
            with open(dump, "w") as fh:
                for e in ENGINES:
                    for op in per_eng[e]:
                        fh.write(f"{e:7s} {op.idx:5d} {op.tag:24s} waits={op.waits} inc={op.inc}\n")
        sems = {}
        ctxs = []
        for key in sem_keys:
            cm = nc.semaphore(key[:30])
            sems[key] = cm.__enter__()
            ctxs.append(cm)

        def run_engine(eng_obj, eng_name):
            for op in per_eng[eng_name]:
                for (key, val) in op.waits:
                    eng_obj.wait_ge(sems[key], val)
                if op.drain:
                    eng_obj.nop(cycle_cnt=PE_DRAIN_NOP)
                if op.settle:
                    eng_obj.nop(cycle_cnt=DRAM_SETTLE_NOP)
                inst = op.emit(eng_obj)
                if op.inc is not None:
                    key, _ = op.inc
                    inst.then_inc(sems[key], 16 if op.is_dma else 1)

        for name in ENGINES:
            if per_eng[name]:
                getattr(block, name)(functools.partial(run_engine, eng_name=name))
        return ctxs




F32 = mybir.dt.float32
BF16 = mybir.dt.bfloat16
AF = mybir.ActivationFunctionType
ALU = mybir.AluOpType
AX = mybir.AxisListType

T = 4096
NT = 32
D = 1024
SEQ = 2048
EPS = 1e-5
DEBUG_WIN = False
DBG_D = None
DEBUG_P0_MEMSET = ()
DEBUG_OUTS = []


class Ctx:
    def __init__(self, nc, prog):
        self.nc = nc
        self.prog = prog
        self.stack = contextlib.ExitStack()
        self.epsc = None
        self.ident = None

    def sb(self, name, shape, dtype=F32):
        return self.stack.enter_context(self.nc.sbuf_tensor(name, shape, dtype))

    def ps(self, name, shape, dtype=F32):
        return self.stack.enter_context(self.nc.psum_tensor(name, shape, dtype))

    def close(self):
        self.stack.close()


def matmul_group(p, ps_ap, pairs, tag):
    n = len(pairs)
    for i, (l, r) in enumerate(pairs):
        p.op("tensor",
             lambda e, o=ps_ap, l=l, r=r, s=(i == 0), st=(i == n - 1):
                 e.matmul(o, l, r, start=s, stop=st),
             reads=[l, r], writes=[ps_ap], tag=tag)


def emit_ln_stats(ctx, src_f32, width, bn, st, tag=""):
    p = ctx.prog
    nb = (width + 511) // 512
    for c in range(nb):
        lo, hi = c * 512, min(width, (c + 1) * 512)
        p.op("vector", lambda e, c=c, lo=lo, hi=hi: e.bn_stats(
            bn[:, 6 * c:6 * c + 6], src_f32[:, lo:hi]),
            reads=[src_f32[:, lo:hi]], writes=[bn[:, 6 * c:6 * c + 6]], tag=tag + "_bn")
    p.op("vector", lambda e: e.bn_aggr(st[:, 0:2], bn[:, 0:6 * nb]),
         reads=[bn[:, 0:6 * nb]], writes=[st[:, 0:2]], tag=tag + "_aggr")
    p.op("scalar", lambda e: e.activation(st[:, 3:4], st[:, 1:2], AF.Sqrt,
                                          bias=ctx.epsc[:, 0:1]),
         reads=[st[:, 1:2], ctx.epsc[:, 0:1]], writes=[st[:, 3:4]], tag=tag + "_sqrt")
    p.op("vector", lambda e: e.reciprocal(st[:, 1:2], st[:, 3:4]),
         reads=[st[:, 3:4]], writes=[st[:, 1:2]], tag=tag + "_rcp")
    p.op("vector", lambda e: e.tensor_scalar(st[:, 2:3], st[:, 0:1], st[:, 1:2],
                                             -1.0, ALU.mult, ALU.mult),
         reads=[st[:, 0:2]], writes=[st[:, 2:3]], tag=tag + "_nb")


def emit_ln_apply(ctx, src, st, out, tag="", engine="scalar"):
    p = ctx.prog
    p.op(engine, lambda e: e.activation(out, src, AF.Identity,
                                        bias=st[:, 2:3], scale=st[:, 1:2]),
         reads=[src, st[:, 0:4]], writes=[out], tag=tag + "_apply")


def transpose_tile(ctx, out_ps_ap, in_ap, n_in, tag=""):
    p = ctx.prog
    ident = ctx.ident[0:n_in, 0:n_in]
    p.op("tensor", lambda e: e.transpose(out_ps_ap, in_ap, ident),
         reads=[in_ap, ctx.ident[:]], writes=[out_ps_ap], tag=tag + "_tr")


def ld8(p, dst, src, q, tag):
    p.dma(q, dst[:].rearrange("p (a n) -> p a n", a=8),
          src.rearrange("(a p) n -> p a n", p=128), tag=tag)



def build_phase0(ctx, xin_d, xout_d, din, cin):
    nc, p = ctx.nc, ctx.prog
    gb = ctx.sb("p0_gb", [128, D])
    bbt = ctx.sb("p0_bb", [128, D])
    p.dma("sync", gb[:], cin["c_l0_gamma_b"], tag="p0_ldg")
    p.dma("sync", bbt[:], cin["c_l0_beta_b"], tag="p0_ldb")

    wq = ctx.sb("p0_wq", [128, 8 * 1024], BF16)
    wk = ctx.sb("p0_wk", [128, 8 * 1024], BF16)
    wkp = ctx.sb("p0_wkp", [128, 8 * 1024], BF16)
    wv = ctx.sb("p0_wv", [128, 8 * 1024], BF16)
    wo = ctx.sb("p0_wo", [128, 8 * 1024], BF16)
    for dst, name in ((wq, "L0_attn_W_Q"), (wk, "L0_attn_W_K"),
                      (wkp, "L0_attn_W_Kp"), (wv, "L0_attn_W_V"),
                      (wo, "L0_attn_W_O")):
        ld8(p, dst, din[name], "gpsimd", "p0_ldw")

    selh = ctx.sb("p0_selh", [128, 2], F32)
    selhT = ctx.sb("p0_selhT", [2, 128], F32)
    p.dma("sync", selh[:], cin["c_selhalf"], tag="p0_lds")
    p.dma("sync", selhT[:], cin["c_selhalfT"], tag="p0_ldst")
    selj = ctx.sb("p0_selj", [32, 8 * 128], BF16)
    p.dma("sync", selj[:], cin["c_selj"], tag="p0_ldsj")
    selk = ctx.sb("p0_selk", [32, 128], BF16)
    p.dma("sync", selk[:], cin["c_selk"], tag="p0_ldsk")
    mask01 = ctx.sb("p0_mask01", [128, 8 * 32], BF16)
    p.dma("sync", mask01[:].rearrange("p (a n) -> p a n", a=8),
          cin["c_winmask01"].rearrange("(a k) n -> k a n", k=128), tag="p0_ldm")

    RD = 4
    ND = 4
    xin = [ctx.sb("p0_xin%d" % i, [128, D]) for i in range(RD)]
    xnf = [ctx.sb("p0_xn%d" % i, [128, D]) for i in range(ND)]
    abuf = [ctx.sb("p0_a%d" % i, [128, D]) for i in range(ND)]
    obuf = [ctx.sb("p0_o%d" % i, [128, D]) for i in range(ND)]
    bnt = [ctx.sb("p0_bn%d" % i, [128, 12]) for i in range(ND)]
    stt = [ctx.sb("p0_st%d" % i, [128, 8]) for i in range(ND)]

    w = {}
    w["xwb"] = ctx.sb("p0_xwb", [32, D], BF16)
    w["xwt"] = ctx.sb("p0_xwt", [128, 8 * 32], BF16)
    w["qf"] = ctx.sb("p0_qf", [128, 8 * 32], BF16)
    w["kf"] = ctx.sb("p0_kf", [128, 8 * 32], BF16)
    w["kpf"] = ctx.sb("p0_kpf", [128, 8 * 32], BF16)
    w["qF"] = ctx.sb("p0_qF", [128, 8 * 32])
    w["sqs"] = ctx.sb("p0_sqs", [128, 8 * 32])
    w["ssq2"] = ctx.sb("p0_ssq2", [2, 256])
    w["vt"] = ctx.sb("p0_vt", [32, D], BF16)
    w["mh"] = ctx.sb("p0_mh", [128, 1024], BF16)
    w["ph"] = ctx.sb("p0_ph", [128, 8 * 32], BF16)
    w["u2"] = ctx.sb("p0_u2", [128, 64], BF16)
    w["vv"] = ctx.sb("p0_vv", [128, 65], BF16)
    w["zt"] = ctx.sb("p0_zt", [32, D])
    w["zb"] = ctx.sb("p0_zb", [32, D], BF16)
    w["zr"] = ctx.sb("p0_zr", [32, 2])
    w["ztt"] = ctx.sb("p0_ztt", [128, 8 * 32], BF16)
    w["ps_a"] = ctx.ps("p0_ps_a", [128, 512])
    w["ps_b"] = ctx.ps("p0_ps_b", [128, 512])
    w["ps_tr"] = ctx.ps("p0_ps_tr", [128, 512], BF16)
    w["ps_n"] = ctx.ps("p0_ps_n", [128, 256])
    w["ps_z"] = ctx.ps("p0_ps_z", [32, 512])
    for key in DEBUG_P0_MEMSET:
        buf = w[key]
        val = 7.0 if key in ("nrm1", "inv") else 0.0
        p.op("vector", lambda e, b=buf, v=val: e.memset(b[:], v),
             reads=[], writes=[buf[:]], tag="dbg_ms_" + key)
    p.op("vector", lambda e: e.memset(w["vv"][:, 64:65], 1.0),
         reads=[], writes=[w["vv"][:, 64:65]], tag="p0_vvones")

    for it in range(NT):
        xi = xin[it % RD]
        xn = xnf[it % ND]
        ab = abuf[it % ND]
        ob = obuf[it % ND]
        bn = bnt[it % ND]
        st = stt[it % ND]
        p.dma("sync", xi[:], xin_d[it * 128:(it + 1) * 128, :], tag="p0_ldx%d" % it)
        emit_ln_stats(ctx, xi[:], D, bn, st, tag="p0_ln1_%d" % it)
        emit_ln_apply(ctx, xi[:], st, xn[:], tag="p0_ln1_%d" % it)
        if it in (15, 31):
            build_p0_window(ctx, xn, w, wq, wk, wkp, wv, wo, selh, selhT,
                            selj, selk, mask01, it)
        emit_ln_stats(ctx, xn[:], D, bn, st, tag="p0_ln2_%d" % it)
        emit_ln_apply(ctx, xn[:], st, ab[:], tag="p0_ln2_%d" % it)
        p.op("gpsimd", lambda e, a=ab: e.tensor_mul(a[:], a[:], gb[:]),
             reads=[ab[:], gb[:]], writes=[ab[:]], tag="p0_gmul%d" % it)
        p.op("gpsimd", lambda e, a=ab: e.tensor_add(a[:], a[:], bbt[:]),
             reads=[ab[:], bbt[:]], writes=[ab[:]], tag="p0_badd%d" % it)
        emit_ln_stats(ctx, ab[:], D, bn, st, tag="p0_ln3_%d" % it)
        emit_ln_apply(ctx, ab[:], st, ab[:], tag="p0_ln3_%d" % it)
        p.op("vector", lambda e, o=ob, xi=xi, a=ab: e.tensor_add(o[:], xi[:], a[:]),
             reads=[xi[:], ab[:]], writes=[ob[:]], tag="p0_add%d" % it)
        p.dma("sync", xout_d[it * 128:(it + 1) * 128, :], ob[:], tag="p0_stx%d" % it)


def build_p0_window(ctx, xn, w, wq, wk, wkp, wv, wo, selh, selhT, selj, selk,
                    mask01, it):
    p = ctx.prog
    tg = "p0w%d_" % it
    psA, psB, psT, psN, psZ = w["ps_a"], w["ps_b"], w["ps_tr"], w["ps_n"], w["ps_z"]
    p.op("scalar", lambda e: e.copy(w["xwb"][:], xn[96:128, :]),
         reads=[xn[:]], writes=[w["xwb"][:]], tag=tg + "cp")
    for k in range(8):
        transpose_tile(ctx, psT[:, k * 32:(k + 1) * 32],
                       w["xwb"][:, k * 128:(k + 1) * 128], 32, tag=tg + "trx%d" % k)
    p.op("vector", lambda e: e.tensor_copy(w["xwt"][:], psT[:, 0:256]),
         reads=[psT[:]], writes=[w["xwt"][:]], tag=tg + "trev")

    for wi_, (wmat, dst, qscale) in enumerate(
            ((wq, "qf", True), (wk, "kf", False), (wkp, "kpf", False))):
        ps = psA
        for ft in range(8):
            pairs = []
            for dt_ in range(8):
                pairs.append((wmat[:, dt_ * 1024 + ft * 128: dt_ * 1024 + ft * 128 + 128],
                              w["xwt"][:, dt_ * 32:(dt_ + 1) * 32]))
            matmul_group(p, ps[:, ft * 32:(ft + 1) * 32], pairs, tg + "proj")
        p.op("vector", lambda e: e.tensor_copy(w["qF"][:], ps[:, 0:256]),
             reads=[ps[:]], writes=[w["qF"][:]], tag=tg + "pev")
        p.op("scalar", lambda e: e.activation(w["sqs"][:], w["qF"][:], AF.Square),
             reads=[w["qF"][:]], writes=[w["sqs"][:]], tag=tg + "sq")
        for ft in range(8):
            matmul_group(p, psN[0:2, ft * 32:(ft + 1) * 32],
                         [(selh[:], w["sqs"][:, ft * 32:(ft + 1) * 32])], tg + "ssq")
        mulv = 8.0 if qscale else 1.0
        p.op("vector", lambda e: e.tensor_copy(w["ssq2"][:], psN[0:2, 0:256]),
             reads=[psN[:]], writes=[w["ssq2"][:]], tag=tg + "nev")
        for ft in range(8):
            matmul_group(p, psB[:, (ft % 4) * 32:(ft % 4) * 32 + 32],
                         [(selhT[:], w["ssq2"][:, ft * 32:ft * 32 + 32])],
                         tg + "nbc")
            sc_t = w["sqs"][:, ft * 32:(ft + 1) * 32]
            p.op("scalar", lambda e, d=sc_t, s=psB[:, (ft % 4) * 32:(ft % 4) * 32 + 32]:
                 e.activation(d, s, AF.Sqrt),
                 reads=[psB[:]], writes=[sc_t], tag=tg + "nsq")
            p.op("vector", lambda e, d=sc_t: e.tensor_scalar(d, d, 1e-7, mulv,
                                                             ALU.add, ALU.mult),
                 reads=[sc_t], writes=[sc_t], tag=tg + "neps")
            p.op("vector", lambda e, d=sc_t: e.reciprocal(d, d),
                 reads=[sc_t], writes=[sc_t], tag=tg + "ninv")
            dstt = w[dst][:, ft * 32:(ft + 1) * 32]
            p.op("vector", lambda e, d=dstt, s=sc_t,
                 q=w["qF"][:, ft * 32:(ft + 1) * 32]: e.tensor_mul(d, q, s),
                 reads=[sc_t, w["qF"][:, ft * 32:(ft + 1) * 32]], writes=[dstt],
                 tag=tg + "nsc")

    for sc in range(2):
        pairs = []
        for dt_ in range(8):
            pairs.append((w["xwt"][:, dt_ * 32:(dt_ + 1) * 32],
                          wv[:, dt_ * 1024 + sc * 512: dt_ * 1024 + sc * 512 + 512]))
        matmul_group(p, psZ[:, 0:512], pairs, tg + "vproj")
        p.op("vector", lambda e, sc=sc: e.tensor_copy(w["vt"][:, sc * 512:(sc + 1) * 512],
                                                      psZ[:, 0:512]),
             reads=[psZ[:]], writes=[w["vt"][:, sc * 512:(sc + 1) * 512]],
             tag=tg + "vev")

    for h in range(16):
        r0 = (h % 2) * 64
        c0 = (h // 2) * 32
        kf_h = w["kf"][r0:r0 + 64, c0:c0 + 32]
        kpf_h = w["kpf"][r0:r0 + 64, c0:c0 + 32]
        qf_h = w["qf"][r0:r0 + 64, c0:c0 + 32]
        mh = w["mh"][r0:r0 + 64, :]
        eng = "vector" if h % 2 == 0 else "gpsimd"
        p.op(eng, lambda e, mh=mh, a=kf_h, b=kpf_h: e.tensor_tensor(
            mh.rearrange("d (j k) -> d j k", k=32),
            a.unsqueeze(2).broadcast_to([64, 32, 32]),
            b.unsqueeze(1).broadcast_to([64, 32, 32]), ALU.mult),
            reads=[kf_h, kpf_h], writes=[mh], tag=tg + "mh%d" % h)
        matmul_group(p, psB[:, 0:64], [(selk[:], w["vt"][:, h * 64:(h + 1) * 64])],
                     tg + "u2_%d" % h)
        p.op("vector", lambda e: e.tensor_copy(w["u2"][:], psB[:, 0:64]),
             reads=[psB[:]], writes=[w["u2"][:]], tag=tg + "u2ev%d" % h)
        for jt in range(8):
            matmul_group(p, psA[:, (jt % 4) * 32:(jt % 4) * 32 + 32],
                         [(mh[:, jt * 128:(jt + 1) * 128], qf_h)], tg + "att%d_%d" % (h, jt))
            php = w["ph"][:, jt * 32:(jt + 1) * 32]
            p.op("scalar", lambda e, d=php, s=psA[:, (jt % 4) * 32:(jt % 4) * 32 + 32]:
                 e.activation(d, s, AF.Exp),
                 reads=[psA[:]], writes=[php], tag=tg + "exp%d_%d" % (h, jt))
            p.op("vector", lambda e, d=php, m=mask01[:, jt * 32:(jt + 1) * 32]:
                 e.tensor_mul(d, d, m),
                 reads=[php, mask01[:]], writes=[php], tag=tg + "msk%d_%d" % (h, jt))
            matmul_group(p, psB[:, 64:128],
                         [(selj[:, jt * 128:(jt + 1) * 128], w["vt"][:, h * 64:(h + 1) * 64])],
                         tg + "u1_%d_%d" % (h, jt))
            p.op("vector", lambda e, s=psB[:, 64:128]: e.tensor_mul(
                w["vv"][:, 0:64], s, w["u2"][:]),
                reads=[psB[:], w["u2"][:]], writes=[w["vv"][:, 0:64]],
                tag=tg + "vv%d_%d" % (h, jt))
            p.op("tensor", lambda e, l=php, s=(jt == 0), st=(jt == 7): e.matmul(
                psZ[:, 0:65], l, w["vv"][:], start=s, stop=st),
                reads=[php, w["vv"][:]], writes=[psZ[:]], tag=tg + "zmm%d_%d" % (h, jt))
        p.op("vector", lambda e: e.tensor_scalar(w["zr"][:, 0:1], psZ[:, 64:65],
                                                 1e-30, None, ALU.add),
             reads=[psZ[:]], writes=[w["zr"][:, 0:1]], tag=tg + "zn%d" % h)
        p.op("vector", lambda e: e.reciprocal(w["zr"][:, 1:2], w["zr"][:, 0:1]),
             reads=[w["zr"][:, 0:1]], writes=[w["zr"][:, 1:2]], tag=tg + "zr%d" % h)
        p.op("vector", lambda e, h=h: e.tensor_scalar(w["zt"][:, h * 64:(h + 1) * 64],
                                                      psZ[:, 0:64], w["zr"][:, 1:2],
                                                      None, ALU.mult),
             reads=[psZ[:], w["zr"][:, 1:2]], writes=[w["zt"][:, h * 64:(h + 1) * 64]],
             tag=tg + "zsc%d" % h)

    p.op("scalar", lambda e: e.copy(w["zb"][:], w["zt"][:]),
         reads=[w["zt"][:]], writes=[w["zb"][:]], tag=tg + "zcast")
    for k in range(8):
        transpose_tile(ctx, psT[:, k * 32:(k + 1) * 32],
                       w["zb"][:, k * 128:(k + 1) * 128], 32, tag=tg + "trz%d" % k)
    p.op("vector", lambda e: e.tensor_copy(w["ztt"][:], psT[:, 0:256]),
         reads=[psT[:]], writes=[w["ztt"][:]], tag=tg + "trzev")
    for sc in range(2):
        pairs = []
        for dt_ in range(8):
            pairs.append((w["ztt"][:, dt_ * 32:(dt_ + 1) * 32],
                          wo[:, dt_ * 1024 + sc * 512: dt_ * 1024 + sc * 512 + 512]))
        matmul_group(p, psZ[:, 0:512], pairs, tg + "omm%d" % sc)
        p.op("vector", lambda e, sc=sc: e.tensor_add(
            xn[96:128, sc * 512:(sc + 1) * 512], xn[96:128, sc * 512:(sc + 1) * 512],
            psZ[:, 0:512]),
            reads=[psZ[:], xn[:]], writes=[xn[:]], tag=tg + "oadd%d" % sc)
    if DEBUG_WIN and it == 15:
        for (row, buf, nr, nc_) in (
                (0, w["xwt"], 128, 256), (128, w["qf"], 128, 256),
                (256, w["kf"], 128, 256), (384, w["kpf"], 128, 256),
                (512, w["vt"], 32, 1024), (544, w["zt"], 32, 1024),
                (576, w["mh"], 128, 1024), (704, w["ph"], 128, 256),
                (832, w["vv"], 128, 65), (960, w["u2"], 128, 64),
                (1088, w["qF"], 128, 256), (1216, w["sqs"], 128, 256)):
            p.dma("gpsimd", DBG_D[row:row + nr, 0:nc_], buf[0:nr, 0:nc_],
                  tag=tg + "dbgdump")



def build_phase1(ctx, xin_d, xout_d, din, cin):
    nc, p = ctx.nc, ctx.prog
    ec = ctx.sb("p1_ec", [128, 8 * 1024], BF16)
    es = ctx.sb("p1_es", [128, 8 * 1024], BF16)
    esn = ctx.sb("p1_esn", [128, 8 * 1024], BF16)
    wr = ctx.sb("p1_wr", [128, 8 * 1024], BF16)
    wi = ctx.sb("p1_wi", [128, 8 * 1024], BF16)
    p1t = ctx.sb("p1_p1t", [128, 8 * 1024], BF16)
    p2t = ctx.sb("p1_p2t", [128, 8 * 1024], BF16)
    msb = ctx.sb("p1_m", [128, 8 * 1024], BF16)

    ld8(p, ec, cin["c_ec"], "sync", "p1_ld")
    ld8(p, es, cin["c_es"], "sync", "p1_ld")
    ld8(p, esn, cin["c_esn"], "sync", "p1_ld")
    ld8(p, wr, din["L0_ffn_w_freq_real"], "gpsimd", "p1_ld")
    ld8(p, wi, din["L0_ffn_w_freq_imag"], "gpsimd", "p1_ld")

    psA = ctx.ps("p1_psA", [128, 512], F32)
    psB = ctx.ps("p1_psB", [128, 512], F32)
    pss = [psA, psB]

    def stage(dst, lr_pairs, scale, tag):
        for ot in range(8):
            for tc in range(2):
                ps = pss[(ot * 2 + tc) % 2]
                pairs = []
                for dt_ in range(8):
                    for (lh, rh) in lr_pairs:
                        pairs.append((
                            lh[:, dt_ * 1024 + ot * 128: dt_ * 1024 + ot * 128 + 128],
                            rh[:, dt_ * 1024 + tc * 512: dt_ * 1024 + tc * 512 + 512]))
                matmul_group(p, ps[:], pairs, tag)
                dap = dst[:, ot * 1024 + tc * 512: ot * 1024 + tc * 512 + 512]
                if scale is None:
                    p.op("vector", lambda e, d=dap, s=ps: e.tensor_copy(d, s[:]),
                         reads=[ps[:]], writes=[dap], tag=tag + "_ev")
                else:
                    p.op("vector", lambda e, d=dap, s=ps: e.tensor_scalar(
                        d, s[:], scale, None, ALU.mult),
                        reads=[ps[:]], writes=[dap], tag=tag + "_ev")

    stage(p1t, [(wr, ec), (wi, es)], None, "p1_P1")
    stage(p2t, [(wi, ec), (wr, esn)], None, "p1_P2")
    stage(msb, [(p1t, ec), (p2t, esn)], 1.0 / 1024.0, "p1_M")

    RD = 4
    ND = 3
    xin = [ctx.sb("p1_xin%d" % i, [128, D]) for i in range(RD)]
    xnb = [ctx.sb("p1_xnb%d" % i, [128, D], BF16) for i in range(ND)]
    xnt = [ctx.sb("p1_xnt%d" % i, [128, D], BF16) for i in range(ND)]
    ybuf = [ctx.sb("p1_y%d" % i, [128, D]) for i in range(ND)]
    obuf = [ctx.sb("p1_o%d" % i, [128, D]) for i in range(ND)]
    bnt = [ctx.sb("p1_bn%d" % i, [128, 12]) for i in range(ND)]
    stt = [ctx.sb("p1_st%d" % i, [128, 8]) for i in range(ND)]
    pstr = [ctx.ps("p1_pstr%d" % i, [128, 1024], BF16) for i in range(2)]
    psm = [ctx.ps("p1_psm%d" % i, [128, 512]) for i in range(4)]
    biasb = ctx.sb("p1_biasb", [128, D])
    p.dma("sync", biasb[:], cin["c_l0ffn_bias_b"], tag="p1_ldbias")

    for it in range(NT):
        xi = xin[it % RD]
        xb = xnb[it % ND]
        xt = xnt[it % ND]
        yb = ybuf[it % ND]
        ob = obuf[it % ND]
        bn = bnt[it % ND]
        st = stt[it % ND]
        ptr_t = pstr[it % 2]
        p.dma("sync", xi[:], xin_d[it * 128:(it + 1) * 128, :], tag="p1_ldx%d" % it)
        emit_ln_stats(ctx, xi[:], D, bn, st, tag="p1_ln1_%d" % it)
        emit_ln_apply(ctx, xi[:], st, xb[:], tag="p1_ln1_%d" % it)
        for k in range(8):
            transpose_tile(ctx, ptr_t[:, k * 128:(k + 1) * 128],
                           xb[:, k * 128:(k + 1) * 128], 128, tag="p1_tr%d_%d" % (it, k))
        p.op("vector", lambda e, d=xt, s=ptr_t: e.tensor_copy(d[:], s[:]),
             reads=[ptr_t[:]], writes=[xt[:]], tag="p1_trev%d" % it)
        for sc in range(2):
            ps = psm[(it * 2 + sc) % 4]
            pairs = []
            for k in range(8):
                pairs.append((xt[:, k * 128:(k + 1) * 128],
                              msb[:, k * 1024 + sc * 512: k * 1024 + sc * 512 + 512]))
            matmul_group(p, ps[:], pairs, "p1_mm%d" % it)
            yap = yb[:, sc * 512:(sc + 1) * 512]
            bap = biasb[:, sc * 512:(sc + 1) * 512]
            p.op("vector", lambda e, d=yap, s=ps, bb=bap: e.tensor_add(d, s[:], bb),
                 reads=[ps[:], biasb[:]], writes=[yap], tag="p1_yev%d" % it)
        emit_ln_stats(ctx, yb[:], D, bn, st, tag="p1_ln3_%d" % it)
        emit_ln_apply(ctx, yb[:], st, yb[:], tag="p1_ln3_%d" % it)
        p.op("gpsimd", lambda e, o=ob, xi=xi, y=yb: e.tensor_add(o[:], xi[:], y[:]),
             reads=[xi[:], yb[:]], writes=[ob[:]], tag="p1_add%d" % it)
        p.dma("sync", xout_d[it * 128:(it + 1) * 128, :], ob[:], tag="p1_stx%d" % it)



def host_constants():
    c = {}
    d = np.arange(1024)
    k = (np.outer(d, d) % 1024).astype(np.float64) * (2.0 * np.pi / 1024.0)
    c["c_ec"] = np.cos(k).astype(np.float32)
    c["c_es"] = np.sin(k).astype(np.float32)
    c["c_esn"] = (-np.sin(k)).astype(np.float32)
    import ml_dtypes
    for n in ("c_ec", "c_es", "c_esn"):
        c[n] = c[n].astype(ml_dtypes.bfloat16)
    c["c_ident"] = np.eye(128, dtype=np.float32).astype(ml_dtypes.bfloat16)
    pp = np.arange(128)
    c["c_selhalf"] = (np.stack([(pp // 64) == 0, (pp // 64) == 1], 1)).astype(np.float32)
    c["c_selhalfT"] = np.ascontiguousarray(c["c_selhalf"].T)
    selj = np.zeros((32, 8, 128), np.float32)
    for jt in range(8):
        for pcol in range(128):
            selj[jt * 4 + pcol // 32, jt, pcol] = 1.0
    c["c_selj"] = selj.reshape(32, 8 * 128).astype(ml_dtypes.bfloat16)
    selk = np.zeros((32, 128), np.float32)
    for pcol in range(128):
        selk[pcol % 32, pcol] = 1.0
    c["c_selk"] = selk.astype(ml_dtypes.bfloat16)
    jj = np.arange(1024) // 32
    kk = np.arange(1024) % 32
    ss = np.arange(32)
    keep = ((jj[:, None] <= ss[None, :]) & (kk[:, None] <= ss[None, :]))
    c["c_winmask01"] = keep.astype(ml_dtypes.bfloat16)
    sidx = np.arange(128)
    triu = (sidx[:, None] <= sidx[None, :]).astype(np.float32)
    c["c_triu"] = triu
    c["c_triu_b"] = triu.astype(ml_dtypes.bfloat16)
    c["c_identf"] = np.eye(128, dtype=np.float32)
    c["c_onesrow"] = np.ones((1, 128), np.float32)
    c["c_onescol"] = np.ones((128, 1), np.float32)
    rs = np.zeros((32, 32, 128), np.float32)
    for h in range(32):
        rs[h, h, :] = 1.0
    c["c_rowsel"] = np.ascontiguousarray(rs.transpose(1, 0, 2).reshape(32, 32 * 128))
    return c


def host_bcasts(inputs):
    b = {}
    def bc(name, v):
        b[name] = np.ascontiguousarray(np.broadcast_to(
            np.asarray(v, np.float32)[None, :], (128, len(v))))
    bc("c_l0_gamma_b", inputs["L0_attn_ln_gamma"])
    bc("c_l0_beta_b", inputs["L0_attn_ln_beta"])
    bc("c_l0ffn_bias_b", inputs["L0_ffn_bias"])
    def tiled(name, v, ntile):
        v = np.asarray(v, np.float32)
        b[name] = np.ascontiguousarray(v.reshape(ntile, 128).T)
    tiled("c_normw_t", inputs["L1_attn_norm_w"], 16)
    tiled("c_rmsw_t", inputs["L1_attn_rms_w"], 8)
    tiled("c_b1_t", inputs["L1_attn_mlp_b1"], 32)
    bc("c_b2_b", inputs["L1_attn_mlp_b2"])
    cw = np.asarray(inputs["L1_attn_conv_w"], np.float32)
    b["c_convw_t"] = np.ascontiguousarray(
        cw.reshape(17, 128, 4).transpose(1, 0, 2).reshape(128, 17 * 4))
    cb = np.asarray(inputs["L1_attn_conv_b"], np.float32)
    b["c_convb_t"] = np.ascontiguousarray(cb.reshape(17, 128).T)
    nope_all = np.concatenate([np.arange(h * 64, h * 64 + 32) for h in range(16)])
    perm = np.concatenate(
        [np.concatenate([np.arange(h * 64, h * 64 + 32),
                         h * 64 + 32 + np.arange(0, 32, 2),
                         h * 64 + 32 + np.arange(1, 32, 2)]) for h in range(16)])
    b["c_wqperm"] = np.ascontiguousarray(np.asarray(inputs["L1_ffn_Wq"], np.float32)[:, perm])
    perm160 = np.concatenate([np.arange(128), 128 + np.arange(0, 32, 2),
                              128 + np.arange(1, 32, 2)])
    b["c_wkvaperm"] = np.ascontiguousarray(np.asarray(inputs["L1_ffn_Wkva"], np.float32)[:, perm160])
    b["c_woeff"] = np.ascontiguousarray(np.asarray(inputs["L1_ffn_Wo"], np.float32)[nope_all, :])
    wkvb = np.asarray(inputs["L1_ffn_Wkvb"], np.float32)
    weff = np.zeros((128, 1024), np.float32)
    for h in range(16):
        ft, sl = h // 2, h % 2
        weff[:, ft * 128 + sl * 64: ft * 128 + sl * 64 + 32] = wkvb[:, h * 32:(h + 1) * 32]
    b["c_wkvbeff"] = weff
    bc("c_wln1_b", inputs["L1_ffn_w_ln1"])
    bc("c_wln2_b", inputs["L1_ffn_w_ln2"])
    bc("c_expbias_b", inputs["L1_ffn_expert_bias"])
    import ml_dtypes
    tpos = np.arange(2048, dtype=np.float64)
    freqs = 1.0 / (1e5 ** (np.arange(0, 32, 2, dtype=np.float64) / 32.0))
    ang = np.outer(tpos, freqs)
    b["c_ropec"] = np.cos(ang).astype(np.float32).astype(ml_dtypes.bfloat16)
    b["c_ropes"] = np.sin(ang).astype(np.float32).astype(ml_dtypes.bfloat16)
    return b


def build_program(phases, io_external=True):
    nc = bass.Bass("TRN2", target_bir_lowering=False)
    prog = Prog()
    ctx = Ctx(nc, prog)

    din = {}
    WEIGHT_SHAPES = {
        "L0_attn_W_Q": (D, D), "L0_attn_W_K": (D, D), "L0_attn_W_V": (D, D),
        "L0_attn_W_Kp": (D, D), "L0_attn_W_O": (D, D),
        "L0_ffn_w_freq_real": (D, D), "L0_ffn_w_freq_imag": (D, D),
        "L1_attn_W_in": (D, 4256), "L1_attn_W_out": (2048, D),
        "L1_attn_mlp_w1": (D, 4096), "L1_attn_mlp_w2": (4096, D),
        "L1_attn_dt_bias": (1, 32), "L1_attn_A_log": (1, 32),
        "L1_attn_D_param": (1, 32),
        "L1_ffn_Wkvb": (128, 512), "L1_ffn_Wgate": (D, 8),
    }
    WEIGHT_SHAPES3 = {
        "L1_ffn_Wg": (8, D, 512), "L1_ffn_Wu": (8, D, 512),
        "L1_ffn_Wd": (8, 512, D),
        "L1_ffn_sWg": (1, D, 512), "L1_ffn_sWu": (1, D, 512),
        "L1_ffn_sWd": (1, 512, D),
    }
    for n, s in WEIGHT_SHAPES3.items():
        din[n] = nc.dram_tensor(n, list(s), F32, kind="ExternalInput").ap()
    for n, s in WEIGHT_SHAPES.items():
        din[n] = nc.dram_tensor(n, list(s), F32, kind="ExternalInput").ap()
    cin = {}
    CONST_SHAPES = {
        "c_ec": (D, D, BF16), "c_es": (D, D, BF16), "c_esn": (D, D, BF16),
        "c_ident": (128, 128, BF16),
        "c_selhalf": (128, 2, F32), "c_selhalfT": (2, 128, F32),
        "c_selj": (32, 8 * 128, BF16), "c_selk": (32, 128, BF16),
        "c_winmask01": (1024, 32, BF16),
        "c_l0_gamma_b": (128, D, F32), "c_l0_beta_b": (128, D, F32),
        "c_l0ffn_bias_b": (128, D, F32),
        "c_triu": (128, 128, F32), "c_triu_b": (128, 128, BF16),
        "c_identf": (128, 128, F32), "c_onesrow": (1, 128, F32),
        "c_onescol": (128, 1, F32), "c_rowsel": (32, 32 * 128, F32),
        "c_normw_t": (128, 16, F32), "c_rmsw_t": (128, 8, F32),
        "c_b1_t": (128, 32, F32), "c_b2_b": (128, D, F32),
        "c_convw_t": (128, 17 * 4, F32), "c_convb_t": (128, 17, F32),
        "c_wqperm": (D, D, F32), "c_wkvaperm": (D, 160, F32), "c_wkvbeff": (128, 1024, F32),
        "c_woeff": (512, D, F32), "c_wln1_b": (128, D, F32),
        "c_wln2_b": (128, D, F32), "c_expbias_b": (128, 8, F32),
        "c_ropec": (2048, 16, BF16), "c_ropes": (2048, 16, BF16),
    }
    for n, (a, b_, dt_) in CONST_SHAPES.items():
        cin[n] = nc.dram_tensor(n, [a, b_], dt_, kind="ExternalInput").ap()

    xd = {}
    if DEBUG_WIN:
        global DBG_D
        DBG_D = nc.dram_tensor("dbg", [1664, 1024], F32, kind="ExternalOutput").ap()
    xd[phases[0]] = nc.dram_tensor("x_in", [T, D], F32, kind="ExternalInput").ap()
    for i, ph in enumerate(phases):
        last = (i == len(phases) - 1)
        if last:
            xd[ph + 1] = nc.dram_tensor("x_out", [T, D], F32, kind="ExternalOutput").ap()
        else:
            xd[ph + 1] = nc.dram_tensor(
                "x_mid%d" % ph, [T, D], F32,
                kind="Internal").ap()

    ctx.epsc = ctx.sb("g_eps", [128, 2])
    prog.op("vector", lambda e: e.memset(ctx.epsc[:, 0:1], EPS),
            reads=[], writes=[ctx.epsc[:, 0:1]], tag="g_eps")
    ctx.ident = ctx.sb("g_ident", [128, 128], BF16)
    prog.dma("sync", ctx.ident[:], cin["c_ident"], tag="g_ldident")

    BUILDERS = {0: build_phase0, 1: build_phase1, 2: build_phase2,
                3: build_phase3}
    for i, ph in enumerate(phases):
        sub = contextlib.ExitStack()
        old = ctx.stack
        ctx.stack = sub
        BUILDERS[ph](ctx, xd[ph], xd[ph + 1], din, cin)
        ctx.stack = old
        sub.close()
        prog.barrier("phase%d_end" % ph)

    prog.op("sync", lambda e: e.nop(), reads=[xd[phases[-1] + 1]], writes=[], tag="final")

    with nc.Block() as block:
        prog.finalize(nc, block)
    ctx.close()
    return nc


def make_in_maps(inputs, phases, x_stage):
    consts = host_constants()
    bcasts = host_bcasts(inputs)
    in_maps = []
    for core in range(8):
        m = {}
        xs = np.ascontiguousarray(
            x_stage[2 * core:2 * core + 2].reshape(T, D).astype(np.float32))
        m["x_in"] = xs
        for n in ("L0_attn_W_Q", "L0_attn_W_K", "L0_attn_W_V", "L0_attn_W_Kp",
                  "L0_attn_W_O", "L0_ffn_w_freq_real", "L0_ffn_w_freq_imag",
                  "L1_attn_W_in", "L1_attn_W_out", "L1_attn_mlp_w1",
                  "L1_attn_mlp_w2"):
            m[n] = np.asarray(inputs[n], np.float32)
        for n in ("L1_attn_dt_bias", "L1_attn_A_log", "L1_attn_D_param"):
            m[n] = np.asarray(inputs[n], np.float32).reshape(1, 32)
        for n in ("L1_ffn_Wkvb", "L1_ffn_Wgate", "L1_ffn_Wg", "L1_ffn_Wu",
                  "L1_ffn_Wd", "L1_ffn_sWg", "L1_ffn_sWu", "L1_ffn_sWd"):
            m[n] = np.asarray(inputs[n], np.float32)
        for n, v in consts.items():
            m[n] = v
        for n, v in bcasts.items():
            m[n] = v
        in_maps.append(m)
    return in_maps


WEIGHTS_USED = {
    0: ["L0_attn_W_Q", "L0_attn_W_K", "L0_attn_W_V", "L0_attn_W_Kp", "L0_attn_W_O"],
    1: ["L0_ffn_w_freq_real", "L0_ffn_w_freq_imag"],
}



DIN = 2048
CONV = 2176
DPROJ = 4256
NH = 32


def build_phase2(ctx, xin_d, xout_d, din, cin):
    nc, p = ctx.nc, ctx.prog
    h2_d = nc.dram_tensor("h2_mid", [T, D], F32, kind="Internal").ap()
    build_phase2a(ctx, xin_d, h2_d, din, cin)
    p.barrier("p2a_end")
    build_phase2b(ctx, xin_d, h2_d, xout_d, din, cin)


def build_phase2a(ctx, xin_d, h2_d, din, cin):
    nc, p = ctx.nc, ctx.prog
    sub = contextlib.ExitStack()
    old = ctx.stack
    ctx.stack = sub

    win = ctx.sb("p2_win", [128, 8 * DPROJ], BF16)
    ld8(p, win, din["L1_attn_W_in"], "gpsimd", "p2_ldwin")
    wout = ctx.sb("p2_wout", [128, 16 * 1024], BF16)
    p.dma("gpsimd", wout[:].rearrange("p (a n) -> p a n", a=16),
          din["L1_attn_W_out"].rearrange("(a p) n -> p a n", p=128), tag="p2_ldwout")
    normw = ctx.sb("p2_normw", [128, 16])
    p.dma("sync", normw[:], cin["c_normw_t"], tag="p2_ldnw")
    for kt in range(16):
        ap_ = wout[:, kt * 1024:(kt + 1) * 1024]
        p.op("vector", lambda e, a=ap_, kt=kt: e.tensor_scalar(
            a, a, normw[:, kt:kt + 1], None, ALU.mult),
            reads=[ap_, normw[:]], writes=[ap_], tag="p2_nwfold")

    convw = ctx.sb("p2_convw", [128, 17 * 4])
    convb = ctx.sb("p2_convb", [128, 17])
    p.dma("sync", convw[:], cin["c_convw_t"], tag="p2_ldcw")
    p.dma("sync", convb[:], cin["c_convb_t"], tag="p2_ldcb")
    triu = ctx.sb("p2_triu", [128, 128])
    triub = ctx.sb("p2_triub", [128, 128], BF16)
    identf = ctx.sb("p2_identf", [128, 128])
    onesrow = ctx.sb("p2_onesrow", [1, 128])
    p.dma("sync", triu[:], cin["c_triu"], tag="p2_ldtr")
    p.dma("sync", triub[:], cin["c_triu_b"], tag="p2_ldtrb")
    p.dma("sync", identf[:], cin["c_identf"], tag="p2_ldid")
    p.dma("sync", onesrow[:], cin["c_onesrow"], tag="p2_ldon")
    onescol = ctx.sb("p2_onescol", [128, 1])
    p.dma("sync", onescol[:], cin["c_onescol"], tag="p2_ldoc")
    rowsel = ctx.sb("p2_rowsel", [32, 32 * 128])
    p.dma("sync", rowsel[:], cin["c_rowsel"], tag="p2_ldrs")
    cumend = ctx.sb("p2_cumend", [1, 64])
    row32 = ctx.sb("p2_row32", [1, 3 * 32])
    p.dma("sync", row32[:, 0:32], din["L1_attn_dt_bias"], tag="p2_lddtb")
    p.dma("sync", row32[:, 32:64], din["L1_attn_A_log"], tag="p2_ldalog")
    p.dma("sync", row32[:, 64:96], din["L1_attn_D_param"], tag="p2_lddp")
    p.op("scalar", lambda e: e.activation(row32[:, 32:64], row32[:, 32:64], AF.Exp),
         reads=[row32[:, 32:64]], writes=[row32[:, 32:64]], tag="p2_nega1")
    p.op("vector", lambda e: e.tensor_scalar(row32[:, 32:64], row32[:, 32:64],
                                             -1.0, None, ALU.mult),
         reads=[row32[:, 32:64]], writes=[row32[:, 32:64]], tag="p2_nega2")

    pA = [ctx.ps("p2_pA%d" % i, [128, 512]) for i in range(2)]
    pB = [ctx.ps("p2_pB%d" % i, [128, 1024], BF16) for i in range(2)]
    pC = [ctx.ps("p2_pC%d" % i, [128, 512]) for i in range(2)]
    pY1 = ctx.ps("p2_pY1", [128, 512])
    pY2 = ctx.ps("p2_pY2", [128, 512])

    bc32 = ctx.sb("p2_bc32", [128, 3 * 32])
    matmul_group(p, pA[0][:, 0:96], [(onesrow[:], row32[:])], "p2_bc32")
    p.op("vector", lambda e: e.tensor_copy(bc32[:], pA[0][:, 0:96]),
         reads=[pA[0][:]], writes=[bc32[:]], tag="p2_bc32ev")
    onec = ctx.sb("p2_onec", [128, 2])
    p.op("vector", lambda e: e.memset(onec[:, 0:1], 1.0), reads=[],
         writes=[onec[:, 0:1]], tag="p2_onec")
    p.op("vector", lambda e: e.memset(onec[:, 1:2], -1.0), reads=[],
         writes=[onec[:, 1:2]], tag="p2_onec2")

    xi_r = [ctx.sb("p2_xi%d" % i, [128, D]) for i in range(2)]
    xnb_r = [ctx.sb("p2_xnb%d" % i, [128, D], BF16) for i in range(1)]
    xnf32 = ctx.sb("p2_xnf32", [128, D])
    xnt_r = [ctx.sb("p2_xnt%d" % i, [128, D], BF16) for i in range(1)]
    zsil_r = [ctx.sb("p2_zsil%d" % i, [128, DIN], BF16) for i in range(1)]
    xbct_r = [ctx.sb("p2_xbct%d" % i, [128, CONV], BF16) for i in range(1)]
    dtb_r = [ctx.sb("p2_dt%d" % i, [128, 320]) for i in range(2)]
    convsrc = ctx.sb("p2_convsrc", [128, 17 * 131], BF16)
    convacc = ctx.sb("p2_convacc", [128, CONV])
    convtmp = ctx.sb("p2_convtmp", [128, CONV], BF16)
    sil_r = [ctx.sb("p2_sil%d" % i, [128, 17 * 128], BF16) for i in range(1)]
    xst_r = [ctx.sb("p2_xst%d" % i, [128, DIN], BF16) for i in range(1)]
    dtx_r = [ctx.sb("p2_dtx%d" % i, [128, DIN], BF16) for i in range(1)]
    dx2_r = [ctx.sb("p2_dx2%d" % i, [128, DIN], BF16) for i in range(1)]
    cum_r = [ctx.sb("p2_cum%d" % i, [128, 32]) for i in range(2)]
    cumT_r = [ctx.sb("p2_cumT%d" % i, [32, 128]) for i in range(2)]
    gm_r = [ctx.sb("p2_gm%d" % i, [128, 128], BF16) for i in range(2)]
    lgt_r = [ctx.sb("p2_lgt%d" % i, [128, 128], BF16) for i in range(4)]
    bt_r = [ctx.sb("p2_bt%d" % i, [128, 64], BF16) for i in range(2)]
    cf0 = ctx.sb("p2_cf0", [64, 128], BF16)
    dif32 = [ctx.sb("p2_dif32_%d" % i, [128, 128]) for i in range(2)]
    y2e = ctx.sb("p2_y2e", [128, 512], BF16)
    yt_r = [ctx.sb("p2_yt%d" % i, [128, DIN]) for i in range(1)]
    S = ctx.sb("p2_S", [64, NH * 64])
    Sb = ctx.sb("p2_Sb", [64, NH * 64], BF16)
    h2buf = [ctx.sb("p2_h2%d" % i, [128, D]) for i in range(1)]
    bnt = [ctx.sb("p2_bn%d" % i, [128, 12]) for i in range(2)]
    stt = [ctx.sb("p2_st%d" % i, [128, 8]) for i in range(2)]

    cs3 = convsrc[:].rearrange("p (c n) -> p c n", n=131)
    ca3 = convacc[:].rearrange("p (c n) -> p c n", n=128)
    ct3 = convtmp[:].rearrange("p (c n) -> p c n", n=128)
    w3 = convw[:].rearrange("p (c k) -> p c k", k=4)
    cb3 = convb[:].rearrange("p (c o) -> p c o", o=1)

    for b in range(2):
        p.op("vector", lambda e: e.memset(S[:], 0.0), reads=[], writes=[S[:]],
             tag="p2_S0_b%d" % b)
        p.op("vector", lambda e: e.memset(cs3[:, :, 0:3], 0.0),
             reads=[], writes=[convsrc[:]], tag="p2_tl0_b%d" % b)
        for c in range(16):
            it = b * 16 + c
            r = it % 2
            tg = "p2c%d_" % it
            xi = xi_r[it % 2]
            xnb, xnt = xnb_r[0], xnt_r[0]
            zsil, xbct, dtb = zsil_r[0], xbct_r[0], dtb_r[r]
            sil, xst, dtx, dx2 = sil_r[0], xst_r[0], dtx_r[0], dx2_r[0]
            cum, cumT, gm, bt, yt = cum_r[r], cumT_r[r], gm_r[r], bt_r[r], yt_r[0]
            h2t = h2buf[0]
            bn, st = bnt[r], stt[r]

            p.dma("sync", xi[:], xin_d[it * 128:(it + 1) * 128, :], tag=tg + "ldx")
            emit_ln_stats(ctx, xi[:], D, bn, st, tag=tg + "ln")
            emit_ln_apply(ctx, xi[:], st, xnf32[:], tag=tg + "ln")
            p.op("vector", lambda e, xb=xnb: e.tensor_copy(xb[:], xnf32[:]),
                 reads=[xnf32[:]], writes=[xnb[:]], tag=tg + "lncast")
            for k in range(8):
                transpose_tile(ctx, pB[0][:, k * 128:(k + 1) * 128],
                               xnb[:, k * 128:(k + 1) * 128], 128, tag=tg + "trx")
            p.op("vector", lambda e, d=xnt: e.tensor_copy(d[:], pB[0][:]),
                 reads=[pB[0][:]], writes=[xnt[:]], tag=tg + "trxev")
            for g in range(9):
                lo = g * 512
                hi = min(DPROJ, lo + 512)
                ps = pA[g % 2]
                pairs = [(xnt[:, k * 128:(k + 1) * 128],
                          win[:, k * DPROJ + lo: k * DPROJ + hi]) for k in range(8)]
                matmul_group(p, ps[:, 0:hi - lo], pairs, tg + "zx")
                if hi <= DIN:
                    p.op("scalar", lambda e, d=zsil[:, lo:hi], s=ps: e.activation(
                        d, s[:, 0:512], AF.Silu),
                        reads=[ps[:]], writes=[zsil[:, lo:hi]], tag=tg + "zsil")
                elif lo >= DIN and hi <= DIN + CONV:
                    d = xbct[:, lo - DIN:hi - DIN]
                    p.op("vector", lambda e, d=d, s=ps, n=hi - lo: e.tensor_copy(
                        d, s[:, 0:n]),
                        reads=[ps[:]], writes=[d], tag=tg + "xbev")
                else:
                    d = xbct[:, 4096 - DIN:4224 - DIN]
                    p.op("vector", lambda e, d=d, s=ps: e.tensor_copy(d, s[:, 0:128]),
                         reads=[ps[:]], writes=[d], tag=tg + "xbev")
                    p.op("vector", lambda e, s=ps, dtb=dtb: e.tensor_copy(
                        dtb[:, 0:32], s[:, 128:160]),
                        reads=[ps[:]], writes=[dtb[:, 0:32]], tag=tg + "dtev")
            for gi, (ct0, nt_) in enumerate(((0, 8), (8, 8), (16, 1))):
                pb = pB[gi % 2]
                for j in range(nt_):
                    ct = ct0 + j
                    transpose_tile(ctx, pb[:, j * 128:(j + 1) * 128],
                                   xbct[:, ct * 128:(ct + 1) * 128], 128,
                                   tag=tg + "trb")
                dst = cs3[:, ct0:ct0 + nt_, 3:131]
                p.op("vector", lambda e, d=dst, s=pb, n=nt_: e.tensor_copy(
                    d, s[:].rearrange("p (c n) -> p c n", n=128)[:, 0:n, :]),
                    reads=[pb[:]], writes=[convsrc[:]], tag=tg + "trbev")
            p.op("gpsimd", lambda e: e.tensor_tensor(
                ca3, cs3[:, :, 0:128],
                w3[:, :, 0:1].broadcast_to([128, 17, 128]), ALU.mult),
                reads=[convsrc[:], convw[:]], writes=[convacc[:]], tag=tg + "cv0")
            p.op("gpsimd", lambda e: e.tensor_tensor(
                ca3, ca3, cb3.broadcast_to([128, 17, 128]), ALU.add),
                reads=[convacc[:], convb[:]], writes=[convacc[:]], tag=tg + "cvb")
            for k in range(1, 4):
                p.op("gpsimd", lambda e, k=k: e.tensor_tensor(
                    ct3, cs3[:, :, k:k + 128],
                    w3[:, :, k:k + 1].broadcast_to([128, 17, 128]), ALU.mult),
                    reads=[convsrc[:], convw[:]], writes=[convtmp[:]], tag=tg + "cvm%d" % k)
                p.op("gpsimd", lambda e: e.tensor_tensor(ca3, ca3, ct3, ALU.add),
                     reads=[convacc[:], convtmp[:]], writes=[convacc[:]],
                     tag=tg + "cva%d" % k)
            p.op("vector", lambda e: e.tensor_copy(cs3[:, :, 0:3], cs3[:, :, 128:131]),
                 reads=[convsrc[:]], writes=[convsrc[:]], tag=tg + "tail")
            p.op("scalar", lambda e: e.activation(sil[:], convacc[:], AF.Silu),
                 reads=[convacc[:]], writes=[sil[:]], tag=tg + "csil")
            p.op("vector", lambda e, dtb=dtb: e.tensor_add(dtb[:, 32:64], dtb[:, 0:32],
                                                  bc32[:, 0:32]),
                 reads=[dtb[:, 0:32], bc32[:]], writes=[dtb[:, 32:64]], tag=tg + "dt0")
            p.op("scalar", lambda e, dtb=dtb: e.activation(dtb[:, 64:96], dtb[:, 32:64], AF.Abs),
                 reads=[dtb[:, 32:64]], writes=[dtb[:, 64:96]], tag=tg + "dtabs")
            p.op("scalar", lambda e, dtb=dtb: e.activation(dtb[:, 64:96], dtb[:, 64:96], AF.Exp,
                                                  scale=onec[:, 1:2]),
                 reads=[dtb[:, 64:96], onec[:]], writes=[dtb[:, 64:96]], tag=tg + "dtexp")
            p.op("scalar", lambda e, dtb=dtb: e.activation(dtb[:, 64:96], dtb[:, 64:96], AF.Ln,
                                                  bias=onec[:, 0:1]),
                 reads=[dtb[:, 64:96], onec[:]], writes=[dtb[:, 64:96]], tag=tg + "dtln")
            p.op("vector", lambda e, dtb=dtb: e.tensor_scalar(dtb[:, 96:128], dtb[:, 32:64],
                                                     0.0, None, ALU.max),
                 reads=[dtb[:, 32:64]], writes=[dtb[:, 96:128]], tag=tg + "dtrelu")
            p.op("vector", lambda e, dtb=dtb: e.tensor_add(dtb[:, 96:128], dtb[:, 96:128],
                                                  dtb[:, 64:96]),
                 reads=[dtb[:, 96:128], dtb[:, 64:96]], writes=[dtb[:, 96:128]],
                 tag=tg + "dtsum")
            p.op("vector", lambda e, dtb=dtb: e.tensor_mul(dtb[:, 128:160], dtb[:, 96:128],
                                                  bc32[:, 32:64]),
                 reads=[dtb[:, 96:128], bc32[:]], writes=[dtb[:, 128:160]],
                 tag=tg + "dtA")
            matmul_group(p, pA[0][:, 0:32], [(triu[:], dtb[:, 128:160])], tg + "cum")
            p.op("vector", lambda e, cum=cum: e.tensor_copy(cum[:], pA[0][:, 0:32]),
                 reads=[pA[0][:]], writes=[cum[:]], tag=tg + "cumev")
            p.op("vector", lambda e, dtb=dtb, cum=cum: e.tensor_scalar(dtb[:, 160:192], cum[:], -1.0,
                                                     None, ALU.mult),
                 reads=[cum[:]], writes=[dtb[:, 160:192]], tag=tg + "ncum")
            p.op("scalar", lambda e, dtb=dtb, cum=cum: e.activation(dtb[:, 224:256], cum[:], AF.Exp),
                 reads=[cum[:]], writes=[dtb[:, 224:256]], tag=tg + "ecum")
            p.op("tensor", lambda e, cum=cum: e.transpose(pC[0][0:32, 0:128], cum[:], identf[:]),
                 reads=[cum[:], identf[:]], writes=[pC[0][:]], tag=tg + "cumT")
            p.op("vector", lambda e, cumT=cumT: e.tensor_copy(cumT[:], pC[0][0:32, 0:128]),
                 reads=[pC[0][:]], writes=[cumT[:]], tag=tg + "cumTev")
            matmul_group(p, pC[1][0:1, 0:32], [(onescol[:], dtb[:, 128:160])],
                         tg + "cerow")
            p.op("vector", lambda e: e.tensor_copy(cumend[:, 0:32], pC[1][0:1, 0:32]),
                 reads=[pC[1][:]], writes=[cumend[:, 0:32]], tag=tg + "cerowev")
            p.op("scalar", lambda e: e.activation(cumend[:, 32:64], cumend[:, 0:32],
                                                  AF.Exp),
                 reads=[cumend[:, 0:32]], writes=[cumend[:, 32:64]], tag=tg + "aerow")
            matmul_group(p, pC[1][:, 0:32], [(onesrow[:], cumend[:, 0:32])], tg + "ceb")
            p.op("vector", lambda e, dtb=dtb, cum=cum: e.tensor_sub(dtb[:, 256:288], pC[1][:, 0:32], cum[:]),
                 reads=[pC[1][:], cum[:]], writes=[dtb[:, 256:288]], tag=tg + "dsub")
            p.op("scalar", lambda e, dtb=dtb: e.activation(dtb[:, 256:288], dtb[:, 256:288],
                                                  AF.Exp),
                 reads=[dtb[:, 256:288]], writes=[dtb[:, 256:288]], tag=tg + "dexp")
            matmul_group(p, pC[1][:, 64:96], [(onesrow[:], cumend[:, 32:64])],
                         tg + "aeb")
            p.op("vector", lambda e, dtb=dtb: e.tensor_copy(dtb[:, 192:224], pC[1][:, 64:96]),
                 reads=[pC[1][:]], writes=[dtb[:, 192:224]], tag=tg + "aebev")
            for half in range(2):
                pb = pB[half]
                for j in range(8):
                    ct = half * 8 + j
                    transpose_tile(ctx, pb[:, j * 128:(j + 1) * 128],
                                   sil[:, ct * 128:(ct + 1) * 128], 128,
                                   tag=tg + "trs")
                p.op("vector", lambda e, d=xst[:, half * 1024:(half + 1) * 1024],
                     s=pb: e.tensor_copy(d, s[:]),
                     reads=[pb[:]], writes=[xst[:, half * 1024:(half + 1) * 1024]],
                     tag=tg + "trsev")
            xst3 = xst[:].rearrange("p (h d) -> p h d", d=64)
            dtx3 = dtx[:].rearrange("p (h d) -> p h d", d=64)
            dx23 = dx2[:].rearrange("p (h d) -> p h d", d=64)
            p.op("gpsimd", lambda e, dtb=dtb, dtx3=dtx3, xst3=xst3: e.tensor_tensor(
                dtx3, xst3, dtb[:, 96:128].unsqueeze(2).broadcast_to([128, 32, 64]),
                ALU.mult),
                reads=[xst[:], dtb[:, 96:128]], writes=[dtx[:]], tag=tg + "dtx")
            p.op("vector", lambda e, dtb=dtb, dx23=dx23, dtx3=dtx3: e.tensor_tensor(
                dx23, dtx3, dtb[:, 256:288].unsqueeze(2).broadcast_to([128, 32, 64]),
                ALU.mult),
                reads=[dtx[:], dtb[:, 256:288]], writes=[dx2[:]], tag=tg + "dx2")
            Bf = sil[0:64, 16 * 128:17 * 128]
            Cfsrc = sil[64:128, 16 * 128:17 * 128]
            Cf = cf0[:]
            p.op("vector", lambda e: e.tensor_copy(cf0[:], Cfsrc),
                 reads=[sil[:]], writes=[cf0[:]], tag=tg + "cf0")
            matmul_group(p, pA[1][:, 0:128], [(Bf, Cf)], tg + "G")
            p.op("vector", lambda e, g=gm: e.tensor_mul(g[:], pA[1][:, 0:128], triub[:]),
                 reads=[pA[1][:], triub[:]], writes=[gm[:]], tag=tg + "Gm")
            transpose_tile(ctx, pB[0][:, 0:64], Bf, 64, tag=tg + "trB")
            p.op("vector", lambda e, d=bt: e.tensor_copy(d[:], pB[0][:, 0:64]),
                 reads=[pB[0][:]], writes=[bt[:]], tag=tg + "trBev")
            for hg in range(4):
                pyt = [pY1, pY2][hg % 2]
                for hh in range(8):
                    h = hg * 8 + hh
                    lg = lgt_r[h % 4]
                    matmul_group(p, pC[h % 2][:, 0:128],
                                 [(rowsel[:, h * 128:(h + 1) * 128], cumT[:])],
                                 tg + "ctb%d" % h)
                    df = dif32[h % 2]
                    p.op("vector", lambda e, h=h, pc=pC[h % 2], dtb=dtb, df=df: e.tensor_scalar(
                        df[:], pc[:, 0:128], dtb[:, 160 + h:160 + h + 1], 0.0,
                        ALU.add, ALU.min),
                        reads=[pC[h % 2][:], dtb[:, 160:192]], writes=[df[:]],
                        tag=tg + "ediff%d" % h)
                    p.op("scalar", lambda e, o=lg, df=df: e.activation(o[:], df[:], AF.Exp),
                         reads=[df[:]], writes=[lg[:]], tag=tg + "edec%d" % h)
                    eng = "vector" if h % 2 == 0 else "gpsimd"
                    p.op(eng, lambda e, o=lg, g=gm: e.tensor_mul(o[:], o[:], g[:]),
                         reads=[lg[:], gm[:]], writes=[lg[:]], tag=tg + "lgt%d" % h)
                    p.op("tensor", lambda e, o=pyt[:, hh * 64:(hh + 1) * 64], l=lg,
                         rr=dtx[:, h * 64:(h + 1) * 64]: e.matmul(o, l[:], rr,
                                                                 start=True, stop=True),
                         reads=[lg[:], dtx[:]], writes=[pyt[:]], tag=tg + "y1_%d" % h)
                py2 = [pY2, pY1][hg % 2]
                if c > 0:
                    for hh in range(8):
                        h = hg * 8 + hh
                        p.op("tensor", lambda e, o=py2[:, hh * 64:(hh + 1) * 64],
                             l=Cf, rr=Sb[:, h * 64:(h + 1) * 64]: e.matmul(
                                 o, l, rr, start=True, stop=True),
                             reads=[Cf, Sb[:]], writes=[py2[:]], tag=tg + "y2_%d" % h)
                    p.op("scalar", lambda e, s=py2: e.copy(y2e[:], s[:]),
                         reads=[py2[:]], writes=[y2e[:]], tag=tg + "y2ev%d" % hg)
                    y2e3 = y2e[:].rearrange("p (h d) -> p h d", d=64)
                    p.op("vector", lambda e, hg=hg, dtb=dtb, y2e3=y2e3: e.tensor_tensor(
                        y2e3, y2e3,
                        dtb[:, 224 + hg * 8:224 + hg * 8 + 8].unsqueeze(2)
                        .broadcast_to([128, 8, 64]), ALU.mult),
                        reads=[y2e[:], dtb[:, 224:256]], writes=[y2e[:]],
                        tag=tg + "y2sc%d" % hg)
                    p.op("vector", lambda e, s=pyt, d=yt[:, hg * 512:(hg + 1) * 512]:
                         e.tensor_add(d, s[:], y2e[:]),
                         reads=[pyt[:], y2e[:]], writes=[yt[:, hg * 512:(hg + 1) * 512]],
                         tag=tg + "ymrg%d" % hg)
                else:
                    p.op("vector", lambda e, s=pyt, d=yt[:, hg * 512:(hg + 1) * 512]:
                         e.tensor_copy(d, s[:]),
                         reads=[pyt[:]], writes=[yt[:, hg * 512:(hg + 1) * 512]],
                         tag=tg + "ycp%d" % hg)
                ptS = pA[(hg + 1) % 2]
                for hh in range(8):
                    h = hg * 8 + hh
                    p.op("tensor", lambda e, o=ptS[0:64, hh * 64:(hh + 1) * 64],
                         l=bt, rr=dx2[:, h * 64:(h + 1) * 64]: e.matmul(
                             o, l[:], rr, start=True, stop=True),
                         reads=[bt[:], dx2[:]], writes=[ptS[:]], tag=tg + "tm%d" % h)
                S3 = S[:].rearrange("p (h d) -> p h d", d=64)
                p.op("vector", lambda e, hg=hg, dtb=dtb, S3=S3: e.tensor_tensor(
                    S3[:, hg * 8:(hg + 1) * 8, :], S3[:, hg * 8:(hg + 1) * 8, :],
                    dtb[0:64, 192 + hg * 8:192 + hg * 8 + 8].unsqueeze(2)
                    .broadcast_to([64, 8, 64]), ALU.mult),
                    reads=[S[:], dtb[:, 192:224]], writes=[S[:]], tag=tg + "sdk%d" % hg)
                p.op("vector", lambda e, s=ptS, hg=hg: e.tensor_add(
                    S[:, hg * 512:(hg + 1) * 512], S[:, hg * 512:(hg + 1) * 512],
                    s[0:64, 0:512]),
                    reads=[ptS[:], S[:]], writes=[S[:]], tag=tg + "sup%d" % hg)
            p.op("scalar", lambda e: e.copy(Sb[:], S[:]),
                 reads=[S[:]], writes=[Sb[:]], tag=tg + "sbcp")
            yt3 = yt[:].rearrange("p (h d) -> p h d", d=64)
            p.op("gpsimd", lambda e: e.tensor_tensor(
                convtmp[:, 0:DIN].rearrange("p (h d) -> p h d", d=64),
                xst3, bc32[:, 64:96].unsqueeze(2).broadcast_to([128, 32, 64]),
                ALU.mult),
                reads=[xst[:], bc32[:]], writes=[convtmp[:]], tag=tg + "dxs")
            p.op("gpsimd", lambda e: e.tensor_add(yt[:], yt[:], convtmp[:, 0:DIN]),
                 reads=[yt[:], convtmp[:]], writes=[yt[:]], tag=tg + "dadd")
            p.op("gpsimd", lambda e: e.tensor_mul(yt[:], yt[:], zsil[:]),
                 reads=[yt[:], zsil[:]], writes=[yt[:]], tag=tg + "gate")
            p.op("scalar", lambda e, st=st, yt=yt: e.activation(convtmp[:, 0:DIN], yt[:], AF.Square,
                                                  accum_out=st[:, 4:5]),
                 reads=[yt[:]], writes=[convtmp[:, 0:DIN], st[:, 4:5]], tag=tg + "rsq")
            p.op("vector", lambda e, st=st: e.tensor_scalar(st[:, 4:5], st[:, 4:5],
                                                     1.0 / DIN, EPS, ALU.mult, ALU.add),
                 reads=[st[:, 4:5]], writes=[st[:, 4:5]], tag=tg + "rms1")
            p.op("scalar", lambda e, st=st: e.activation(st[:, 5:6], st[:, 4:5], AF.Sqrt),
                 reads=[st[:, 4:5]], writes=[st[:, 5:6]], tag=tg + "rms2")
            p.op("vector", lambda e, st=st: e.reciprocal(st[:, 4:5], st[:, 5:6]),
                 reads=[st[:, 5:6]], writes=[st[:, 4:5]], tag=tg + "rms3")
            p.op("vector", lambda e, st=st, yt=yt, dtx=dtx: e.tensor_scalar(dtx[:], yt[:], st[:, 4:5],
                                                     None, ALU.mult),
                 reads=[yt[:], st[:, 4:5]], writes=[dtx[:]], tag=tg + "ynorm")
            for half in range(2):
                pb = pB[half]
                for j in range(8):
                    kt = half * 8 + j
                    transpose_tile(ctx, pb[:, j * 128:(j + 1) * 128],
                                   dtx[:, kt * 128:(kt + 1) * 128], 128,
                                   tag=tg + "try")
                p.op("vector", lambda e, d=dx2[:, half * 1024:(half + 1) * 1024],
                     s=pb: e.tensor_copy(d, s[:]),
                     reads=[pb[:]], writes=[dx2[:, half * 1024:(half + 1) * 1024]],
                     tag=tg + "tryev")
            for sc in range(2):
                ps = pA[sc]
                pairs = [(dx2[:, kt * 128:(kt + 1) * 128],
                          wout[:, kt * 1024 + sc * 512: kt * 1024 + sc * 512 + 512])
                         for kt in range(16)]
                matmul_group(p, ps[:], pairs, tg + "mam")
                p.op("vector", lambda e, s=ps, d=h2t[:, sc * 512:(sc + 1) * 512],
                     x=xnf32[:, sc * 512:(sc + 1) * 512]: e.tensor_add(d, s[:], x),
                     reads=[ps[:], xnf32[:]], writes=[h2t[:, sc * 512:(sc + 1) * 512]],
                     tag=tg + "h2")
            p.dma("sync", h2_d[it * 128:(it + 1) * 128, :], h2t[:], tag=tg + "sth2")

    ctx.stack = old
    sub.close()


def build_phase2b(ctx, xin_d, h2_d, xout_d, din, cin):
    nc, p = ctx.nc, ctx.prog
    sub = contextlib.ExitStack()
    old = ctx.stack
    ctx.stack = sub

    w1 = ctx.sb("p2b_w1", [128, 8 * 4096], BF16)
    ld8(p, w1, din["L1_attn_mlp_w1"], "gpsimd", "p2b_ldw1")
    w2 = ctx.sb("p2b_w2", [128, 32 * 1024], BF16)
    p.dma("gpsimd", w2[:].rearrange("p (a n) -> p a n", a=32),
          din["L1_attn_mlp_w2"].rearrange("(a p) n -> p a n", p=128), tag="p2b_ldw2")
    rmsw = ctx.sb("p2b_rmsw", [128, 8])
    p.dma("sync", rmsw[:], cin["c_rmsw_t"], tag="p2b_ldrw")
    for kt in range(8):
        ap_ = w1[:, kt * 4096:(kt + 1) * 4096]
        p.op("vector", lambda e, a=ap_, kt=kt: e.tensor_scalar(
            a, a, rmsw[:, kt:kt + 1], None, ALU.mult),
            reads=[ap_, rmsw[:]], writes=[ap_], tag="p2b_rwfold")
    b1t = ctx.sb("p2b_b1", [128, 32])
    p.dma("sync", b1t[:], cin["c_b1_t"], tag="p2b_ldb1")
    b2b = ctx.sb("p2b_b2", [128, D])
    p.dma("sync", b2b[:], cin["c_b2_b"], tag="p2b_ldb2")

    pM = [ctx.ps("p2b_pM%d" % i, [128, 512]) for i in range(4)]
    pB = [ctx.ps("p2b_pB%d" % i, [128, 1024], BF16) for i in range(2)]

    h2 = [ctx.sb("p2b_h2_%d" % i, [128, D]) for i in range(4)]
    x2 = [ctx.sb("p2b_x2_%d" % i, [128, D]) for i in range(2)]
    hnb = [ctx.sb("p2b_hnb%d" % i, [128, D], BF16) for i in range(2)]
    hnT = [ctx.sb("p2b_hnT%d" % i, [128, 8 * 512], BF16) for i in range(1)]
    hid = [ctx.sb("p2b_hid%d" % i, [128, 32 * 512], BF16) for i in range(1)]
    ob = [ctx.sb("p2b_ob%d" % i, [128, D]) for i in range(1)]
    bnt = [ctx.sb("p2b_bn%d" % i, [128, 12]) for i in range(2)]
    stt = [ctx.sb("p2b_st%d" % i, [128, 8]) for i in range(4)]

    for u in range(8):
        ur = u % 2
        tg = "p2u%d_" % u
        ht = hnT[0]
        hd = hid[0]
        for tt in range(4):
            it = u * 4 + tt
            h2t = h2[tt]
            st = stt[tt]
            bn = bnt[ur]
            p.dma("sync", h2t[:], h2_d[it * 128:(it + 1) * 128, :], tag=tg + "ldh%d" % tt)
            p.op("scalar", lambda e, h=h2t, st=st, hb=hnb[ur]: e.activation(
                hb[:], h[:], AF.Square, accum_out=st[:, 4:5]),
                reads=[h2t[:]], writes=[hnb[ur][:], st[:, 4:5]], tag=tg + "rsq%d" % tt)
            p.op("vector", lambda e, st=st: e.tensor_scalar(
                st[:, 4:5], st[:, 4:5], 1.0 / D, EPS, ALU.mult, ALU.add),
                reads=[st[:, 4:5]], writes=[st[:, 4:5]], tag=tg + "r1%d" % tt)
            p.op("scalar", lambda e, st=st: e.activation(st[:, 5:6], st[:, 4:5], AF.Sqrt),
                 reads=[st[:, 4:5]], writes=[st[:, 5:6]], tag=tg + "r2%d" % tt)
            p.op("vector", lambda e, st=st: e.reciprocal(st[:, 4:5], st[:, 5:6]),
                 reads=[st[:, 5:6]], writes=[st[:, 4:5]], tag=tg + "r3%d" % tt)
            p.op("vector", lambda e, h=h2t, st=st, hb=hnb[ur]: e.tensor_scalar(
                hb[:], h[:], st[:, 4:5], None, ALU.mult),
                reads=[h2t[:], st[:, 4:5]], writes=[hnb[ur][:]], tag=tg + "hn%d" % tt)
            pb = pB[tt % 2]
            for k in range(8):
                transpose_tile(ctx, pb[:, k * 128:(k + 1) * 128],
                               hnb[ur][:, k * 128:(k + 1) * 128], 128, tag=tg + "trh")
            dst = ht[:].rearrange("p (k t) -> p k t", t=512)[:, :, tt * 128:(tt + 1) * 128]
            p.op("vector", lambda e, d=dst, s=pb: e.tensor_copy(
                d, s[:].rearrange("p (k t) -> p k t", t=128)),
                reads=[pb[:]], writes=[ht[:]], tag=tg + "trhev%d" % tt)
        for ft in range(32):
            ps = pM[ft % 2]
            pairs = [(w1[:, k * 4096 + ft * 128: k * 4096 + ft * 128 + 128],
                      ht[:, k * 512:(k + 1) * 512]) for k in range(8)]
            matmul_group(p, ps[:], pairs, tg + "w1")
            p.op("scalar", lambda e, d=hd[:, ft * 512:(ft + 1) * 512], s=ps, ft=ft:
                 e.activation(d, s[:], AF.Silu, bias=b1t[:, ft:ft + 1]),
                 reads=[ps[:], b1t[:]], writes=[hd[:, ft * 512:(ft + 1) * 512]],
                 tag=tg + "silu%d" % ft)
        for tt in range(4):
            it = u * 4 + tt
            h2t = h2[tt]
            ot = ob[0]
            bn = bnt[tt % 2]
            st = stt[tt]
            x2t = x2[tt % 2]
            p.dma("sync", x2t[:], xin_d[it * 128:(it + 1) * 128, :], tag=tg + "ldx%d" % tt)
            for sc in range(2):
                ps = pM[2 + sc]
                pairs = [(hd[:].rearrange("p (f t) -> p f t", t=512)
                          [:, k, tt * 128:(tt + 1) * 128],
                          w2[:, k * 1024 + sc * 512: k * 1024 + sc * 512 + 512])
                         for k in range(32)]
                matmul_group(p, ps[:], pairs, tg + "w2_%d_%d" % (tt, sc))
                p.op("vector", lambda e, s=ps, d=ot[:, sc * 512:(sc + 1) * 512],
                     bb=b2b[:, sc * 512:(sc + 1) * 512]: e.tensor_add(d, s[:], bb),
                     reads=[ps[:], b2b[:]], writes=[ot[:, sc * 512:(sc + 1) * 512]],
                     tag=tg + "oev%d_%d" % (tt, sc))
            p.op("gpsimd", lambda e, o=ot, h=h2t: e.tensor_add(o[:], o[:], h[:]),
                 reads=[ot[:], h2t[:]], writes=[ot[:]], tag=tg + "hadd%d" % tt)
            emit_ln_stats(ctx, ot[:], D, bn, st, tag=tg + "ln%d" % tt)
            emit_ln_apply(ctx, ot[:], st, ot[:], tag=tg + "ln%d" % tt)
            p.op("vector", lambda e, o=ot, x=x2t: e.tensor_add(o[:], o[:], x[:]),
                 reads=[ot[:], x2t[:]], writes=[ot[:]], tag=tg + "xadd%d" % tt)
            p.dma("sync", xout_d[it * 128:(it + 1) * 128, :], ot[:], tag=tg + "stx%d" % tt)

    ctx.stack = old
    sub.close()



H = 16
HD = 64
NOPE = 32
ROPE = 32
R = 128
E = 8
I = 512


def build_phase3(ctx, xin_d, xout_d, din, cin):
    nc, p = ctx.nc, ctx.prog
    xb_d = nc.dram_tensor("xb_mid", [T, D], F32, kind="Internal").ap()
    t_d = nc.dram_tensor("t_mid", [T, D], F32, kind="Internal").ap()
    build_phase3a(ctx, xin_d, xb_d, t_d, din, cin)
    p.barrier("p3a_end")
    build_phase3b(ctx, xin_d, t_d, xout_d, din, cin)


def build_phase3a(ctx, xin_d, xb_d, t_d, din, cin):
    nc, p = ctx.nc, ctx.prog
    sub = contextlib.ExitStack()
    old = ctx.stack
    ctx.stack = sub

    wq = ctx.sb("p3_wq", [128, 8 * 1024], BF16)
    ld8(p, wq, cin["c_wqperm"], "gpsimd", "p3_ldwq")
    wkva = ctx.sb("p3_wkva", [128, 8 * 160], BF16)
    p.dma("gpsimd", wkva[:].rearrange("p (a n) -> p a n", a=8),
          cin["c_wkvaperm"].rearrange("(a p) n -> p a n", p=128), tag="p3_ldkva")
    wkvb = ctx.sb("p3_wkvb", [128, 8 * 128], BF16)
    p.dma("gpsimd", wkvb[:].rearrange("p (a n) -> p a n", a=8),
          cin["c_wkvbeff"].rearrange("p (a n) -> p a n", a=8), tag="p3_ldkvb")
    woe = ctx.sb("p3_woe", [128, 4 * 1024], BF16)
    p.dma("gpsimd", woe[:].rearrange("p (a n) -> p a n", a=4),
          cin["c_woeff"].rearrange("(a p) n -> p a n", p=128), tag="p3_ldwoe")
    wln1b = ctx.sb("p3_wln1b", [128, D])
    p.dma("sync", wln1b[:], cin["c_wln1_b"], tag="p3_ldln1")
    ropec = ctx.sb("p3_ropec", [128, 16 * 16], BF16)
    ropes = ctx.sb("p3_ropes", [128, 16 * 16], BF16)
    p.dma("sync", ropec[:].rearrange("p (a n) -> p a n", a=16),
          cin["c_ropec"].rearrange("(a p) n -> p a n", p=128), tag="p3_ldrc")
    p.dma("sync", ropes[:].rearrange("p (a n) -> p a n", a=16),
          cin["c_ropes"].rearrange("(a p) n -> p a n", p=128), tag="p3_ldrs")
    cpool = ctx.sb("p3_cpool", [128, 4])
    p.op("vector", lambda e: e.memset(cpool[:, 0:1], 0.125), reads=[],
         writes=[cpool[:, 0:1]], tag="p3_c0")
    p.op("vector", lambda e: e.memset(cpool[:, 1:2], -20.0), reads=[],
         writes=[cpool[:, 1:2]], tag="p3_c1")

    qf = ctx.sb("p3_qf", [128, 8 * 2048], BF16)
    kvf = ctx.sb("p3_kvf", [128, 8 * 2048], BF16)
    ckvf = ctx.sb("p3_ckvf", [128, 2048], BF16)
    krr = ctx.sb("p3_krr", [32, 2048], BF16)
    vaug = ctx.sb("p3_vaug", [128, 16 * 528], BF16)
    ytok = ctx.sb("p3_ytok", [128, 16 * 512], BF16)

    xi_r = [ctx.sb("p3_xi%d" % i, [128, D]) for i in range(2)]
    xbf = ctx.sb("p3_xbf", [128, D])
    xab = ctx.sb("p3_xab", [128, D], BF16)
    xaT = ctx.sb("p3_xaT", [128, D], BF16)
    qtok_r = [ctx.sb("p3_qtok%d" % i, [128, D], BF16) for i in range(2)]
    cktok = ctx.sb("p3_cktok", [128, 192], BF16)
    Pb = [ctx.sb("p3_P%d" % i, [128, 512], BF16) for i in range(2)]
    yaug = [ctx.sb("p3_yaug%d" % i, [33, 512], BF16) for i in range(2)]
    zr = ctx.sb("p3_zr", [128, 8])
    bnt = ctx.sb("p3_bn", [128, 12])
    stt = ctx.sb("p3_st", [128, 8])
    obuf = ctx.sb("p3_ob", [128, D])
    ropescr = ctx.sb("p3_ropescr", [128, 1024], BF16)

    pq = [ctx.ps("p3_pq%d" % i, [128, 512]) for i in range(2)]
    ptr = [ctx.ps("p3_ptr%d" % i, [128, 1024], BF16) for i in range(2)]
    pS = [ctx.ps("p3_pS%d" % i, [128, 512]) for i in range(2)]
    pZ0 = ctx.ps("p3_pZ", [33, 512])
    pZ = None
    pO = ctx.ps("p3_pO", [128, 512])

    pZ = [pZ0, pq[1]]
    p.op("vector", lambda e: e.memset(
        vaug[:].rearrange("p (k h c) -> p k h c", h=16, c=33)[:, :, :, 32:33], 1.0),
        reads=[], writes=[vaug[:]], tag="p3_vones")

    for b in range(2):
        tb = "p3b%d_" % b
        toff = b * 2048
        for it in range(16):
            tg = tb + "t%d_" % it
            xi = xi_r[it % 2]
            row = (toff + it * 128)
            p.dma("sync", xi[:], xin_d[row:row + 128, :], tag=tg + "ldx")
            emit_ln_stats(ctx, xi[:], D, bnt, stt, tag=tg + "ln")
            emit_ln_apply(ctx, xi[:], stt, xbf[:], tag=tg + "ln")
            p.dma("sync", xb_d[row:row + 128, :], xbf[:], tag=tg + "stxb")
            p.op("scalar", lambda e: e.activation(xab[:], xbf[:], AF.Square,
                                                  accum_out=stt[:, 4:5]),
                 reads=[xbf[:]], writes=[xab[:], stt[:, 4:5]], tag=tg + "rsq")
            p.op("vector", lambda e: e.tensor_scalar(stt[:, 4:5], stt[:, 4:5],
                                                     1.0 / D, EPS, ALU.mult, ALU.add),
                 reads=[stt[:, 4:5]], writes=[stt[:, 4:5]], tag=tg + "r1")
            p.op("scalar", lambda e: e.activation(stt[:, 5:6], stt[:, 4:5], AF.Sqrt),
                 reads=[stt[:, 4:5]], writes=[stt[:, 5:6]], tag=tg + "r2")
            p.op("vector", lambda e: e.reciprocal(stt[:, 4:5], stt[:, 5:6]),
                 reads=[stt[:, 5:6]], writes=[stt[:, 4:5]], tag=tg + "r3")
            p.op("vector", lambda e: e.tensor_scalar(obuf[:], xbf[:], stt[:, 4:5],
                                                     None, ALU.mult),
                 reads=[xbf[:], stt[:, 4:5]], writes=[obuf[:]], tag=tg + "rap")
            p.op("gpsimd", lambda e: e.tensor_mul(xab[:], obuf[:], wln1b[:]),
                 reads=[obuf[:], wln1b[:]], writes=[xab[:]], tag=tg + "wl1")
            for k in range(8):
                transpose_tile(ctx, ptr[0][:, k * 128:(k + 1) * 128],
                               xab[:, k * 128:(k + 1) * 128], 128, tag=tg + "trx")
            p.op("vector", lambda e: e.tensor_copy(xaT[:], ptr[0][:]),
                 reads=[ptr[0][:]], writes=[xaT[:]], tag=tg + "trxev")
            qt = qtok_r[it % 2]
            for sc in range(2):
                pairs = [(xaT[:, k * 128:(k + 1) * 128],
                          wq[:, k * 1024 + sc * 512: k * 1024 + sc * 512 + 512])
                         for k in range(8)]
                matmul_group(p, pq[sc][:], pairs, tg + "qmm")
                p.op("vector", lambda e, sc=sc, qt=qt: e.tensor_copy(
                    qt[:, sc * 512:(sc + 1) * 512], pq[sc][:]),
                    reads=[pq[sc][:]], writes=[qt[:, sc * 512:(sc + 1) * 512]],
                    tag=tg + "qev")
            q3e = qt[:].rearrange("p (h c) -> p h c", c=64)[:, :, 32:48]
            q3o = qt[:].rearrange("p (h c) -> p h c", c=64)[:, :, 48:64]
            ce = ropec[:, it * 16:(it + 1) * 16].unsqueeze(1).broadcast_to([128, 16, 16])
            se = ropes[:, it * 16:(it + 1) * 16].unsqueeze(1).broadcast_to([128, 16, 16])
            re1 = ropescr[:, 0:256].rearrange("p (h c) -> p h c", c=16)
            re2 = ropescr[:, 256:512].rearrange("p (h c) -> p h c", c=16)
            re3 = ropescr[:, 512:768].rearrange("p (h c) -> p h c", c=16)
            re4 = ropescr[:, 768:1024].rearrange("p (h c) -> p h c", c=16)
            p.op("gpsimd", lambda e, q3e=q3e, ce=ce, re1=re1: e.tensor_tensor(re1, q3e, ce, ALU.mult),
                 reads=[qt[:]], writes=[ropescr[:, 0:256]], tag=tg + "rp1")
            p.op("gpsimd", lambda e, q3o=q3o, se=se, re2=re2: e.tensor_tensor(re2, q3o, se, ALU.mult),
                 reads=[qt[:]], writes=[ropescr[:, 256:512]], tag=tg + "rp2")
            p.op("gpsimd", lambda e, q3e=q3e, se=se, re3=re3: e.tensor_tensor(
                re3, q3e, se, ALU.mult),
                reads=[qt[:]], writes=[ropescr[:, 512:768]], tag=tg + "rp3")
            p.op("gpsimd", lambda e, q3o=q3o, ce=ce, re4=re4: e.tensor_tensor(
                re4, q3o, ce, ALU.mult),
                reads=[qt[:]], writes=[ropescr[:, 768:1024]], tag=tg + "rp4")
            p.op("vector", lambda e, q3e=q3e, re1=re1, re2=re2: e.tensor_tensor(q3e, re1, re2, ALU.subtract),
                 reads=[ropescr[:, 0:512]], writes=[qt[:]], tag=tg + "rp5")
            p.op("vector", lambda e, q3o=q3o, re3=re3, re4=re4: e.tensor_tensor(q3o, re3, re4, ALU.add),
                 reads=[ropescr[:, 512:1024]], writes=[qt[:]], tag=tg + "rp6")
            for k in range(8):
                transpose_tile(ctx, ptr[1][:, k * 128:(k + 1) * 128],
                               qt[:, k * 128:(k + 1) * 128], 128, tag=tg + "trq")
            p.op("vector", lambda e, it=it: e.tensor_copy(
                qf[:].rearrange("p (k t) -> p k t", t=2048)[:, :, it * 128:(it + 1) * 128],
                ptr[1][:].rearrange("p (k t) -> p k t", t=128)),
                reads=[ptr[1][:]], writes=[qf[:]], tag=tg + "trqev")
            pairs = [(xaT[:, k * 128:(k + 1) * 128],
                      wkva[:, k * 160:(k + 1) * 160]) for k in range(8)]
            matmul_group(p, pq[0][:, 0:160], pairs, tg + "ckmm")
            p.op("vector", lambda e: e.tensor_copy(cktok[:, 0:160], pq[0][:, 0:160]),
                 reads=[pq[0][:]], writes=[cktok[:, 0:160]], tag=tg + "ckev")
            k3e = cktok[:, 128:144]
            k3o = cktok[:, 144:160]
            ce2 = ropec[:, it * 16:(it + 1) * 16]
            se2 = ropes[:, it * 16:(it + 1) * 16]
            p.op("vector", lambda e: e.tensor_mul(ropescr[:, 0:16], k3e, ce2),
                 reads=[cktok[:]], writes=[ropescr[:, 0:16]], tag=tg + "kr1")
            p.op("vector", lambda e: e.tensor_mul(ropescr[:, 16:32], k3o, se2),
                 reads=[cktok[:]], writes=[ropescr[:, 16:32]], tag=tg + "kr2")
            p.op("vector", lambda e: e.tensor_mul(ropescr[:, 32:48], k3e, se2),
                 reads=[cktok[:]], writes=[ropescr[:, 32:48]], tag=tg + "kr3")
            p.op("vector", lambda e: e.tensor_mul(ropescr[:, 48:64], k3o, ce2),
                 reads=[cktok[:]], writes=[ropescr[:, 48:64]], tag=tg + "kr4")
            p.op("vector", lambda e: e.tensor_sub(cktok[:, 160:176], ropescr[:, 0:16],
                                                  ropescr[:, 16:32]),
                 reads=[ropescr[:, 0:32]], writes=[cktok[:, 160:176]], tag=tg + "kr5")
            p.op("vector", lambda e: e.tensor_add(cktok[:, 176:192], ropescr[:, 32:48],
                                                  ropescr[:, 48:64]),
                 reads=[ropescr[:, 32:64]], writes=[cktok[:, 176:192]], tag=tg + "kr6")
            transpose_tile(ctx, ptr[0][:, 0:128], cktok[:, 0:128], 128, tag=tg + "trck")
            p.op("vector", lambda e, it=it: e.tensor_copy(
                ckvf[:, it * 128:(it + 1) * 128], ptr[0][:, 0:128]),
                reads=[ptr[0][:]], writes=[ckvf[:, it * 128:(it + 1) * 128]],
                tag=tg + "trckev")
            transpose_tile(ctx, ptr[1][0:32, 0:128], cktok[:, 160:192], 128, tag=tg + "trkr")
            p.op("vector", lambda e, it=it: e.tensor_copy(
                krr[:, it * 128:(it + 1) * 128], ptr[1][0:32, 0:128]),
                reads=[ptr[1][:]], writes=[krr[:, it * 128:(it + 1) * 128]],
                tag=tg + "krcp")
        for ft in range(8):
            for tc in range(4):
                pairs = [(wkvb[:, ft * 128:(ft + 1) * 128],
                          ckvf[:, tc * 512:(tc + 1) * 512])]
                matmul_group(p, pq[tc % 2][:], pairs, tb + "kvmm")
                p.op("vector", lambda e, ft=ft, tc=tc: e.tensor_copy(
                    kvf[:, ft * 2048 + tc * 512: ft * 2048 + tc * 512 + 512],
                    pq[tc % 2][:]),
                    reads=[pq[tc % 2][:]],
                    writes=[kvf[:, ft * 2048 + tc * 512: ft * 2048 + tc * 512 + 512]],
                    tag=tb + "kvev")
        for ft in range(8):
            for sl in range(2):
                dst = kvf[sl * 64 + 32: sl * 64 + 64, ft * 2048:(ft + 1) * 2048]
                p.op("vector", lambda e, d=dst: e.tensor_copy(d, krr[:]),
                     reads=[krr[:]], writes=[dst], tag=tb + "krfill")
        for kt in range(16):
            for fg in range(2):
                pb = ptr[fg]
                for fj in range(4):
                    ft = fg * 4 + fj
                    transpose_tile(ctx, pb[:, fj * 128:(fj + 1) * 128],
                                   kvf[:, ft * 2048 + kt * 128: ft * 2048 + kt * 128 + 128],
                                   128, tag=tb + "trv")
                for fj in range(4):
                    ft = fg * 4 + fj
                    for j in range(2):
                        h = ft * 2 + j
                        dst = vaug[:, kt * 528 + h * 33: kt * 528 + h * 33 + 32]
                        src_ = pb[:, fj * 128 + j * 64: fj * 128 + j * 64 + 32]
                        p.op("vector", lambda e, d=dst, s=src_: e.tensor_copy(d, s),
                             reads=[pb[:]], writes=[dst], tag=tb + "vev")
        for h in range(16):
            hb = (h % 2) * 64
            fb = (h // 2) * 2048
            for qc in range(4):
                m = h * 4 + qc
                pZv = pZ[m % 2]
                ya = yaug[m % 2]
                ptv = ptr[m % 2]
                zc = (m % 2) * 4
                nkt = 4 * qc + 4
                for kt in range(nkt):
                    ps = pS[kt % 2]
                    pairs = [
                        (kvf[hb:hb + 64, fb + kt * 128: fb + kt * 128 + 128],
                         qf[hb:hb + 64, fb + qc * 512: fb + qc * 512 + 512]),
                    ]
                    matmul_group(p, ps[:], pairs, tb + "sc%d_%d_%d" % (h, qc, kt))
                    Pt = Pb[kt % 2]
                    p.op("scalar", lambda e, d=Pt, s=ps: e.activation(
                        d[:], s[:], AF.Exp, bias=cpool[:, 1:2], scale=cpool[:, 0:1]),
                        reads=[ps[:], cpool[:]], writes=[Pt[:]],
                        tag=tb + "exp%d_%d_%d" % (h, qc, kt))
                    if kt >= 4 * qc:
                        base = qc * 512 - kt * 128
                        p.op("gpsimd", lambda e, d=Pt, base=base: e.affine_select(
                            d[:], d[:], [[1, 512]], ALU.is_ge, 0.0, base=base,
                            channel_multiplier=-1),
                            reads=[Pt[:]], writes=[Pt[:]],
                            tag=tb + "msk%d_%d_%d" % (h, qc, kt))
                    p.op("tensor", lambda e, l=vaug[:, kt * 528 + h * 33: kt * 528 + h * 33 + 33],
                         rr=Pt, s=(kt == 0), st=(kt == nkt - 1), o=pZv: e.matmul(
                             o[0:33, :], l, rr[:], start=s, stop=st),
                         reads=[vaug[:], Pt[:]], writes=[pZv[:]],
                         tag=tb + "pv%d_%d_%d" % (h, qc, kt))
                p.op("vector", lambda e, ya=ya, o=pZv: e.tensor_copy(ya[:], o[0:33, :]),
                     reads=[pZv[:]], writes=[ya[:]], tag=tb + "yev%d_%d" % (h, qc))
                for j in range(4):
                    transpose_tile(ctx, ptv[:, j * 34:j * 34 + 33],
                                   ya[:, j * 128:(j + 1) * 128], 33,
                                   tag=tb + "try%d_%d_%d" % (h, qc, j))
                p.op("vector", lambda e, ptv=ptv, zc=zc: e.reciprocal(
                    zr[:, zc:zc + 4],
                    ptv[:, 0:136].rearrange("p (j c) -> p j c", c=34)[:, :, 32:33]),
                    reads=[ptv[:]], writes=[zr[:, zc:zc + 4]], tag=tb + "zr%d_%d" % (h, qc))
                for j in range(4):
                    qtile = qc * 4 + j
                    dst = ytok[:, qtile * 512 + h * 32: qtile * 512 + h * 32 + 32]
                    p.op("vector", lambda e, d=dst, j=j, ptv=ptv, zc=zc: e.tensor_scalar(
                        d, ptv[:, j * 34:j * 34 + 32], zr[:, zc + j:zc + j + 1],
                        None, ALU.mult),
                        reads=[ptv[:], zr[:, zc:zc + 4]], writes=[dst],
                        tag=tb + "ynm%d_%d_%d" % (h, qc, j))
        for it in range(16):
            tg = tb + "o%d_" % it
            row = toff + it * 128
            for j in range(4):
                transpose_tile(ctx, ptr[0][:, j * 128:(j + 1) * 128],
                               ytok[:, it * 512 + j * 128: it * 512 + (j + 1) * 128],
                               128, tag=tg + "tro")
            p.op("vector", lambda e: e.tensor_copy(xaT[:, 0:512], ptr[0][:, 0:512]),
                 reads=[ptr[0][:]], writes=[xaT[:, 0:512]], tag=tg + "troev")
            xi = xi_r[it % 2]
            p.dma("sync", xi[:], xb_d[row:row + 128, :], tag=tg + "ldxb")
            for sc in range(2):
                pairs = [(xaT[:, k * 128:(k + 1) * 128],
                          woe[:, k * 1024 + sc * 512: k * 1024 + sc * 512 + 512])
                         for k in range(4)]
                matmul_group(p, pO[:], pairs, tg + "omm%d" % sc)
                p.op("vector", lambda e, sc=sc, xi=xi: e.tensor_add(
                    obuf[:, sc * 512:(sc + 1) * 512], pO[:],
                    xi[:, sc * 512:(sc + 1) * 512]),
                    reads=[pO[:], xi[:]], writes=[obuf[:, sc * 512:(sc + 1) * 512]],
                    tag=tg + "oadd%d" % sc)
            p.dma("sync", t_d[row:row + 128, :], obuf[:], tag=tg + "stt")

    ctx.stack = old
    sub.close()


def build_phase3b(ctx, xin_d, t_d, xout_d, din, cin):
    nc, p = ctx.nc, ctx.prog
    sub = contextlib.ExitStack()
    old = ctx.stack
    ctx.stack = sub

    wgate = ctx.sb("p3b_wgate", [128, 8 * 8], BF16)
    p.dma("gpsimd", wgate[:].rearrange("p (a n) -> p a n", a=8),
          din["L1_ffn_Wgate"].rearrange("(a p) n -> p a n", p=128), tag="p3b_ldwg")
    ebb = ctx.sb("p3b_ebb", [128, 8])
    p.dma("sync", ebb[:], cin["c_expbias_b"], tag="p3b_ldeb")
    wln2b = ctx.sb("p3b_wln2b", [128, D])
    p.dma("sync", wln2b[:], cin["c_wln2_b"], tag="p3b_ldln2")

    NW = 3
    wg_b = [ctx.sb("p3b_wgb%d" % i, [128, 8 * 512], BF16) for i in range(NW)]
    wu_b = [ctx.sb("p3b_wub%d" % i, [128, 8 * 512], BF16) for i in range(NW)]
    wd_b = [ctx.sb("p3b_wdb%d" % i, [128, 4 * 1024], BF16) for i in range(NW)]

    xmT = ctx.sb("p3b_xmT", [128, 8 * 1024], BF16)
    acc = ctx.sb("p3b_acc", [128, 8 * 1024])
    comb = ctx.sb("p3b_comb", [128, 8 * 16])
    gbuf = ctx.sb("p3b_g", [128, 4 * 512], BF16)
    gated = ctx.sb("p3b_gated", [128, 4 * 512], BF16)
    xt_r = [ctx.sb("p3b_xt%d" % i, [128, D]) for i in range(2)]
    xmb = ctx.sb("p3b_xmb", [128, D], BF16)
    gsc = ctx.sb("p3b_gsc", [128, 16])
    bnt = ctx.sb("p3b_bn", [128, 12])
    stt = ctx.sb("p3b_st", [128, 8])
    obuf = ctx.sb("p3b_ob", [128, D])

    pM = [ctx.ps("p3b_pM%d" % i, [128, 512]) for i in range(4)]
    ptr = [ctx.ps("p3b_ptr%d" % i, [128, 1024], BF16) for i in range(2)]

    WNAMES = [("L1_ffn_sWg", "L1_ffn_sWu", "L1_ffn_sWd", None)] + \
             [("L1_ffn_Wg", "L1_ffn_Wu", "L1_ffn_Wd", e) for e in range(E)]

    for u in range(4):
        tu = "p3u%d_" % u
        for tt in range(8):
            it = u * 8 + tt
            tg = tu + "t%d_" % tt
            xt = xt_r[tt % 2]
            p.dma("sync", xt[:], t_d[it * 128:(it + 1) * 128, :], tag=tg + "ldt")
            p.op("scalar", lambda e, xt=xt: e.activation(xmb[:], xt[:], AF.Square,
                                                         accum_out=stt[:, 4:5]),
                 reads=[xt[:]], writes=[xmb[:], stt[:, 4:5]], tag=tg + "rsq")
            p.op("vector", lambda e: e.tensor_scalar(stt[:, 4:5], stt[:, 4:5],
                                                     1.0 / D, EPS, ALU.mult, ALU.add),
                 reads=[stt[:, 4:5]], writes=[stt[:, 4:5]], tag=tg + "r1")
            p.op("scalar", lambda e: e.activation(stt[:, 5:6], stt[:, 4:5], AF.Sqrt),
                 reads=[stt[:, 4:5]], writes=[stt[:, 5:6]], tag=tg + "r2")
            p.op("vector", lambda e: e.reciprocal(stt[:, 4:5], stt[:, 5:6]),
                 reads=[stt[:, 5:6]], writes=[stt[:, 4:5]], tag=tg + "r3")
            p.op("vector", lambda e, xt=xt: e.tensor_scalar(obuf[:], xt[:], stt[:, 4:5],
                                                            None, ALU.mult),
                 reads=[xt[:], stt[:, 4:5]], writes=[obuf[:]], tag=tg + "rap")
            p.op("gpsimd", lambda e: e.tensor_mul(xmb[:], obuf[:], wln2b[:]),
                 reads=[obuf[:], wln2b[:]], writes=[xmb[:]], tag=tg + "wl2")
            for k in range(8):
                transpose_tile(ctx, ptr[tt % 2][:, k * 128:(k + 1) * 128],
                               xmb[:, k * 128:(k + 1) * 128], 128, tag=tg + "trm")
            p.op("vector", lambda e, tt=tt: e.tensor_copy(
                xmT[:].rearrange("p (k t) -> p k t", t=1024)[:, :, tt * 128:(tt + 1) * 128],
                ptr[tt % 2][:].rearrange("p (k t) -> p k t", t=128)),
                reads=[ptr[tt % 2][:]], writes=[xmT[:]], tag=tg + "trmev")
            pairs = [(xmT[:].rearrange("p (k t) -> p k t", t=1024)[:, k, tt * 128:(tt + 1) * 128],
                      wgate[:, k * 8:(k + 1) * 8]) for k in range(8)]
            matmul_group(p, pM[3][:, 0:8], pairs, tg + "gt")
            p.op("vector", lambda e: e.tensor_add(gsc[:, 0:8], pM[3][:, 0:8], ebb[:]),
                 reads=[pM[3][:], ebb[:]], writes=[gsc[:, 0:8]], tag=tg + "gt1")
            p.op("vector", lambda e: e.reduce_max(gsc[:, 8:9], gsc[:, 0:8], axis=AX.X),
                 reads=[gsc[:, 0:8]], writes=[gsc[:, 8:9]], tag=tg + "gt2")
            p.op("vector", lambda e: e.tensor_scalar(gsc[:, 8:9], gsc[:, 8:9], -1.0,
                                                     None, ALU.mult),
                 reads=[gsc[:, 8:9]], writes=[gsc[:, 8:9]], tag=tg + "gt3")
            p.op("scalar", lambda e: e.activation(gsc[:, 0:8], gsc[:, 0:8], AF.Exp,
                                                  bias=gsc[:, 8:9]),
                 reads=[gsc[:, 0:9]], writes=[gsc[:, 0:8]], tag=tg + "gt4")
            p.op("vector", lambda e: e.reduce_sum(gsc[:, 9:10], gsc[:, 0:8], axis=AX.X),
                 reads=[gsc[:, 0:8]], writes=[gsc[:, 9:10]], tag=tg + "gt5")
            p.op("vector", lambda e: e.reciprocal(gsc[:, 10:11], gsc[:, 9:10]),
                 reads=[gsc[:, 9:10]], writes=[gsc[:, 10:11]], tag=tg + "gt6")
            p.op("vector", lambda e: e.tensor_scalar(gsc[:, 0:8], gsc[:, 0:8],
                                                     gsc[:, 10:11], None, ALU.mult),
                 reads=[gsc[:, 0:11]], writes=[gsc[:, 0:8]], tag=tg + "gt7")
            c8 = comb[:, tt * 16:tt * 16 + 8]
            e8 = comb[:, tt * 16 + 8:tt * 16 + 16]
            csh = comb[:, tt * 16 + 8:tt * 16 + 9]
            p.op("vector", lambda e: e.reduce_max(gsc[:, 8:9], gsc[:, 0:8], axis=AX.X),
                 reads=[gsc[:, 0:8]], writes=[gsc[:, 8:9]], tag=tg + "tp1")
            p.op("vector", lambda e, e8=e8: e.tensor_scalar(e8, gsc[:, 0:8],
                                                            gsc[:, 8:9], None,
                                                            ALU.is_equal),
                 reads=[gsc[:, 0:9]], writes=[e8], tag=tg + "tp2")
            p.op("vector", lambda e, c8=c8, e8=e8: e.scalar_tensor_tensor(
                c8, e8, -2.0, gsc[:, 0:8], ALU.mult, ALU.add),
                reads=[e8, gsc[:, 0:8]], writes=[c8], tag=tg + "tp3")
            p.op("vector", lambda e, c8=c8: e.reduce_max(gsc[:, 9:10], c8, axis=AX.X),
                 reads=[c8], writes=[gsc[:, 9:10]], tag=tg + "tp4")
            p.op("vector", lambda e: e.tensor_add(gsc[:, 10:11], gsc[:, 8:9],
                                                  gsc[:, 9:10]),
                 reads=[gsc[:, 8:10]], writes=[gsc[:, 10:11]], tag=tg + "tp5")
            p.op("vector", lambda e: e.reciprocal(gsc[:, 11:12], gsc[:, 10:11]),
                 reads=[gsc[:, 10:11]], writes=[gsc[:, 11:12]], tag=tg + "tp6")
            p.op("vector", lambda e, c8=c8: e.tensor_scalar(c8, gsc[:, 0:8],
                                                            gsc[:, 9:10], None,
                                                            ALU.is_equal),
                 reads=[gsc[:, 0:10]], writes=[c8], tag=tg + "tp7")
            p.op("vector", lambda e: e.tensor_scalar(gsc[:, 12:13], gsc[:, 8:9],
                                                     gsc[:, 11:12], None, ALU.mult),
                 reads=[gsc[:, 8:12]], writes=[gsc[:, 12:13]], tag=tg + "tp8")
            p.op("vector", lambda e: e.tensor_scalar(gsc[:, 13:14], gsc[:, 9:10],
                                                     gsc[:, 11:12], None, ALU.mult),
                 reads=[gsc[:, 9:12]], writes=[gsc[:, 13:14]], tag=tg + "tp9")
            p.op("vector", lambda e, c8=c8: e.tensor_scalar(c8, c8, gsc[:, 13:14],
                                                            None, ALU.mult),
                 reads=[c8, gsc[:, 13:14]], writes=[c8], tag=tg + "tp10")
            p.op("vector", lambda e, c8=c8, e8=e8: e.scalar_tensor_tensor(
                c8, e8, gsc[:, 12:13], c8, ALU.mult, ALU.add),
                reads=[e8, gsc[:, 12:13], c8], writes=[c8], tag=tg + "tp11")
            p.op("vector", lambda e, csh=csh: e.memset(csh, 1.0),
                 reads=[], writes=[csh], tag=tg + "tp12")
        for ei, (gn, un, dn, eidx) in enumerate(WNAMES):
            wgs, wus, wds = wg_b[ei % NW], wu_b[ei % NW], wd_b[ei % NW]
            tg = tu + "e%d_" % ei
            gsrc = din[gn][0] if eidx is None else din[gn][eidx]
            usrc = din[un][0] if eidx is None else din[un][eidx]
            dsrc = din[dn][0] if eidx is None else din[dn][eidx]
            if True:
                p.dma("gpsimd", wgs[:].rearrange("p (a n) -> p a n", a=8),
                      gsrc.rearrange("(a p) n -> p a n", p=128), tag=tg + "ldg")
                p.dma("gpsimd", wus[:].rearrange("p (a n) -> p a n", a=8),
                      usrc.rearrange("(a p) n -> p a n", p=128), tag=tg + "ldu")
                p.dma("gpsimd", wds[:].rearrange("p (a n) -> p a n", a=4),
                      dsrc.rearrange("(a p) n -> p a n", p=128), tag=tg + "ldd")
            ccol = 8 if eidx is None else eidx
            for blk in range(2):
                for ft in range(4):
                    pairs = [(wgs[:, k * 512 + ft * 128: k * 512 + ft * 128 + 128],
                              xmT[:].rearrange("p (k t) -> p k t", t=1024)
                              [:, k, blk * 512:(blk + 1) * 512]) for k in range(8)]
                    matmul_group(p, pM[ft % 2][:], pairs, tg + "gmm%d_%d" % (blk, ft))
                    p.op("scalar", lambda e, ft=ft: e.activation(
                        gbuf[:, ft * 512:(ft + 1) * 512], pM[ft % 2][:], AF.Silu),
                        reads=[pM[ft % 2][:]], writes=[gbuf[:, ft * 512:(ft + 1) * 512]],
                        tag=tg + "sil%d_%d" % (blk, ft))
                for ft in range(4):
                    pairs = [(wus[:, k * 512 + ft * 128: k * 512 + ft * 128 + 128],
                              xmT[:].rearrange("p (k t) -> p k t", t=1024)
                              [:, k, blk * 512:(blk + 1) * 512]) for k in range(8)]
                    matmul_group(p, pM[2 + ft % 2][:], pairs, tg + "umm%d_%d" % (blk, ft))
                    p.op("vector", lambda e, ft=ft: e.tensor_mul(
                        gated[:, ft * 512:(ft + 1) * 512],
                        gbuf[:, ft * 512:(ft + 1) * 512], pM[2 + ft % 2][:]),
                        reads=[pM[2 + ft % 2][:], gbuf[:, ft * 512:(ft + 1) * 512]],
                        writes=[gated[:, ft * 512:(ft + 1) * 512]],
                        tag=tg + "gu%d_%d" % (blk, ft))
                for ts in range(4):
                    tt = blk * 4 + ts
                    for sc in range(2):
                        pairs = [(gated[:].rearrange("p (i t) -> p i t", t=512)
                                  [:, i, ts * 128:(ts + 1) * 128],
                                  wds[:, i * 1024 + sc * 512: i * 1024 + sc * 512 + 512])
                                 for i in range(4)]
                        ps = pM[sc]
                        matmul_group(p, ps[:], pairs, tg + "dmm%d_%d" % (tt, sc))
                        dst = acc[:, tt * 1024 + sc * 512: tt * 1024 + sc * 512 + 512]
                        cap = comb[:, tt * 16 + ccol: tt * 16 + ccol + 1]
                        if ei == 0:
                            p.op("vector", lambda e, d=dst, s=ps, c=cap: e.tensor_scalar(
                                d, s[:], c, None, ALU.mult),
                                reads=[ps[:], comb[:]], writes=[dst],
                                tag=tg + "acc%d_%d" % (tt, sc))
                        else:
                            p.op("vector", lambda e, d=dst, s=ps, c=cap:
                                 e.scalar_tensor_tensor(d, s[:], c, d, ALU.mult, ALU.add),
                                 reads=[ps[:], comb[:], dst], writes=[dst],
                                 tag=tg + "acc%d_%d" % (tt, sc))
        for tt in range(8):
            it = u * 8 + tt
            tg = tu + "f%d_" % tt
            xt = xt_r[tt % 2]
            p.dma("sync", xt[:], t_d[it * 128:(it + 1) * 128, :], tag=tg + "ldt2")
            p.op("gpsimd", lambda e, xt=xt, tt=tt: e.tensor_add(
                obuf[:], xt[:], acc[:, tt * 1024:(tt + 1) * 1024]),
                reads=[xt[:], acc[:]], writes=[obuf[:]], tag=tg + "tadd")
            emit_ln_stats(ctx, obuf[:], D, bnt, stt, tag=tg + "ln")
            emit_ln_apply(ctx, obuf[:], stt, obuf[:], tag=tg + "ln")
            p.dma("sync", xt[:], xin_d[it * 128:(it + 1) * 128, :], tag=tg + "ldx3")
            p.op("vector", lambda e, xt=xt: e.tensor_add(obuf[:], obuf[:], xt[:]),
                 reads=[obuf[:], xt[:]], writes=[obuf[:]], tag=tg + "xadd")
            p.dma("sync", xout_d[it * 128:(it + 1) * 128, :], obuf[:], tag=tg + "stx")

    ctx.stack = old
    sub.close()



def kernel(**inputs):
    x0 = np.asarray(inputs["L0_attn_x"], np.float32)
    phases = [0, 1, 2, 3]
    nc = build_program(phases)
    in_maps = make_in_maps(inputs, phases, x0)
    res = run_bass_kernel_spmd(nc, in_maps, core_ids=list(range(8)))
    out = np.concatenate(
        [res.results[c]["x_out"].reshape(2, SEQ, D) for c in range(8)], axis=0)
    return out.astype(np.float32)
